# version 01
# baseline (speedup 1.0000x reference)
import numpy as np
import concourse.bass as bass
import concourse.mybir as mybir
from concourse.bass_utils import run_bass_kernel_spmd
from contextlib import ExitStack

F32 = mybir.dt.float32
BF16 = mybir.dt.bfloat16
I32 = mybir.dt.int32
AF = mybir.ActivationFunctionType
ALU = mybir.AluOpType
AX = mybir.AxisListType

T = 2048
D = 1024
H = 16
E = 64
W = 128
NW = T // W
DH = 4096
NDC = D // 128
NHC = DH // 128
TGS = 512
NTG = T // TGS
FT = 512
NFT = T // FT
QKVN = 3 * D
EXP_SHIFT = -20.0
HEXW = 515


def _ffn_tiles():
    tiles = []
    for i in range(NFT):
        tok0 = FT * i - 1
        if i == 0:
            ccmin, ccmax = 1, 512
            slices = [(1, 128), (129, 128), (257, 128), (385, 127)]
        elif i < NFT - 1:
            ccmin, ccmax = 0, 512
            slices = [(0, 128), (128, 128), (256, 128), (384, 128)]
        else:
            ccmin, ccmax = 0, 513
            slices = [(0, 128), (128, 128), (256, 128), (384, 128), (512, 1)]
        tiles.append(dict(i=i, tok0=tok0, ccmin=ccmin, ccmax=ccmax,
                          slices=slices))
    return tiles


TILES = _ffn_tiles()
GSL = []
for _td in TILES:
    for _si, (_c0, _wd) in enumerate(_td["slices"]):
        GSL.append((_td["i"], _si, _c0, _wd))
NSL = len(GSL)


def build_nc(debug=False, maxphase=4):
    nc = bass.Bass("TRN2", target_bir_lowering=False,
                   dynamic_dma_scratch_size=4096,
                   detect_race_conditions=False)

    x_h = nc.dram_tensor("x", [T, D], F32, kind="ExternalInput").ap()
    wqkv_h = nc.dram_tensor("wqkv", [D, QKVN], F32, kind="ExternalInput").ap()
    w1_h = nc.dram_tensor("w1", [DH, D], F32, kind="ExternalInput").ap()
    convw_h = nc.dram_tensor("convw", [DH, 3], F32, kind="ExternalInput").ap()
    convb_h = nc.dram_tensor("convb", [DH, 1], F32, kind="ExternalInput").ap()
    w2_h = nc.dram_tensor("w2", [D, DH], F32, kind="ExternalInput").ap()
    out_h = nc.dram_tensor("out", [T, D], F32, kind="ExternalOutput").ap()

    dbgk = "ExternalOutput" if debug else "Internal"
    qh_h = nc.dram_tensor("qh", [NDC, 128, T], BF16, kind=dbgk).ap()
    x2_h = nc.dram_tensor("x2s", [T, D], F32, kind=dbgk).ap()
    xqh_h = nc.dram_tensor("xqs", [NDC, 128, T], BF16, kind=dbgk).ap()
    if debug:
        dk_h = nc.dram_tensor("dk", [128, NDC * T], BF16, kind="ExternalOutput").ap()
        dxnT_h = nc.dram_tensor("dxnT", [128, NDC * TGS], BF16, kind="ExternalOutput").ap()
        dst1_h = nc.dram_tensor("dst1", [128, 48], F32, kind="ExternalOutput").ap()
        dxnbf_h = nc.dram_tensor("dxnbf", [128, 2 * D], BF16, kind="ExternalOutput").ap()
        dv_h = nc.dram_tensor("dv", [128, NW * D], BF16, kind="ExternalOutput").ap()
        dt1_h = nc.dram_tensor("dt1", [128, NDC * DH], BF16, kind="ExternalOutput").ap()
        dt2_h = nc.dram_tensor("dt2", [128, NHC * D], BF16, kind="ExternalOutput").ap()
        dsw_h = nc.dram_tensor("dsw", [128, 4], F32, kind="ExternalOutput").ap()
        dhx_h = nc.dram_tensor("dhx", [128, NHC * HEXW], BF16, kind="ExternalOutput").ap()
        dcC_h = nc.dram_tensor("dcC", [128, 8 * NHC], BF16, kind="ExternalOutput").ap()
        dcA_h = nc.dram_tensor("dcA", [128, 513], BF16, kind="ExternalOutput").ap()
        dcM_h = nc.dram_tensor("dcM", [128, 32], F32, kind="ExternalOutput").ap()
        dcS_h = nc.dram_tensor("dcS", [128, 512], F32, kind="ExternalOutput").ap()

    ctx = ExitStack()
    with ctx:
        e = ctx.enter_context

        ident = e(nc.sbuf_tensor("ident", [128, 128], BF16))
        identf = e(nc.sbuf_tensor("identf", [128, 128], F32))
        onesrow = e(nc.sbuf_tensor("onesrow", [1, 128], F32))
        onescol = e(nc.sbuf_tensor("onescol", [128, 1], F32))
        c_m20 = e(nc.sbuf_tensor("c_m20", [128, 1], F32))
        c_eps = e(nc.sbuf_tensor("c_eps", [128, 1], F32))
        c_negeps = e(nc.sbuf_tensor("c_negeps", [128, 1], F32))
        cw = e(nc.sbuf_tensor("cw", [128, 3 * NHC], F32))
        cbv_ = e(nc.sbuf_tensor("cbv", [128, NHC], F32))
        cwsc = e(nc.sbuf_tensor("cwsc", [128, 3 * NHC], F32))
        swv = e(nc.sbuf_tensor("swv", [128, 4], F32))
        sums = e(nc.sbuf_tensor("sums", [128, 32], F32))
        sinv = e(nc.sbuf_tensor("sinv", [128, 16], F32))
        st1 = e(nc.sbuf_tensor("st1", [128, 48], F32))
        wacc = e(nc.sbuf_tensor("wacc", [128, 2], F32))
        sqb = e(nc.sbuf_tensor("sqb", [128, D], F32))
        jkA = e(nc.sbuf_tensor("jkA", [128, 256], F32))
        jkB = e(nc.sbuf_tensor("jkB", [128, 256], F32))
        stS = e(nc.sbuf_tensor("stS", [128, 48], F32))

        S = {}
        for name in ["d_wq", "d_cw", "d_dbg",
                     "d_x", "v_st", "s_xn", "t_tr", "v_xnT",
                     "t_qk", "v_qk", "t_v", "v_v", "d_qh",
                     "d_qw", "d_xr", "d_x2", "d_xq", "d_xqt",
                     "t_sc", "s_exp", "t_tr2", "v_tr", "t_av",
                     "v_ep", "s_e1", "v_e2", "s_e2", "v_e3", "s_r1",
                     "v_e3a", "s_e3a", "v_e3b", "s_e3b",
                     "v_s2b", "s_s2a", "s_s2b", "g_e3",
                     "d_w1", "g_p1", "t_p1", "t_sw", "v_sw", "d_w2", "v_q2", "d_tq",
                     "d_xt", "d_x2i", "t_f1", "v_ev", "v_cv", "s_si", "g_am",
                     "t_s2", "v_s2", "g_rd", "t_f2", "v_f2", "s_l4", "v_l4",
                     "d_out", "g_i4"]:
            S[name] = e(nc.semaphore(name))

        def ln_stats_v(v, b, src, sub=None):
            p = sub if sub is not None else slice(0, 128)
            v.scalar_tensor_tensor(sqb[p, :], src, 1.0, src, ALU.bypass, ALU.mult)
            v.reduce_sum(st1[p, b:b + 1], src, axis=AX.X)
            return v.reduce_sum(st1[p, b + 1:b + 2], sqb[p, :], axis=AX.X)

        def ln_rstd_s(s, b, sub=None):
            p = sub if sub is not None else slice(0, 128)
            s.activation(stS[p, b:b + 1], st1[p, b:b + 1], AF.Identity,
                         scale=1.0 / D)
            s.activation(stS[p, b + 1:b + 2], stS[p, b:b + 1], AF.Identity,
                         scale=stS[p, b:b + 1])
            s.activation(stS[p, b + 1:b + 2], stS[p, b + 1:b + 2], AF.Identity,
                         scale=-1.0)
            s.activation(stS[p, b + 1:b + 2], st1[p, b + 1:b + 2], AF.Identity,
                         scale=1.0 / D, bias=stS[p, b + 1:b + 2])
            s.activation(stS[p, b + 2:b + 3], stS[p, b + 1:b + 2], AF.Ln,
                         bias=c_eps[p, :])
            s.activation(stS[p, b + 2:b + 3], stS[p, b + 2:b + 3], AF.Exp,
                         scale=-0.5)
            s.activation(stS[p, b + 3:b + 4], stS[p, b:b + 1], AF.Identity,
                         scale=stS[p, b + 2:b + 3])
            return s.activation(stS[p, b + 3:b + 4], stS[p, b + 3:b + 4],
                                AF.Identity, scale=-1.0)

        def vpad(v):
            v.tensor_copy(jkA[:], jkB[:])
            return v.tensor_copy(jkB[:], jkA[:])

        with nc.Block("b0") as b0:
            @b0.sync
            def _(sync):
                for c in range(NHC):
                    sync.dma_start(
                        out=cw[:, 3 * c:3 * c + 3],
                        in_=convw_h[c * 128:(c + 1) * 128, :],
                    ).then_inc(S["d_cw"], 16)
                    sync.dma_start(
                        out=cbv_[:, c:c + 1],
                        in_=convb_h[c * 128:(c + 1) * 128, :],
                    ).then_inc(S["d_cw"], 16)
                sync.wait_ge(S["d_cw"], 32 * NHC)

            @b0.gpsimd
            def _(g):
                for idt in (ident, identf):
                    g.memset(idt[:], 0.0)
                    g.affine_select(out=idt[:], in_=idt[:],
                                    compare_op=ALU.not_equal, fill=1.0, base=0,
                                    pattern=[[-1, 128]], channel_multiplier=1)
                g.memset(onesrow[:], 1.0)
                g.memset(onescol[:], 1.0)
                g.memset(c_m20[:], EXP_SHIFT)
                g.memset(c_eps[:], 1e-5)
                g.memset(c_negeps[:], -1e-5)
                g.memset(st1[:], 0.0)
                g.memset(stS[:], 0.0)
                g.memset(sqb[:], 0.0)
                g.memset(jkA[:], 0.0)
                g.memset(jkB[:], 0.0)

        with ExitStack() as ctxA:
            ea = ctxA.enter_context
            wqkv = ea(nc.sbuf_tensor("wqkv_sb", [128, NDC * QKVN], BF16))
            xnT = ea(nc.sbuf_tensor("xnT", [128, NDC * TGS], BF16))
            xnbf = ea(nc.sbuf_tensor("xnbf", [128, 2 * D], BF16))
            kf = ea(nc.sbuf_tensor("kf", [128, NDC * T], BF16))
            vt = ea(nc.sbuf_tensor("vt", [128, NW * D], BF16))
            qsb = ea(nc.sbuf_tensor("qsb", [128, 8 * TGS], BF16))
            qwin = ea(nc.sbuf_tensor("qwin", [128, 2 * NDC * 128], BF16))
            xln = ea(nc.sbuf_tensor("xln", [128, 3 * D], F32))
            xres = ea(nc.sbuf_tensor("xres", [128, 2 * D], F32))
            attno = ea(nc.sbuf_tensor("attno", [128, 2 * D], F32))
            wscr = ea(nc.sbuf_tensor("wscr", [128, 2 * D], F32))
            wscrB = ea(nc.sbuf_tensor("wscrB", [128, 2 * D], F32))
            x2b = ea(nc.sbuf_tensor("x2b", [128, 3 * D], F32))
            xqf = ea(nc.sbuf_tensor("xqf", [128, 1024], F32))
            xqbf = ea(nc.sbuf_tensor("xqbf", [128, 2 * D], BF16))
            xqtw = ea(nc.sbuf_tensor("xqtw", [128, 2 * NDC * 128], BF16,
                                     is_transpose_dest=True))
            probs = ea(nc.sbuf_tensor("probs", [128, 2 * 768], BF16))
            probsT = ea(nc.sbuf_tensor("probsT", [128, 2 * 768], BF16))

            with nc.Block("b0b") as b0b:
                @b0b.gpsimd
                def _(g):
                    for d in range(NDC):
                        g.dma_start(
                            out=wqkv[:, d * QKVN:(d + 1) * QKVN],
                            in_=wqkv_h[d * 128:(d + 1) * 128, :],
                        ).then_inc(S["d_wq"], 16)

            with ExitStack() as ctx1:
                e1 = ctx1.enter_context
                ptr = e1(nc.psum_tensor("ptr", [128, 1024], F32))
                psA = [e1(nc.psum_tensor(f"psA{i}", [128, 512], F32))
                       for i in (0, 1)]
                pv = [e1(nc.psum_tensor(f"pv{i}", [128, 1024], F32))
                      for i in (0, 1)]

                with nc.Block("b1") as b1:
                    @b1.sync
                    def _(sync):
                        for c in range(NW + 1):
                            if c < NW:
                                if c >= 3:
                                    sync.wait_ge(S["s_xn"], c - 2)
                                sync.dma_start(
                                    out=xln[:, (c % 3) * D:][:, 0:D],
                                    in_=x_h[c * 128:(c + 1) * 128, :],
                                ).then_inc(S["d_x"], 16)
                            if c % 4 == 0 and c >= 4:
                                tg = c // 4
                                for F in range(8):
                                    sync.wait_ge(S["v_qk"], (tg - 1) * 16 + F + 1)
                                    sync.dma_start(
                                        out=qh_h[F, :, (tg - 1) * TGS:tg * TGS],
                                        in_=qsb[:, F * TGS:][:, 0:TGS],
                                    ).then_inc(S["d_qh"], 16)
                        sync.wait_ge(S["d_qh"], 16 * NTG * 8)

                    @b1.vector
                    def _(v):
                        for c in range(NW + 1):
                            if c < NW:
                                b = 4 * (c % 2)
                                v.wait_ge(S["d_x"], 16 * (c + 1))
                                xc = xln[:, (c % 3) * D:][:, 0:D]
                                ln_stats_v(v, b, xc).then_inc(S["v_st"], 1)
                            if c < 1:
                                continue
                            cp = c - 1
                            tg, l = cp // 4, cp % 4
                            v.wait_ge(S["t_tr"], cp + 1)
                            if l == 0 and tg > 0:
                                v.wait_ge(S["t_qk"], 16 * tg)
                                v.wait_ge(S["t_v"], 4 * tg)
                            v.tensor_copy(
                                xnT[:].rearrange("p (d t) -> p d t", d=NDC)[:, :, l * 128:(l + 1) * 128],
                                ptr[:].rearrange("p (d t) -> p d t", d=NDC),
                            ).then_inc(S["v_xnT"], 1)
                            if l == 3:
                                for F in range(16):
                                    gidx = tg * 16 + F
                                    v.wait_ge(S["t_qk"], gidx + 1)
                                    if F < 8:
                                        if tg >= 1:
                                            v.wait_ge(S["d_qh"],
                                                      16 * (8 * (tg - 1) + F + 1))
                                        v.tensor_scalar(
                                            qsb[:, F * TGS:][:, 0:TGS],
                                            psA[F % 2][:], 0.125, None, ALU.mult,
                                        ).then_inc(S["v_qk"], 1)
                                    else:
                                        v.tensor_copy(
                                            kf[:, (F - 8) * T + tg * TGS:][:, 0:TGS],
                                            psA[F % 2][:],
                                        ).then_inc(S["v_qk"], 1)
                                for sl in range(4):
                                    gidx = tg * 4 + sl
                                    v.wait_ge(S["t_v"], gidx + 1)
                                    v.tensor_copy(
                                        vt[:, gidx * D:(gidx + 1) * D],
                                        pv[sl % 2][:],
                                    ).then_inc(S["v_v"], 1)

                    @b1.scalar
                    def _(s):
                        for c in range(NW + 1):
                            if c < NW:
                                b = 4 * (c % 2)
                                s.wait_ge(S["v_st"], c + 1)
                                ln_rstd_s(s, b)
                            if c >= 1:
                                cp = c - 1
                                b = 4 * (cp % 2)
                                if cp >= 2:
                                    s.wait_ge(S["t_tr"], cp - 1)
                                s.activation(xnbf[:, (cp % 2) * D:][:, 0:D],
                                             xln[:, (cp % 3) * D:][:, 0:D],
                                             AF.Identity, scale=stS[:, b + 2:b + 3],
                                             bias=stS[:, b + 3:b + 4]).then_inc(S["s_xn"], 1)

                    @b1.tensor
                    def _(t):
                        for tg in range(NTG):
                            for l in range(4):
                                c = tg * 4 + l
                                t.wait_ge(S["s_xn"], c + 1)
                                if c > 0:
                                    t.wait_ge(S["v_xnT"], c)
                                ins = None
                                for d in range(NDC):
                                    ins = t.matmul(
                                        ptr[:, d * 128:(d + 1) * 128],
                                        xnbf[:, (c % 2) * D + d * 128:][:, 0:128],
                                        ident[:], start=True, stop=True)
                                ins.then_inc(S["t_tr"], 1)
                            t.wait_ge(S["v_xnT"], 4 * (tg + 1))
                            if tg == 0:
                                t.wait_ge(S["d_wq"], 16 * NDC)
                            for F in range(16):
                                gidx = tg * 16 + F
                                if gidx >= 2:
                                    t.wait_ge(S["v_qk"], gidx - 1)
                                ins = None
                                for d in range(NDC):
                                    ins = t.matmul(
                                        psA[F % 2][:],
                                        wqkv[:, d * QKVN + F * 128:][:, 0:128],
                                        xnT[:, d * TGS:(d + 1) * TGS],
                                        start=(d == 0), stop=(d == NDC - 1))
                                ins.then_inc(S["t_qk"], 1)
                            for sl in range(4):
                                gidx = tg * 4 + sl
                                if gidx >= 2:
                                    t.wait_ge(S["v_v"], gidx - 1)
                                ins = None
                                for d in range(NDC):
                                    for hh in range(2):
                                        ins = t.matmul(
                                            pv[sl % 2][:, hh * 512:][:, 0:512],
                                            xnT[:, d * TGS + sl * 128:][:, 0:128],
                                            wqkv[:, d * QKVN + 2 * D + hh * 512:][:, 0:512],
                                            start=(d == 0), stop=(d == NDC - 1))
                                ins.then_inc(S["t_v"], 1)

            if debug:
                with nc.Block("dbgB1") as dbgb1:
                    @dbgb1.sync
                    def _(sync):
                        sync.dma_start(out=dxnT_h, in_=xnT[:]).then_inc(S["d_dbg"], 16)
                        sync.dma_start(out=dst1_h, in_=st1[:]).then_inc(S["d_dbg"], 16)
                        sync.dma_start(out=dxnbf_h, in_=xnbf[:]).then_inc(S["d_dbg"], 16)
                        sync.wait_ge(S["d_dbg"], 48)

            if maxphase < 2:
                return nc
            wins = []
            for n in range(NW):
                cb0 = max(n - 1, 0)
                cb1 = min(n + 1, NW - 1)
                wins.append((cb0, cb1 - cb0 + 1))

            with ExitStack() as ctx2:
                e2 = ctx2.enter_context
                scp = [e2(nc.psum_tensor(f"scp{i}", [128, 1024], F32))
                       for i in (0, 1)]
                ptp = e2(nc.psum_tensor("ptp", [128, 1024], F32))
                avp = e2(nc.psum_tensor("avp", [128, 1024], F32))

                with nc.Block("b2") as b2:
                    @b2.sync
                    def _(sync):
                        for w in range(NW + 2):
                            if w < NW:
                                n = w
                                if n >= 2:
                                    sync.wait_ge(S["t_sc"], 8 * (n - 1))
                                for d in range(NDC):
                                    sync.dma_start(
                                        out=qwin[:, (n % 2) * NDC * 128 + d * 128:][:, 0:128],
                                        in_=qh_h[d, :, n * 128:(n + 1) * 128],
                                    ).then_inc(S["d_qw"], 16)
                                if n >= 2:
                                    sync.wait_ge(S["v_e2"], n - 1)
                                sync.dma_start(
                                    out=xres[:, (n % 2) * D:][:, 0:D],
                                    in_=x_h[n * 128:(n + 1) * 128, :],
                                ).then_inc(S["d_xr"], 16)
                            if 1 <= w <= NW:
                                n = w - 1
                                sync.wait_ge(S["v_e2"], w)
                                sync.dma_start(
                                    out=x2_h[n * 128:(n + 1) * 128, :],
                                    in_=x2b[:, (n % 3) * D:][:, 0:D],
                                ).then_inc(S["d_x2"], 16)
                            if 2 <= w <= NW + 1:
                                n = w - 2
                                sync.wait_ge(S["g_e3"], n + 1)
                                if n >= 2:
                                    sync.wait_ge(S["d_xq"], 16 * NDC * (n - 1))
                                for d in range(NDC):
                                    sync.dma_start_transpose(
                                        xqtw[:, (n % 2) * NDC * 128 + d * 128:][:, 0:128],
                                        xqbf[:, (n % 2) * D + d * 128:][:, 0:128],
                                    ).then_inc(S["d_xqt"], 16)
                                sync.wait_ge(S["d_xqt"], 16 * NDC * (n + 1))
                                for d in range(NDC):
                                    sync.dma_start(
                                        out=xqh_h[d, :, n * 128:(n + 1) * 128],
                                        in_=xqtw[:, (n % 2) * NDC * 128 + d * 128:][:, 0:128],
                                    ).then_inc(S["d_xq"], 16)
                        sync.wait_ge(S["d_x2"], 16 * NW)
                        sync.wait_ge(S["d_xq"], 16 * NDC * NW)

                    @b2.tensor
                    def _(t):
                        NP = 8 * NW
                        for p in range(NP + 1):
                            if p >= 1:
                                q = p - 1
                                n, j = divmod(q, 8)
                                cb0, nch = wins[n]
                                t.wait_ge(S["s_exp"], q + 1)
                                if q >= 1:
                                    t.wait_ge(S["v_tr"], q)
                                ins = None
                                for hh in range(2):
                                    for c in range(nch):
                                        ins = t.matmul(
                                            ptp[:, hh * 512 + c * 128:][:, 0:128],
                                            probs[:, (q % 2) * 768 + hh * 384 + c * 128:][:, 0:128],
                                            ident[:], start=True, stop=True)
                                ins.then_inc(S["t_tr2"], 1)
                                t.wait_ge(S["v_tr"], q + 1)
                                if j == 0:
                                    t.wait_ge(S["v_ep"], n)
                                ins = None
                                for hh in range(2):
                                    hgl = 2 * j + hh
                                    for c in range(nch):
                                        ins = t.matmul(
                                            avp[:, hgl * 64:][:, 0:64],
                                            probsT[:, (q % 2) * 768 + hh * 384 + c * 128:][:, 0:128],
                                            vt[:, (cb0 + c) * D + hgl * 64:][:, 0:64],
                                            start=(c == 0), stop=(c == nch - 1))
                                ins.then_inc(S["t_av"], 1)
                            if p < NP:
                                n, j = divmod(p, 8)
                                cb0, nch = wins[n]
                                ctxw = nch * 128
                                if j == 0:
                                    t.wait_ge(S["d_qw"], 16 * NDC * (n + 1))
                                if p >= 2:
                                    t.wait_ge(S["s_exp"], p - 1)
                                ins = None
                                for hh in range(2):
                                    h0 = 64 * hh
                                    ins = t.matmul(
                                        scp[p % 2][:, hh * 512:][:, 0:ctxw],
                                        qwin[h0:h0 + 64,
                                             (n % 2) * NDC * 128 + j * 128:][:, 0:128],
                                        kf[h0:h0 + 64, j * T + cb0 * 128:][:, 0:ctxw],
                                        start=True, stop=True,
                                        tile_position=(h0, 0))
                                ins.then_inc(S["t_sc"], 1)

                    @b2.gpsimd
                    def _(g):
                        MAGIC = 12582912.0
                        for w in range(2, NW + 2):
                            n = w - 2
                            qb = 24 + 4 * (n % 2)
                            g.wait_ge(S["v_e3"], n + 1)
                            g.tensor_scalar(xqbf[:, (n % 2) * D:][:, 0:D],
                                            xqf[:], -MAGIC, None, ALU.add)
                            g.tensor_scalar(xqbf[:, (n % 2) * D:][:, 0:D],
                                            xqbf[:, (n % 2) * D:][:, 0:D],
                                            stS[:, qb + 1:qb + 2], None,
                                            ALU.mult).then_inc(S["g_e3"], 1)

                    @b2.scalar
                    def _(s):
                        for w in range(NW + 2):
                            if w < NW:
                                n = w
                                cb0, nch = wins[n]
                                ctxw = nch * 128
                                for j in range(8):
                                    p = 8 * n + j
                                    s.wait_ge(S["t_sc"], p + 1)
                                    if p >= 2:
                                        s.wait_ge(S["t_tr2"], p - 1)
                                    if j == 0 and n >= 2:
                                        s.wait_ge(S["v_ep"], n - 1)
                                    ins = None
                                    for hh in range(2):
                                        ins = s.activation(
                                            probs[:, (p % 2) * 768 + hh * 384:][:, 0:ctxw],
                                            scp[p % 2][:, hh * 512:][:, 0:ctxw],
                                            AF.Exp, bias=c_m20[:],
                                            accum_out=sums[:, 16 * (n % 2) + 2 * j + hh:][:, 0:1])
                                    ins.then_inc(S["s_exp"], 1)
                            if 1 <= w <= NW:
                                n1 = w - 1
                                b1c = 8 + 4 * (n1 % 2)
                                s.wait_ge(S["v_ep"], w)
                                ln_rstd_s(s, b1c)
                            if 2 <= w <= NW + 1:
                                n2 = w - 2
                                b2c = 16 + 4 * (n2 % 2)
                                s.wait_ge(S["v_e2"], n2 + 1)
                                if n2 >= 2:
                                    s.wait_ge(S["v_e3"], n2 - 1)
                                ln_rstd_s(s, b2c)
                            if 1 <= w <= NW:
                                n1 = w - 1
                                b1c = 8 + 4 * (n1 % 2)
                                s.activation(wscr[:, (n1 % 2) * D:][:, 0:D],
                                             attno[:, (n1 % 2) * D:][:, 0:D],
                                             AF.Identity, scale=stS[:, b1c + 2:b1c + 3],
                                             bias=stS[:, b1c + 3:b1c + 4]).then_inc(S["s_e1"], 1)
                            if 2 <= w <= NW + 1:
                                n2 = w - 2
                                b2c = 16 + 4 * (n2 % 2)
                                s.activation(wscrB[:, (n2 % 2) * D:][:, 0:D],
                                             x2b[:, (n2 % 3) * D:][:, 0:D],
                                             AF.Identity, scale=stS[:, b2c + 2:b2c + 3],
                                             bias=stS[:, b2c + 3:b2c + 4]).then_inc(S["s_e2"], 1)
                                qb = 24 + 4 * (n2 % 2)
                                qa = 24 + 2 * (n2 % 2)
                                s.wait_ge(S["v_e3a"], n2 + 1)
                                s.activation(stS[:, qb:qb + 1], st1[:, qa:qa + 1],
                                             AF.Relu, bias=c_negeps[:])
                                s.activation(stS[:, qb:qb + 1], stS[:, qb:qb + 1],
                                             AF.Identity, bias=c_eps[:])
                                s.activation(stS[:, qb + 1:qb + 2], stS[:, qb:qb + 1],
                                             AF.Identity,
                                             scale=1.0 / 127.0).then_inc(S["s_e3a"], 1)
                                s.wait_ge(S["v_e3b"], n2 + 1)
                                s.activation(stS[:, qb + 2:qb + 3], st1[:, qa + 1:qa + 2],
                                             AF.Identity,
                                             scale=127.0).then_inc(S["s_e3b"], 1)

                    @b2.vector
                    def _(v):
                        for w in range(NW + 2):
                            if w < NW:
                                n = w
                                cb0, nch = wins[n]
                                ctxw = nch * 128
                                for j in range(8):
                                    p = 8 * n + j
                                    v.wait_ge(S["t_tr2"], p + 1)
                                    if p >= 2:
                                        v.wait_ge(S["t_av"], p - 1)
                                    v.tensor_copy(
                                        probsT[:, (p % 2) * 768:][:, 0:768]
                                            .rearrange("p (a b) -> p a b", a=2)[:, :, 0:ctxw],
                                        ptp[:].rearrange("p (a b) -> p a b", a=2)[:, :, 0:ctxw],
                                    ).then_inc(S["v_tr"], 1)
                                b = 8 + 4 * (n % 2)
                                v.wait_ge(S["t_av"], 8 * (n + 1))
                                v.wait_ge(S["s_exp"], 8 * (n + 1))
                                if n >= 2:
                                    v.wait_ge(S["s_e1"], n - 1)
                                v.reciprocal(sinv[:], sums[:, 16 * (n % 2):][:, 0:16])
                                vpad(v)
                                ao = attno[:, (n % 2) * D:][:, 0:D]
                                for hgl in range(H):
                                    v.tensor_scalar(ao[:, hgl * 64:][:, 0:64],
                                                    avp[:, hgl * 64:][:, 0:64],
                                                    sinv[:, hgl:hgl + 1], None,
                                                    ALU.mult)
                                ln_stats_v(v, b, ao).then_inc(S["v_ep"], 1)
                            if 1 <= w <= NW:
                                n = w - 1
                                b = 16 + 4 * (n % 2)
                                v.wait_ge(S["s_e1"], n + 1)
                                v.wait_ge(S["d_xr"], 16 * (n + 1))
                                if n >= 2:
                                    v.wait_ge(S["d_x2"], 16 * (n - 1))
                                xb2 = x2b[:, (n % 3) * D:][:, 0:D]
                                v.tensor_tensor(xb2, wscr[:, (n % 2) * D:][:, 0:D],
                                                xres[:, (n % 2) * D:][:, 0:D],
                                                ALU.add)
                                ln_stats_v(v, b, xb2).then_inc(S["v_e2"], 1)
                            if 2 <= w <= NW + 1:
                                n = w - 2
                                v.wait_ge(S["s_e2"], n + 1)
                                qb = 24 + 4 * (n % 2)
                                qa = 24 + 2 * (n % 2)
                                xn3 = wscrB[:, (n % 2) * D:][:, 0:D]
                                v.reduce_max(st1[:, qa:qa + 1], xn3, axis=AX.X,
                                             apply_absolute_value=True).then_inc(S["v_e3a"], 1)
                                v.wait_ge(S["s_e3a"], n + 1)
                                v.reciprocal(st1[:, qa + 1:qa + 2],
                                             stS[:, qb:qb + 1]).then_inc(S["v_e3b"], 1)
                                v.wait_ge(S["s_e3b"], n + 1)
                                if n >= 2:
                                    v.wait_ge(S["d_xqt"], 16 * NDC * (n - 1))
                                MAGIC = 12582912.0
                                ins = None
                                for hf in range(2):
                                    ins = v.tensor_scalar(
                                        xqf[:, hf * 512:][:, 0:512],
                                        xn3[:, hf * 512:][:, 0:512],
                                        stS[:, qb + 2:qb + 3], MAGIC, ALU.mult, ALU.add)
                                ins.then_inc(S["v_e3"], 1)

            if debug:
                with nc.Block("dbgA") as dbga:
                    @dbga.sync
                    def _(sync):
                        sync.dma_start(out=dk_h, in_=kf[:]).then_inc(S["d_dbg"], 16)
                        sync.dma_start(out=dv_h, in_=vt[:]).then_inc(S["d_dbg"], 16)
                        sync.wait_ge(S["d_dbg"], 48 + 32)

        if maxphase < 3:
            return nc
        with ExitStack() as ctxB:
            eb = ctxB.enter_context
            tq1T = eb(nc.sbuf_tensor("tq1T", [128, NDC * DH], BF16,
                                     is_transpose_dest=True))
            tq2T = eb(nc.sbuf_tensor("tq2T", [128, NHC * D], BF16,
                                     is_transpose_dest=True))

            chunks = []
            for c in range(NHC):
                chunks.append(("w1", c))
            for c in range(8):
                for g2 in range(4):
                    chunks.append(("w2", (c, g2)))
            NCH = len(chunks)

            with ExitStack() as ctx3:
                e3 = ctx3.enter_context
                wst = e3(nc.sbuf_tensor("wst", [128, 2 * 1024], F32))
                wst2 = e3(nc.sbuf_tensor("wst2", [128, 2 * 1024], F32))
                wtmp = e3(nc.sbuf_tensor("wtmp", [128, 1024], F32))
                absb = e3(nc.sbuf_tensor("absb", [128, 2 * 2048], F32))
                tqn = e3(nc.sbuf_tensor("tqn", [128, 2 * 1024], BF16))
                swsc = e3(nc.sbuf_tensor("swsc", [1, 8], F32))
                psw = e3(nc.psum_tensor("psw", [128, 8], F32))
                pp1 = e3(nc.psum_tensor("pp1", [128, 512], F32))
                pp2 = e3(nc.psum_tensor("pp2", [128, 512], F32))

                def chunk_src(ch):
                    kind, meta = chunks[ch]
                    if kind == "w1":
                        return w1_h[meta * 128:(meta + 1) * 128, :]
                    c, g2 = meta
                    return w2_h[c * 128:(c + 1) * 128, g2 * 1024:(g2 + 1) * 1024]

                with nc.Block("b3") as b3:
                    @b3.sync
                    def _(sync):
                        for ch in range(NCH):
                            if ch >= 2:
                                sync.wait_ge(S["g_p1"], ch - 1)
                            sync.dma_start(
                                out=wst[:, (ch % 2) * 1024:][:, 0:1024],
                                in_=chunk_src(ch),
                            ).then_inc(S["d_w1"], 16)
                        for ch in range(NCH + 2):
                            if ch < NCH:
                                if ch >= 2:
                                    sync.wait_ge(S["v_q2"], 2 * (ch - 1))
                                sync.dma_start(
                                    out=wst2[:, (ch % 2) * 1024:][:, 0:1024],
                                    in_=chunk_src(ch),
                                ).then_inc(S["d_w2"], 16)
                            if ch >= 2:
                                ch2 = ch - 2
                                kind, meta = chunks[ch2]
                                sync.wait_ge(S["v_q2"], 2 * (ch2 + 1))
                                for jj in range(8):
                                    src = tqn[:, (ch2 % 2) * 1024 + jj * 128:][:, 0:128]
                                    if kind == "w1":
                                        dst = tq1T[:, jj * DH + meta * 128:][:, 0:128]
                                    else:
                                        c, g2 = meta
                                        hc = g2 * 8 + jj
                                        dst = tq2T[:, hc * D + c * 128:][:, 0:128]
                                    sync.dma_start_transpose(dst, src).then_inc(
                                        S["d_tq"], 16)
                        sync.wait_ge(S["d_tq"], 16 * 8 * NCH)

                    @b3.gpsimd
                    def _(g):
                        for ch in range(NCH):
                            g.wait_ge(S["d_w1"], 16 * (ch + 1))
                            if ch >= 2:
                                g.wait_ge(S["t_p1"], ch - 1)
                            wsrc = wst[:, (ch % 2) * 1024:][:, 0:1024]
                            ab = absb[:, (ch % 2) * 2048:][:, 0:2048]
                            g.tensor_scalar(ab[:, 0:1024], wsrc, 0.0, None,
                                            ALU.max)
                            g.tensor_scalar(ab[:, 1024:2048], wsrc, -1.0, 0.0,
                                            ALU.mult, ALU.max).then_inc(S["g_p1"], 1)

                    @b3.tensor
                    def _(t):
                        n1 = 0
                        n2 = 0
                        for ch in range(NCH):
                            t.wait_ge(S["g_p1"], ch + 1)
                            ab = absb[:, (ch % 2) * 2048:][:, 0:2048]
                            pw = pp1 if chunks[ch][0] == "w1" else pp2
                            first = (n1 == 0) if chunks[ch][0] == "w1" else (n2 == 0)
                            ins = None
                            for q4 in range(4):
                                ins = t.matmul(pw[0:1, :], onescol[:],
                                               ab[:, q4 * 512:][:, 0:512],
                                               start=(first and q4 == 0),
                                               stop=(q4 == 3 and (ch == NHC - 1 or ch == NCH - 1)))
                            if chunks[ch][0] == "w1":
                                n1 += 1
                            else:
                                n2 += 1
                            ins.then_inc(S["t_p1"], 1)
                        t.wait_ge(S["v_sw"], 1)
                        t.matmul(psw[:, 4:8], onesrow[:], swsc[0:1, 0:4],
                                 start=True, stop=True).then_inc(S["t_sw"], 1)

                    @b3.vector
                    def _(v):
                        v.wait_ge(S["t_p1"], NCH)
                        v.reduce_sum(swsc[0:1, 4:5], pp1[0:1, :], axis=AX.X)
                        v.reduce_sum(swsc[0:1, 5:6], pp2[0:1, :], axis=AX.X)
                        vpad(v)
                        vpad(v)
                        v.tensor_scalar(swsc[0:1, 4:6], swsc[0:1, 4:6],
                                        1.0 / (DH * D), 1e-5, ALU.mult, ALU.max)
                        vpad(v)
                        v.reciprocal(swsc[0:1, 6:8], swsc[0:1, 4:6])
                        vpad(v)
                        vpad(v)
                        v.tensor_copy(swsc[0:1, 0:1], swsc[0:1, 6:7])
                        v.tensor_copy(swsc[0:1, 1:2], swsc[0:1, 4:5])
                        v.tensor_copy(swsc[0:1, 2:3], swsc[0:1, 7:8])
                        v.tensor_copy(swsc[0:1, 3:4], swsc[0:1, 5:6])
                        vpad(v)
                        v.engine_nop().then_inc(S["v_sw"], 1)
                        v.wait_ge(S["t_sw"], 1)
                        v.tensor_copy(swv[:], psw[:, 4:8])
                        vpad(v)
                        v.tensor_scalar(cwsc[:], cw[:], swv[:, 1:2], None,
                                        ALU.mult).then_inc(S["v_sw"], 1)
                        for ch in range(NCH):
                            v.wait_ge(S["d_w2"], 16 * (ch + 1))
                            if ch >= 2:
                                v.wait_ge(S["d_tq"], 16 * 8 * (ch - 1))
                            wsrc = wst2[:, (ch % 2) * 1024:][:, 0:1024]
                            scol = 0 if chunks[ch][0] == "w1" else 2
                            CLIP = 1.4999998807907104
                            MAGIC = 12582912.0
                            v.tensor_scalar(wtmp[:], wsrc, swv[:, scol:scol + 1],
                                            CLIP, ALU.mult,
                                            ALU.min).then_inc(S["v_q2"], 1)
                            v.tensor_scalar(wtmp[:], wtmp[:], -CLIP, MAGIC,
                                            ALU.max, ALU.add)
                            v.tensor_scalar(tqn[:, (ch % 2) * 1024:][:, 0:1024],
                                            wtmp[:], MAGIC, None,
                                            ALU.subtract).then_inc(S["v_q2"], 1)

            if maxphase < 4:
                if debug:
                    with nc.Block("dbgB3") as dbgb3:
                        @dbgb3.sync
                        def _(sync):
                            sync.dma_start(out=dt1_h, in_=tq1T[:]).then_inc(S["d_dbg"], 16)
                            sync.dma_start(out=dt2_h, in_=tq2T[:]).then_inc(S["d_dbg"], 16)
                            sync.dma_start(out=dsw_h, in_=swv[:]).then_inc(S["d_dbg"], 16)
                            sync.wait_ge(S["d_dbg"], 48 + 80)
                return nc
            hx = eb(nc.sbuf_tensor("hx", [128, NHC * HEXW], BF16))
            cbuf = eb(nc.sbuf_tensor("cbuf", [128, 2 * 516], BF16))
            ftmp = eb(nc.sbuf_tensor("ftmp", [128, 516], F32))
            accb = eb(nc.sbuf_tensor("accb", [128, 513], BF16))
            accm = eb(nc.sbuf_tensor("accm", [128, 513], BF16))
            s2b = eb(nc.sbuf_tensor("s2b", [128, 513], F32))
            mxb = eb(nc.sbuf_tensor("mxb", [128, 32], F32))
            rowsb = eb(nc.sbuf_tensor("rowsb", [1, 640], F32))
            xqt = eb(nc.sbuf_tensor("xqt", [128, 2 * NDC * 512], BF16))
            x2i = eb(nc.sbuf_tensor("x2i", [128, 2 * D], F32))
            outy = eb(nc.sbuf_tensor("outy", [128, D], F32))
            outst = eb(nc.sbuf_tensor("outst", [128, 2 * D], F32))
            hcar = eb(nc.sbuf_tensor("hcar", [128, 2 * NHC], BF16))
            if debug:
                capC = eb(nc.sbuf_tensor("capC", [128, 8 * NHC], BF16))
                capA = eb(nc.sbuf_tensor("capA", [128, 513], BF16))
                capM = eb(nc.sbuf_tensor("capM", [128, 32], F32))
                capS = eb(nc.sbuf_tensor("capS", [128, 512], F32))
            ps1 = [eb(nc.psum_tensor(f"ps1{i}", [128, 512], F32)) for i in range(3)]
            ps2 = eb(nc.psum_tensor("ps2", [128, 1024], F32))
            ps3 = eb(nc.psum_tensor("ps3", [128, 1024], F32))

            tstart = [0]
            for td in TILES:
                tstart.append(tstart[-1] + len(td["slices"]))

            with nc.Block("b4") as b4:
                @b4.sync
                def _(sync):
                    def xqt_load(ti):
                        if ti >= 2:
                            sync.wait_ge(S["t_f1"], NHC * (ti - 1))
                        for d in range(NDC):
                            sync.dma_start(
                                out=xqt[:, (ti % 2) * NDC * 512 + d * 512:][:, 0:512],
                                in_=xqh_h[d, :, ti * FT:(ti + 1) * FT],
                            ).then_inc(S["d_xt"], 16)
                    xqt_load(0)
                    xqt_load(1)
                    for gi, (ti, si, c0, wd) in enumerate(GSL):
                        if si == 0 and ti >= 2:
                            xqt_load(ti)
                        r0 = TILES[ti]["tok0"] + c0
                        if gi >= 2:
                            sync.wait_ge(S["v_l4"], gi - 1)
                        sync.dma_start(
                            out=x2i[:, (gi % 2) * D:][:, 0:D][0:wd, :],
                            in_=x2_h[r0:r0 + wd, :],
                        ).then_inc(S["d_x2i"], 16)
                        if gi >= 2:
                            pi, psi, pc0, pwd = GSL[gi - 2]
                            pr0 = TILES[pi]["tok0"] + pc0
                            sync.wait_ge(S["v_l4"], gi - 1)
                            sync.dma_start(
                                out=out_h[pr0:pr0 + pwd, :],
                                in_=outst[:, ((gi - 2) % 2) * D:][:, 0:D][0:pwd, :],
                            ).then_inc(S["d_out"], 16)
                    for gi in (NSL - 2, NSL - 1):
                        ti, si, c0, wd = GSL[gi]
                        r0 = TILES[ti]["tok0"] + c0
                        sync.wait_ge(S["v_l4"], gi + 1)
                        sync.dma_start(
                            out=out_h[r0:r0 + wd, :],
                            in_=outst[:, (gi % 2) * D:][:, 0:D][0:wd, :],
                        ).then_inc(S["d_out"], 16)
                    sync.wait_ge(S["d_out"], 16 * NSL)

                @b4.gpsimd
                def _(g):
                    g.memset(hcar[:], 0.0)
                    g.memset(hx[:].rearrange("p (c w) -> p c w", c=NHC)[:, :, 514:515],
                             0.0)
                    g.memset(mxb[:], 1.0)
                    g.memset(accb[:], 0.0)
                    if debug:
                        g.memset(s2b[:], 0.0)
                    g.engine_nop().then_inc(S["g_i4"], 1)
                    for ti, td in enumerate(TILES):
                        ccmin, ccmax = td["ccmin"], td["ccmax"]
                        ccw = ccmax - ccmin
                        for c in range(NHC):
                            g.wait_ge(S["v_s2"], 3 * (ti + 1))
                            MAGIC = 12582912.0
                            hs = hx[:, c * HEXW + ccmin + 1:][:, 0:ccw]
                            g.tensor_tensor(ftmp[:, 0:ccw], hs,
                                            s2b[:, ccmin:][:, 0:ccw], ALU.mult)
                            g.tensor_scalar(hs, ftmp[:, 0:ccw], MAGIC, -MAGIC,
                                            ALU.add, ALU.add).then_inc(S["g_rd"], 1)

                @b4.tensor
                def _(t):
                    for ti, td in enumerate(TILES):
                        nsg = len(td["slices"])
                        for c in range(NHC):
                            k = ti * NHC + c
                            if c == 0:
                                t.wait_ge(S["d_xt"], 16 * NDC * (ti + 1))
                                if ti == 0:
                                    t.wait_ge(S["g_i4"], 1)
                            if k >= 3:
                                t.wait_ge(S["v_ev"], k - 2)
                            ins = None
                            for d in range(NDC):
                                ins = t.matmul(
                                    ps1[c % 3][:],
                                    tq1T[:, d * DH + c * 128:][:, 0:128],
                                    xqt[:, (ti % 2) * NDC * 512 + d * 512:][:, 0:512],
                                    start=(d == 0), stop=(d == NDC - 1))
                            ins.then_inc(S["t_f1"], 1)
                        t.wait_ge(S["g_am"], NHC * (ti + 1))
                        ins = None
                        for gl, (c0, wd) in enumerate(td["slices"]):
                            ins = t.matmul(ps3[0:wd, 128 * gl:][:, 0:128],
                                           accb[:, c0:c0 + wd], ident[:],
                                           start=True, stop=True)
                        ins.then_inc(S["t_s2"], 1)
                        t.wait_ge(S["s_s2b"], ti + 1)
                        ins = None
                        for gl, (c0, wd) in enumerate(td["slices"]):
                            ins = t.matmul(ps3[0:1, 128 * gl:][:, 0:128],
                                           mxb[0:wd, 24 + gl:25 + gl],
                                           identf[0:wd, :],
                                           start=True, stop=True)
                        ins.then_inc(S["t_s2"], 1)
                        t.wait_ge(S["v_s2"], 3 * ti + 2)
                        ins = None
                        for gl, (c0, wd) in enumerate(td["slices"]):
                            ins = t.matmul(ps3[:, 128 * gl:][:, 0:wd],
                                           onesrow[:],
                                           rowsb[0:1, 128 * gl:][:, 0:wd],
                                           start=True, stop=True)
                        ins.then_inc(S["t_s2"], 1)
                        t.wait_ge(S["g_rd"], NHC * (ti + 1))
                        for si, (c0, wd) in enumerate(td["slices"]):
                            gi = tstart[ti] + si
                            if gi >= 1:
                                t.wait_ge(S["v_f2"], gi)
                            ins = None
                            for c in range(NHC):
                                for half in range(2):
                                    ins = t.matmul(
                                        ps2[0:wd, half * 512:][:, 0:512],
                                        hx[:, c * HEXW + 1 + c0:][:, 0:wd],
                                        tq2T[:, c * D + half * 512:][:, 0:512],
                                        start=(c == 0), stop=(c == NHC - 1))
                            ins.then_inc(S["t_f2"], 1)

                @b4.vector
                def _(v):
                    for ti, td in enumerate(TILES):
                        ccmin, ccmax = td["ccmin"], td["ccmax"]
                        ccw = ccmax - ccmin
                        nsg = len(td["slices"])
                        for c in range(NHC):
                            k = ti * NHC + c
                            v.wait_ge(S["t_f1"], k + 1)
                            if k >= 2:
                                v.wait_ge(S["s_si"], k - 1)
                            if c == 0 and ti >= 1:
                                v.wait_ge(S["t_f2"], tstart[ti])
                                v.wait_ge(S["g_rd"], NHC * ti)
                            v.tensor_copy(hx[:, c * HEXW:][:, 0:2],
                                          hcar[:, 2 * c:][:, 0:2])
                            v.tensor_copy(hcar[:, 2 * c:][:, 0:2],
                                          ps1[c % 3][:, 510:512])
                            v.tensor_copy(hx[:, c * HEXW + 2:][:, 0:512],
                                          ps1[c % 3][:]).then_inc(S["v_ev"], 1)
                            if debug and ti == 1:
                                v.tensor_copy(capC[:, c * 8:(c + 1) * 8],
                                              hx[:, c * HEXW:][:, 0:8])
                            cbs = cbuf[:, (k % 2) * 516:][:, 0:ccw]
                            v.tensor_scalar(cbs,
                                            hx[:, c * HEXW + ccmin + 1:][:, 0:ccw],
                                            cwsc[:, 3 * c + 1:3 * c + 2],
                                            cbv_[:, c:c + 1], ALU.mult, ALU.add)
                            v.scalar_tensor_tensor(cbs,
                                                   hx[:, c * HEXW + ccmin:][:, 0:ccw],
                                                   cwsc[:, 3 * c:3 * c + 1], cbs,
                                                   ALU.mult, ALU.add)
                            v.scalar_tensor_tensor(cbs,
                                                   hx[:, c * HEXW + ccmin + 2:][:, 0:ccw],
                                                   cwsc[:, 3 * c + 2:3 * c + 3], cbs,
                                                   ALU.mult, ALU.add).then_inc(S["v_cv"], 1)
                            if c >= 1:
                                cm = c - 1
                                v.wait_ge(S["s_si"], ti * NHC + cm + 1)
                                hsm = hx[:, cm * HEXW + ccmin + 1:][:, 0:ccw]
                                acs = accb[:, ccmin:][:, 0:ccw]
                                acm = accm[:, ccmin:][:, 0:ccw]
                                if cm == 0:
                                    if ti >= 1:
                                        v.wait_ge(S["t_s2"], 3 * ti - 2)
                                    v.tensor_copy(acs, hsm)
                                    v.tensor_copy(acm, hsm).then_inc(S["g_am"], 1)
                                else:
                                    v.tensor_tensor(acs, acs, hsm, ALU.max)
                                    v.tensor_tensor(acm, acm, hsm,
                                                    ALU.min).then_inc(S["g_am"], 1)
                        cm = NHC - 1
                        v.wait_ge(S["s_si"], ti * NHC + cm + 1)
                        hsm = hx[:, cm * HEXW + ccmin + 1:][:, 0:ccw]
                        acs = accb[:, ccmin:][:, 0:ccw]
                        acm = accm[:, ccmin:][:, 0:ccw]
                        v.tensor_tensor(acs, acs, hsm, ALU.max)
                        v.tensor_tensor(acm, acm, hsm, ALU.min)
                        v.tensor_scalar(acm, acm, -1.0, None, ALU.mult)
                        v.tensor_tensor(acs, acs, acm,
                                        ALU.max).then_inc(S["g_am"], 1)
                        if debug and ti == 1:
                            v.tensor_copy(capA[:], accb[:])
                        v.wait_ge(S["t_s2"], 3 * ti + 1)
                        for gl, (c0, wd) in enumerate(td["slices"]):
                            v.reduce_max(mxb[0:wd, gl:gl + 1],
                                         ps3[0:wd, 128 * gl:][:, 0:128], axis=AX.X)
                        v.engine_nop().then_inc(S["v_s2"], 1)
                        v.wait_ge(S["s_s2a"], ti + 1)
                        v.reciprocal(st1[:, 32:32 + nsg],
                                     mxb[:, 16:16 + nsg]).then_inc(S["v_s2b"], 1)
                        v.wait_ge(S["t_s2"], 3 * ti + 2)
                        v.tensor_copy(rowsb[0:1, 0:128 * nsg],
                                      ps3[0:1, 0:128 * nsg]).then_inc(S["v_s2"], 1)
                        v.wait_ge(S["t_s2"], 3 * ti + 3)
                        ins = None
                        for gl, (c0, wd) in enumerate(td["slices"]):
                            ins = v.tensor_copy(s2b[:, c0:c0 + wd],
                                                ps3[:, 128 * gl:][:, 0:wd])
                        ins.then_inc(S["v_s2"], 1)
                        if debug and ti == 1:
                            v.tensor_copy(capS[:], s2b[:, 1:513])
                        for si, (c0, wd) in enumerate(td["slices"]):
                            gi = tstart[ti] + si
                            b = 28 + 4 * (gi % 2)
                            v.wait_ge(S["t_f2"], gi + 1)
                            v.wait_ge(S["d_x2i"], 16 * (gi + 1))
                            if gi >= 1:
                                v.wait_ge(S["s_l4"], gi)
                            if debug and gi == 4:
                                v.tensor_copy(capM[:], mxb[:])
                            for half in range(2):
                                v.tensor_scalar(
                                    outy[0:wd, half * 512:][:, 0:512],
                                    ps2[0:wd, half * 512:][:, 0:512],
                                    mxb[0:wd, 8 + si:9 + si], None, ALU.mult)
                            oy = outy[0:wd, :]
                            ost = outst[0:wd, (gi % 2) * D:][:, 0:D]
                            ln_stats_v(v, b, oy, sub=slice(0, wd)).then_inc(S["v_f2"], 1)
                            v.wait_ge(S["s_l4"], gi + 1)
                            v.tensor_tensor(ost, ost,
                                            x2i[0:wd, (gi % 2) * D:][:, 0:D],
                                            ALU.add).then_inc(S["v_l4"], 1)

                @b4.scalar
                def _(s):
                    for ti, td in enumerate(TILES):
                        ccmin, ccmax = td["ccmin"], td["ccmax"]
                        ccw = ccmax - ccmin
                        nsg = len(td["slices"])
                        for c in range(NHC):
                            k = ti * NHC + c
                            s.wait_ge(S["v_cv"], k + 1)
                            s.activation(hx[:, c * HEXW + ccmin + 1:][:, 0:ccw],
                                         cbuf[:, (k % 2) * 516:][:, 0:ccw],
                                         AF.Silu).then_inc(S["s_si"], 1)
                        s.wait_ge(S["v_s2"], 3 * ti + 1)
                        if ti >= 1:
                            s.wait_ge(S["v_f2"], tstart[ti])
                        s.activation(mxb[:, 16:16 + nsg], mxb[:, 0:nsg],
                                     AF.Relu, bias=c_negeps[:])
                        s.activation(mxb[:, 16:16 + nsg], mxb[:, 16:16 + nsg],
                                     AF.Identity, bias=c_eps[:])
                        s.activation(mxb[:, 8:8 + nsg], mxb[:, 16:16 + nsg],
                                     AF.Identity, scale=swv[:, 3:4])
                        s.activation(mxb[:, 8:8 + nsg], mxb[:, 8:8 + nsg],
                                     AF.Identity,
                                     scale=1.0 / 127.0).then_inc(S["s_s2a"], 1)
                        s.wait_ge(S["v_s2b"], ti + 1)
                        s.activation(mxb[:, 24:24 + nsg], st1[:, 32:32 + nsg],
                                     AF.Identity, scale=127.0).then_inc(S["s_s2b"], 1)
                        for si, (c0, wd) in enumerate(td["slices"]):
                            gi = tstart[ti] + si
                            b = 28 + 4 * (gi % 2)
                            s.wait_ge(S["v_f2"], gi + 1)
                            if gi >= 2:
                                s.wait_ge(S["d_out"], 16 * (gi - 1))
                            ln_rstd_s(s, b, sub=slice(0, wd))
                            s.activation(jkA[:, 0:256], jkB[:, 0:256], AF.Identity)
                            s.activation(jkB[:, 0:256], jkA[:, 0:256], AF.Identity)
                            s.activation(outst[0:wd, (gi % 2) * D:][:, 0:D],
                                         outy[0:wd, :], AF.Identity,
                                         scale=stS[0:wd, b + 2:b + 3],
                                         bias=stS[0:wd, b + 3:b + 4]).then_inc(S["s_l4"], 1)

            if debug:
                with nc.Block("dbgB") as dbgb:
                    @dbgb.sync
                    def _(sync):
                        sync.dma_start(out=dt1_h, in_=tq1T[:]).then_inc(S["d_dbg"], 16)
                        sync.dma_start(out=dt2_h, in_=tq2T[:]).then_inc(S["d_dbg"], 16)
                        sync.dma_start(out=dsw_h, in_=swv[:]).then_inc(S["d_dbg"], 16)
                        sync.dma_start(out=dhx_h, in_=hx[:]).then_inc(S["d_dbg"], 16)
                        sync.dma_start(out=dcC_h, in_=capC[:]).then_inc(S["d_dbg"], 16)
                        sync.dma_start(out=dcA_h, in_=capA[:]).then_inc(S["d_dbg"], 16)
                        sync.dma_start(out=dcM_h, in_=capM[:]).then_inc(S["d_dbg"], 16)
                        sync.dma_start(out=dcS_h, in_=capS[:]).then_inc(S["d_dbg"], 16)
                        sync.wait_ge(S["d_dbg"], 48 + 96 + 64)

    return nc


def kernel(L0_attn_x, L0_attn_w_qkv, L0_ffn_w1, L0_ffn_conv_w, L0_ffn_conv_b,
           L0_ffn_w2, _debug=False, _maxphase=4):
    x = np.ascontiguousarray(np.asarray(L0_attn_x, dtype=np.float32))
    wqkv = np.ascontiguousarray(np.asarray(L0_attn_w_qkv, dtype=np.float32))
    w1 = np.ascontiguousarray(np.asarray(L0_ffn_w1, dtype=np.float32))
    convw = np.ascontiguousarray(
        np.asarray(L0_ffn_conv_w, dtype=np.float32).reshape(DH, 3))
    convb = np.ascontiguousarray(
        np.asarray(L0_ffn_conv_b, dtype=np.float32).reshape(DH, 1))
    w2 = np.ascontiguousarray(np.asarray(L0_ffn_w2, dtype=np.float32))

    nc = build_nc(debug=_debug, maxphase=_maxphase)
    in_maps = [{"x": x[i], "wqkv": wqkv, "w1": w1, "convw": convw,
                "convb": convb, "w2": w2} for i in range(8)]
    res = run_bass_kernel_spmd(nc, in_maps, core_ids=list(range(8)))
    out = np.stack([np.asarray(r["out"]) for r in res.results], axis=0)
    if _debug:
        return out, res
    return out


# version 02
# speedup vs baseline: 1.9031x; 1.9031x over previous
import numpy as np
import concourse.bass as bass
import concourse.mybir as mybir
from concourse.bass_utils import run_bass_kernel_spmd
from contextlib import ExitStack

F32 = mybir.dt.float32
BF16 = mybir.dt.bfloat16
I32 = mybir.dt.int32
AF = mybir.ActivationFunctionType
ALU = mybir.AluOpType
AX = mybir.AxisListType

T = 2048
D = 1024
H = 16
E = 64
W = 128
NW = T // W
DH = 4096
NDC = D // 128
NHC = DH // 128
TGS = 512
NTG = T // TGS
FT = 512
NFT = T // FT
QKVN = 3 * D
EXP_SHIFT = -20.0
HEXW = 515


def _ffn_tiles():
    tiles = []
    for i in range(NFT):
        tok0 = FT * i - 1
        if i == 0:
            ccmin, ccmax = 1, 512
            slices = [(1, 128), (129, 128), (257, 128), (385, 127)]
        elif i < NFT - 1:
            ccmin, ccmax = 0, 512
            slices = [(0, 128), (128, 128), (256, 128), (384, 128)]
        else:
            ccmin, ccmax = 0, 513
            slices = [(0, 128), (128, 128), (256, 128), (384, 128), (512, 1)]
        tiles.append(dict(i=i, tok0=tok0, ccmin=ccmin, ccmax=ccmax,
                          slices=slices))
    return tiles


TILES = _ffn_tiles()
GSL = []
for _td in TILES:
    for _si, (_c0, _wd) in enumerate(_td["slices"]):
        GSL.append((_td["i"], _si, _c0, _wd))
NSL = len(GSL)


def build_nc(debug=False, maxphase=4):
    nc = bass.Bass("TRN2", target_bir_lowering=False,
                   dynamic_dma_scratch_size=4096,
                   detect_race_conditions=False)

    x_h = nc.dram_tensor("x", [T, D], F32, kind="ExternalInput").ap()
    wqkv_h = nc.dram_tensor("wqkv", [D, QKVN], F32, kind="ExternalInput").ap()
    w1_h = nc.dram_tensor("w1", [DH, D], F32, kind="ExternalInput").ap()
    convw_h = nc.dram_tensor("convw", [DH, 3], F32, kind="ExternalInput").ap()
    convb_h = nc.dram_tensor("convb", [DH, 1], F32, kind="ExternalInput").ap()
    w2_h = nc.dram_tensor("w2", [D, DH], F32, kind="ExternalInput").ap()
    out_h = nc.dram_tensor("out", [T, D], F32, kind="ExternalOutput").ap()

    dbgk = "ExternalOutput" if debug else "Internal"
    qh_h = nc.dram_tensor("qh", [NDC, 128, T], BF16, kind=dbgk).ap()
    x2_h = nc.dram_tensor("x2s", [T, D], F32, kind=dbgk).ap()
    xqh_h = nc.dram_tensor("xqs", [NDC, 128, T], BF16, kind=dbgk).ap()
    if debug:
        dk_h = nc.dram_tensor("dk", [128, NDC * T], BF16, kind="ExternalOutput").ap()
        dxnT_h = nc.dram_tensor("dxnT", [128, NDC * TGS], BF16, kind="ExternalOutput").ap()
        dst1_h = nc.dram_tensor("dst1", [128, 48], F32, kind="ExternalOutput").ap()
        dxnbf_h = nc.dram_tensor("dxnbf", [128, 2 * D], BF16, kind="ExternalOutput").ap()
        dv_h = nc.dram_tensor("dv", [128, NW * D], BF16, kind="ExternalOutput").ap()
        dt1_h = nc.dram_tensor("dt1", [128, NDC * DH], BF16, kind="ExternalOutput").ap()
        dt2_h = nc.dram_tensor("dt2", [128, NHC * D], BF16, kind="ExternalOutput").ap()
        dsw_h = nc.dram_tensor("dsw", [128, 4], F32, kind="ExternalOutput").ap()
        dhx_h = nc.dram_tensor("dhx", [128, NHC * HEXW], BF16, kind="ExternalOutput").ap()

    ctx = ExitStack()
    with ctx:
        e = ctx.enter_context

        ident = e(nc.sbuf_tensor("ident", [128, 128], BF16))
        identf = e(nc.sbuf_tensor("identf", [128, 128], F32))
        onesrow = e(nc.sbuf_tensor("onesrow", [1, 128], F32))
        onescol = e(nc.sbuf_tensor("onescol", [128, 1], F32))
        c_m20 = e(nc.sbuf_tensor("c_m20", [128, 1], F32))
        c_eps = e(nc.sbuf_tensor("c_eps", [128, 1], F32))
        c_negeps = e(nc.sbuf_tensor("c_negeps", [128, 1], F32))
        c_mg = e(nc.sbuf_tensor("c_mg", [128, 1], F32))
        c_nmg = e(nc.sbuf_tensor("c_nmg", [128, 1], F32))
        cw = e(nc.sbuf_tensor("cw", [128, 3 * NHC], F32))
        cbv_ = e(nc.sbuf_tensor("cbv", [128, NHC], F32))
        cwsc = e(nc.sbuf_tensor("cwsc", [128, 3 * NHC], F32))
        swv = e(nc.sbuf_tensor("swv", [128, 4], F32))
        sums = e(nc.sbuf_tensor("sums", [128, 32], F32))
        sinv = e(nc.sbuf_tensor("sinv", [128, 16], F32))
        st1 = e(nc.sbuf_tensor("st1", [128, 48], F32))
        wacc = e(nc.sbuf_tensor("wacc", [128, 2], F32))
        sqb = e(nc.sbuf_tensor("sqb", [128, D], F32))
        jkA = e(nc.sbuf_tensor("jkA", [128, 256], F32))
        jkB = e(nc.sbuf_tensor("jkB", [128, 256], F32))
        stS = e(nc.sbuf_tensor("stS", [128, 48], F32))

        S = {}
        for name in ["d_wq", "d_cw", "d_dbg",
                     "d_x", "v_st", "s_xn", "t_tr", "v_xnT",
                     "t_qk", "v_qk", "t_v", "v_v", "d_qh",
                     "d_qw", "d_xr", "d_x2", "d_xq", "d_xqt",
                     "t_sc", "s_exp", "t_tr2", "v_tr", "t_av",
                     "v_ep", "s_e1", "v_e2", "s_e2", "v_e3", "s_r1",
                     "v_e3a", "s_e3a", "v_e3b", "s_e3b",
                     "v_s2b", "s_s2a", "s_s2b", "g_e3",
                     "d_w1", "g_p1", "t_p1", "t_sw", "v_sw", "d_w2", "v_q2", "d_tq",
                     "d_xt", "d_x2i", "t_f1", "v_ev", "v_cv", "s_si", "g_am",
                     "t_s2", "v_s2", "g_rd", "t_f2", "v_f2", "s_l4", "v_l4",
                     "d_out", "g_i4", "v_p1", "t_tq", "v_tqe", "v_rm", "s_rd"]:
            S[name] = e(nc.semaphore(name))

        def ln_stats_v(v, b, src, sub=None):
            p = sub if sub is not None else slice(0, 128)
            v.scalar_tensor_tensor(sqb[p, :], src, 1.0, src, ALU.bypass, ALU.mult)
            v.reduce_sum(st1[p, b:b + 1], src, axis=AX.X)
            return v.reduce_sum(st1[p, b + 1:b + 2], sqb[p, :], axis=AX.X)

        def ln_rstd_s(s, b, sub=None):
            p = sub if sub is not None else slice(0, 128)
            s.activation(stS[p, b:b + 1], st1[p, b:b + 1], AF.Identity,
                         scale=1.0 / D)
            s.activation(stS[p, b + 1:b + 2], stS[p, b:b + 1], AF.Identity,
                         scale=stS[p, b:b + 1])
            s.activation(stS[p, b + 1:b + 2], stS[p, b + 1:b + 2], AF.Identity,
                         scale=-1.0)
            s.activation(stS[p, b + 1:b + 2], st1[p, b + 1:b + 2], AF.Identity,
                         scale=1.0 / D, bias=stS[p, b + 1:b + 2])
            s.activation(stS[p, b + 2:b + 3], stS[p, b + 1:b + 2], AF.Ln,
                         bias=c_eps[p, :])
            s.activation(stS[p, b + 2:b + 3], stS[p, b + 2:b + 3], AF.Exp,
                         scale=-0.5)
            s.activation(stS[p, b + 3:b + 4], stS[p, b:b + 1], AF.Identity,
                         scale=stS[p, b + 2:b + 3])
            return s.activation(stS[p, b + 3:b + 4], stS[p, b + 3:b + 4],
                                AF.Identity, scale=-1.0)

        def vpad(v):
            v.tensor_copy(jkA[:], jkB[:])
            return v.tensor_copy(jkB[:], jkA[:])

        with nc.Block("b0") as b0:
            @b0.sync
            def _(sync):
                for c in range(NHC):
                    sync.dma_start(
                        out=cw[:, 3 * c:3 * c + 3],
                        in_=convw_h[c * 128:(c + 1) * 128, :],
                    ).then_inc(S["d_cw"], 16)
                    sync.dma_start(
                        out=cbv_[:, c:c + 1],
                        in_=convb_h[c * 128:(c + 1) * 128, :],
                    ).then_inc(S["d_cw"], 16)
                sync.wait_ge(S["d_cw"], 32 * NHC)

            @b0.gpsimd
            def _(g):
                for idt in (ident, identf):
                    g.memset(idt[:], 0.0)
                    g.affine_select(out=idt[:], in_=idt[:],
                                    compare_op=ALU.not_equal, fill=1.0, base=0,
                                    pattern=[[-1, 128]], channel_multiplier=1)
                g.memset(onesrow[:], 1.0)
                g.memset(onescol[:], 1.0)
                g.memset(c_m20[:], EXP_SHIFT)
                g.memset(c_eps[:], 1e-5)
                g.memset(c_negeps[:], -1e-5)
                g.memset(c_mg[:], 12582912.0)
                g.memset(c_nmg[:], -12582912.0)
                g.memset(st1[:], 0.0)
                g.memset(stS[:], 0.0)
                g.memset(sqb[:], 0.0)
                g.memset(jkA[:], 0.0)
                g.memset(jkB[:], 0.0)

        with ExitStack() as ctxA:
            ea = ctxA.enter_context
            wqkv = ea(nc.sbuf_tensor("wqkv_sb", [128, NDC * QKVN], BF16))
            xnT = ea(nc.sbuf_tensor("xnT", [128, NDC * TGS], BF16))
            xnbf = ea(nc.sbuf_tensor("xnbf", [128, 2 * D], BF16))
            kf = ea(nc.sbuf_tensor("kf", [128, NDC * T], BF16))
            vt = ea(nc.sbuf_tensor("vt", [128, NW * D], BF16))
            qsb = ea(nc.sbuf_tensor("qsb", [128, 8 * TGS], BF16))
            qwin = ea(nc.sbuf_tensor("qwin", [128, 2 * NDC * 128], BF16))
            xln = ea(nc.sbuf_tensor("xln", [128, 3 * D], F32))
            xres = ea(nc.sbuf_tensor("xres", [128, 2 * D], F32))
            attno = ea(nc.sbuf_tensor("attno", [128, 2 * D], F32))
            wscr = ea(nc.sbuf_tensor("wscr", [128, 2 * D], F32))
            wscrB = ea(nc.sbuf_tensor("wscrB", [128, 2 * D], F32))
            x2b = ea(nc.sbuf_tensor("x2b", [128, 3 * D], F32))
            xqf = ea(nc.sbuf_tensor("xqf", [128, 1024], F32))
            xqbf = ea(nc.sbuf_tensor("xqbf", [128, 2 * D], BF16))
            xqtw = ea(nc.sbuf_tensor("xqtw", [128, 2 * NDC * 128], BF16,
                                     is_transpose_dest=True))
            probs = ea(nc.sbuf_tensor("probs", [128, 2 * 768], BF16))
            probsT = ea(nc.sbuf_tensor("probsT", [128, 2 * 768], BF16))

            with nc.Block("b0b") as b0b:
                @b0b.gpsimd
                def _(g):
                    for d in range(NDC):
                        g.dma_start(
                            out=wqkv[:, d * QKVN:(d + 1) * QKVN],
                            in_=wqkv_h[d * 128:(d + 1) * 128, :],
                        ).then_inc(S["d_wq"], 16)

            with ExitStack() as ctx1:
                e1 = ctx1.enter_context
                ptr = e1(nc.psum_tensor("ptr", [128, 1024], F32))
                psA = [e1(nc.psum_tensor(f"psA{i}", [128, 512], F32))
                       for i in (0, 1)]
                pv = [e1(nc.psum_tensor(f"pv{i}", [128, 1024], F32))
                      for i in (0, 1)]

                with nc.Block("b1") as b1:
                    @b1.sync
                    def _(sync):
                        for c in range(NW + 1):
                            if c < NW:
                                if c >= 3:
                                    sync.wait_ge(S["s_xn"], c - 2)
                                sync.dma_start(
                                    out=xln[:, (c % 3) * D:][:, 0:D],
                                    in_=x_h[c * 128:(c + 1) * 128, :],
                                ).then_inc(S["d_x"], 16)
                            if c % 4 == 0 and c >= 4:
                                tg = c // 4
                                for F in range(8):
                                    sync.wait_ge(S["v_qk"], (tg - 1) * 16 + F + 1)
                                    sync.dma_start(
                                        out=qh_h[F, :, (tg - 1) * TGS:tg * TGS],
                                        in_=qsb[:, F * TGS:][:, 0:TGS],
                                    ).then_inc(S["d_qh"], 16)
                        sync.wait_ge(S["d_qh"], 16 * NTG * 8)

                    @b1.vector
                    def _(v):
                        for c in range(NW + 1):
                            if c < NW:
                                b = 4 * (c % 2)
                                v.wait_ge(S["d_x"], 16 * (c + 1))
                                xc = xln[:, (c % 3) * D:][:, 0:D]
                                ln_stats_v(v, b, xc).then_inc(S["v_st"], 1)
                            if c < 1:
                                continue
                            cp = c - 1
                            tg, l = cp // 4, cp % 4
                            v.wait_ge(S["t_tr"], cp + 1)
                            if l == 0 and tg > 0:
                                v.wait_ge(S["t_qk"], 16 * tg)
                                v.wait_ge(S["t_v"], 4 * tg)
                            v.tensor_copy(
                                xnT[:].rearrange("p (d t) -> p d t", d=NDC)[:, :, l * 128:(l + 1) * 128],
                                ptr[:].rearrange("p (d t) -> p d t", d=NDC),
                            ).then_inc(S["v_xnT"], 1)
                            if l == 3:
                                for F in range(16):
                                    gidx = tg * 16 + F
                                    v.wait_ge(S["t_qk"], gidx + 1)
                                    if F < 8:
                                        if tg >= 1:
                                            v.wait_ge(S["d_qh"],
                                                      16 * (8 * (tg - 1) + F + 1))
                                        v.tensor_scalar(
                                            qsb[:, F * TGS:][:, 0:TGS],
                                            psA[F % 2][:], 0.125, None, ALU.mult,
                                        ).then_inc(S["v_qk"], 1)
                                    else:
                                        v.tensor_copy(
                                            kf[:, (F - 8) * T + tg * TGS:][:, 0:TGS],
                                            psA[F % 2][:],
                                        ).then_inc(S["v_qk"], 1)
                                for sl in range(4):
                                    gidx = tg * 4 + sl
                                    v.wait_ge(S["t_v"], gidx + 1)
                                    v.tensor_copy(
                                        vt[:, gidx * D:(gidx + 1) * D],
                                        pv[sl % 2][:],
                                    ).then_inc(S["v_v"], 1)

                    @b1.scalar
                    def _(s):
                        for c in range(NW + 1):
                            if c < NW:
                                b = 4 * (c % 2)
                                s.wait_ge(S["v_st"], c + 1)
                                ln_rstd_s(s, b)
                            if c >= 1:
                                cp = c - 1
                                b = 4 * (cp % 2)
                                if cp >= 2:
                                    s.wait_ge(S["t_tr"], cp - 1)
                                s.activation(xnbf[:, (cp % 2) * D:][:, 0:D],
                                             xln[:, (cp % 3) * D:][:, 0:D],
                                             AF.Identity, scale=stS[:, b + 2:b + 3],
                                             bias=stS[:, b + 3:b + 4]).then_inc(S["s_xn"], 1)

                    @b1.tensor
                    def _(t):
                        for tg in range(NTG):
                            for l in range(4):
                                c = tg * 4 + l
                                t.wait_ge(S["s_xn"], c + 1)
                                if c > 0:
                                    t.wait_ge(S["v_xnT"], c)
                                ins = None
                                for d in range(NDC):
                                    ins = t.matmul(
                                        ptr[:, d * 128:(d + 1) * 128],
                                        xnbf[:, (c % 2) * D + d * 128:][:, 0:128],
                                        ident[:], start=True, stop=True)
                                ins.then_inc(S["t_tr"], 1)
                            t.wait_ge(S["v_xnT"], 4 * (tg + 1))
                            if tg == 0:
                                t.wait_ge(S["d_wq"], 16 * NDC)
                            for F in range(16):
                                gidx = tg * 16 + F
                                if gidx >= 2:
                                    t.wait_ge(S["v_qk"], gidx - 1)
                                ins = None
                                for d in range(NDC):
                                    ins = t.matmul(
                                        psA[F % 2][:],
                                        wqkv[:, d * QKVN + F * 128:][:, 0:128],
                                        xnT[:, d * TGS:(d + 1) * TGS],
                                        start=(d == 0), stop=(d == NDC - 1))
                                ins.then_inc(S["t_qk"], 1)
                            for sl in range(4):
                                gidx = tg * 4 + sl
                                if gidx >= 2:
                                    t.wait_ge(S["v_v"], gidx - 1)
                                ins = None
                                for d in range(NDC):
                                    for hh in range(2):
                                        ins = t.matmul(
                                            pv[sl % 2][:, hh * 512:][:, 0:512],
                                            xnT[:, d * TGS + sl * 128:][:, 0:128],
                                            wqkv[:, d * QKVN + 2 * D + hh * 512:][:, 0:512],
                                            start=(d == 0), stop=(d == NDC - 1))
                                ins.then_inc(S["t_v"], 1)

            if debug:
                with nc.Block("dbgB1") as dbgb1:
                    @dbgb1.sync
                    def _(sync):
                        sync.dma_start(out=dxnT_h, in_=xnT[:]).then_inc(S["d_dbg"], 16)
                        sync.dma_start(out=dst1_h, in_=st1[:]).then_inc(S["d_dbg"], 16)
                        sync.dma_start(out=dxnbf_h, in_=xnbf[:]).then_inc(S["d_dbg"], 16)
                        sync.wait_ge(S["d_dbg"], 48)

            if maxphase < 2:
                return nc
            wins = []
            for n in range(NW):
                cb0 = max(n - 1, 0)
                cb1 = min(n + 1, NW - 1)
                wins.append((cb0, cb1 - cb0 + 1))

            with ExitStack() as ctx2:
                e2 = ctx2.enter_context
                scp = [e2(nc.psum_tensor(f"scp{i}", [128, 1024], F32))
                       for i in (0, 1)]
                ptp = e2(nc.psum_tensor("ptp", [128, 1024], F32))
                avp = e2(nc.psum_tensor("avp", [128, 1024], F32))

                with nc.Block("b2") as b2:
                    @b2.sync
                    def _(sync):
                        for w in range(NW + 2):
                            if w < NW:
                                n = w
                                if n >= 2:
                                    sync.wait_ge(S["t_sc"], 8 * (n - 1))
                                for d in range(NDC):
                                    sync.dma_start(
                                        out=qwin[:, (n % 2) * NDC * 128 + d * 128:][:, 0:128],
                                        in_=qh_h[d, :, n * 128:(n + 1) * 128],
                                    ).then_inc(S["d_qw"], 16)
                                if n >= 2:
                                    sync.wait_ge(S["v_e2"], n - 1)
                                sync.dma_start(
                                    out=xres[:, (n % 2) * D:][:, 0:D],
                                    in_=x_h[n * 128:(n + 1) * 128, :],
                                ).then_inc(S["d_xr"], 16)
                            if 1 <= w <= NW:
                                n = w - 1
                                sync.wait_ge(S["v_e2"], w)
                                sync.dma_start(
                                    out=x2_h[n * 128:(n + 1) * 128, :],
                                    in_=x2b[:, (n % 3) * D:][:, 0:D],
                                ).then_inc(S["d_x2"], 16)
                            if 2 <= w <= NW + 1:
                                n = w - 2
                                sync.wait_ge(S["g_e3"], n + 1)
                                if n >= 2:
                                    sync.wait_ge(S["d_xq"], 16 * NDC * (n - 1))
                                for d in range(NDC):
                                    sync.dma_start_transpose(
                                        xqtw[:, (n % 2) * NDC * 128 + d * 128:][:, 0:128],
                                        xqbf[:, (n % 2) * D + d * 128:][:, 0:128],
                                    ).then_inc(S["d_xqt"], 16)
                                sync.wait_ge(S["d_xqt"], 16 * NDC * (n + 1))
                                for d in range(NDC):
                                    sync.dma_start(
                                        out=xqh_h[d, :, n * 128:(n + 1) * 128],
                                        in_=xqtw[:, (n % 2) * NDC * 128 + d * 128:][:, 0:128],
                                    ).then_inc(S["d_xq"], 16)
                        sync.wait_ge(S["d_x2"], 16 * NW)
                        sync.wait_ge(S["d_xq"], 16 * NDC * NW)

                    @b2.tensor
                    def _(t):
                        NP = 8 * NW
                        for p in range(NP + 1):
                            if p >= 1:
                                q = p - 1
                                n, j = divmod(q, 8)
                                cb0, nch = wins[n]
                                t.wait_ge(S["s_exp"], q + 1)
                                if q >= 1:
                                    t.wait_ge(S["v_tr"], q)
                                ins = None
                                for hh in range(2):
                                    for c in range(nch):
                                        ins = t.matmul(
                                            ptp[:, hh * 512 + c * 128:][:, 0:128],
                                            probs[:, (q % 2) * 768 + hh * 384 + c * 128:][:, 0:128],
                                            ident[:], start=True, stop=True)
                                ins.then_inc(S["t_tr2"], 1)
                                t.wait_ge(S["v_tr"], q + 1)
                                if j == 0:
                                    t.wait_ge(S["v_ep"], n)
                                ins = None
                                for hh in range(2):
                                    hgl = 2 * j + hh
                                    for c in range(nch):
                                        ins = t.matmul(
                                            avp[:, hgl * 64:][:, 0:64],
                                            probsT[:, (q % 2) * 768 + hh * 384 + c * 128:][:, 0:128],
                                            vt[:, (cb0 + c) * D + hgl * 64:][:, 0:64],
                                            start=(c == 0), stop=(c == nch - 1))
                                ins.then_inc(S["t_av"], 1)
                            if p < NP:
                                n, j = divmod(p, 8)
                                cb0, nch = wins[n]
                                ctxw = nch * 128
                                if j == 0:
                                    t.wait_ge(S["d_qw"], 16 * NDC * (n + 1))
                                if p >= 2:
                                    t.wait_ge(S["s_exp"], p - 1)
                                ins = None
                                for hh in range(2):
                                    h0 = 64 * hh
                                    ins = t.matmul(
                                        scp[p % 2][:, hh * 512:][:, 0:ctxw],
                                        qwin[h0:h0 + 64,
                                             (n % 2) * NDC * 128 + j * 128:][:, 0:128],
                                        kf[h0:h0 + 64, j * T + cb0 * 128:][:, 0:ctxw],
                                        start=True, stop=True,
                                        tile_position=(h0, 0))
                                ins.then_inc(S["t_sc"], 1)

                    @b2.gpsimd
                    def _(g):
                        MAGIC = 12582912.0
                        for w in range(2, NW + 2):
                            n = w - 2
                            qb = 24 + 4 * (n % 2)
                            g.wait_ge(S["v_e3"], n + 1)
                            g.tensor_scalar(xqbf[:, (n % 2) * D:][:, 0:D],
                                            xqf[:], -MAGIC, None, ALU.add)
                            g.tensor_scalar(xqbf[:, (n % 2) * D:][:, 0:D],
                                            xqbf[:, (n % 2) * D:][:, 0:D],
                                            stS[:, qb + 1:qb + 2], None,
                                            ALU.mult).then_inc(S["g_e3"], 1)

                    @b2.scalar
                    def _(s):
                        for w in range(NW + 2):
                            if w < NW:
                                n = w
                                cb0, nch = wins[n]
                                ctxw = nch * 128
                                for j in range(8):
                                    p = 8 * n + j
                                    s.wait_ge(S["t_sc"], p + 1)
                                    if p >= 2:
                                        s.wait_ge(S["t_tr2"], p - 1)
                                    if j == 0 and n >= 2:
                                        s.wait_ge(S["v_ep"], n - 1)
                                    ins = None
                                    for hh in range(2):
                                        ins = s.activation(
                                            probs[:, (p % 2) * 768 + hh * 384:][:, 0:ctxw],
                                            scp[p % 2][:, hh * 512:][:, 0:ctxw],
                                            AF.Exp, bias=c_m20[:],
                                            accum_out=sums[:, 16 * (n % 2) + 2 * j + hh:][:, 0:1])
                                    ins.then_inc(S["s_exp"], 1)
                            if 1 <= w <= NW:
                                n1 = w - 1
                                b1c = 8 + 4 * (n1 % 2)
                                s.wait_ge(S["v_ep"], w)
                                ln_rstd_s(s, b1c)
                            if 2 <= w <= NW + 1:
                                n2 = w - 2
                                b2c = 16 + 4 * (n2 % 2)
                                s.wait_ge(S["v_e2"], n2 + 1)
                                if n2 >= 2:
                                    s.wait_ge(S["v_e3"], n2 - 1)
                                ln_rstd_s(s, b2c)
                            if 1 <= w <= NW:
                                n1 = w - 1
                                b1c = 8 + 4 * (n1 % 2)
                                s.activation(wscr[:, (n1 % 2) * D:][:, 0:D],
                                             attno[:, (n1 % 2) * D:][:, 0:D],
                                             AF.Identity, scale=stS[:, b1c + 2:b1c + 3],
                                             bias=stS[:, b1c + 3:b1c + 4]).then_inc(S["s_e1"], 1)
                            if 2 <= w <= NW + 1:
                                n2 = w - 2
                                b2c = 16 + 4 * (n2 % 2)
                                s.activation(wscrB[:, (n2 % 2) * D:][:, 0:D],
                                             x2b[:, (n2 % 3) * D:][:, 0:D],
                                             AF.Identity, scale=stS[:, b2c + 2:b2c + 3],
                                             bias=stS[:, b2c + 3:b2c + 4]).then_inc(S["s_e2"], 1)
                                qb = 24 + 4 * (n2 % 2)
                                qa = 24 + 2 * (n2 % 2)
                                s.wait_ge(S["v_e3a"], n2 + 1)
                                s.activation(stS[:, qb:qb + 1], st1[:, qa:qa + 1],
                                             AF.Relu, bias=c_negeps[:])
                                s.activation(stS[:, qb:qb + 1], stS[:, qb:qb + 1],
                                             AF.Identity, bias=c_eps[:])
                                s.activation(stS[:, qb + 1:qb + 2], stS[:, qb:qb + 1],
                                             AF.Identity,
                                             scale=1.0 / 127.0).then_inc(S["s_e3a"], 1)
                                s.wait_ge(S["v_e3b"], n2 + 1)
                                s.activation(stS[:, qb + 2:qb + 3], st1[:, qa + 1:qa + 2],
                                             AF.Identity,
                                             scale=127.0).then_inc(S["s_e3b"], 1)

                    @b2.vector
                    def _(v):
                        for w in range(NW + 2):
                            if w < NW:
                                n = w
                                cb0, nch = wins[n]
                                ctxw = nch * 128
                                for j in range(8):
                                    p = 8 * n + j
                                    v.wait_ge(S["t_tr2"], p + 1)
                                    if p >= 2:
                                        v.wait_ge(S["t_av"], p - 1)
                                    v.tensor_copy(
                                        probsT[:, (p % 2) * 768:][:, 0:768]
                                            .rearrange("p (a b) -> p a b", a=2)[:, :, 0:ctxw],
                                        ptp[:].rearrange("p (a b) -> p a b", a=2)[:, :, 0:ctxw],
                                    ).then_inc(S["v_tr"], 1)
                                b = 8 + 4 * (n % 2)
                                v.wait_ge(S["t_av"], 8 * (n + 1))
                                v.wait_ge(S["s_exp"], 8 * (n + 1))
                                if n >= 2:
                                    v.wait_ge(S["s_e1"], n - 1)
                                v.reciprocal(sinv[:], sums[:, 16 * (n % 2):][:, 0:16])
                                vpad(v)
                                ao = attno[:, (n % 2) * D:][:, 0:D]
                                for hgl in range(H):
                                    v.tensor_scalar(ao[:, hgl * 64:][:, 0:64],
                                                    avp[:, hgl * 64:][:, 0:64],
                                                    sinv[:, hgl:hgl + 1], None,
                                                    ALU.mult)
                                ln_stats_v(v, b, ao).then_inc(S["v_ep"], 1)
                            if 1 <= w <= NW:
                                n = w - 1
                                b = 16 + 4 * (n % 2)
                                v.wait_ge(S["s_e1"], n + 1)
                                v.wait_ge(S["d_xr"], 16 * (n + 1))
                                if n >= 2:
                                    v.wait_ge(S["d_x2"], 16 * (n - 1))
                                xb2 = x2b[:, (n % 3) * D:][:, 0:D]
                                v.tensor_tensor(xb2, wscr[:, (n % 2) * D:][:, 0:D],
                                                xres[:, (n % 2) * D:][:, 0:D],
                                                ALU.add)
                                ln_stats_v(v, b, xb2).then_inc(S["v_e2"], 1)
                            if 2 <= w <= NW + 1:
                                n = w - 2
                                v.wait_ge(S["s_e2"], n + 1)
                                qb = 24 + 4 * (n % 2)
                                qa = 24 + 2 * (n % 2)
                                xn3 = wscrB[:, (n % 2) * D:][:, 0:D]
                                v.reduce_max(st1[:, qa:qa + 1], xn3, axis=AX.X,
                                             apply_absolute_value=True).then_inc(S["v_e3a"], 1)
                                v.wait_ge(S["s_e3a"], n + 1)
                                v.reciprocal(st1[:, qa + 1:qa + 2],
                                             stS[:, qb:qb + 1]).then_inc(S["v_e3b"], 1)
                                v.wait_ge(S["s_e3b"], n + 1)
                                if n >= 2:
                                    v.wait_ge(S["d_xqt"], 16 * NDC * (n - 1))
                                MAGIC = 12582912.0
                                ins = None
                                for hf in range(2):
                                    ins = v.tensor_scalar(
                                        xqf[:, hf * 512:][:, 0:512],
                                        xn3[:, hf * 512:][:, 0:512],
                                        stS[:, qb + 2:qb + 3], MAGIC, ALU.mult, ALU.add)
                                ins.then_inc(S["v_e3"], 1)

            if debug:
                with nc.Block("dbgA") as dbga:
                    @dbga.sync
                    def _(sync):
                        sync.dma_start(out=dk_h, in_=kf[:]).then_inc(S["d_dbg"], 16)
                        sync.dma_start(out=dv_h, in_=vt[:]).then_inc(S["d_dbg"], 16)
                        sync.wait_ge(S["d_dbg"], 48 + 32)

        if maxphase < 3:
            return nc
        with ExitStack() as ctxB:
            eb = ctxB.enter_context
            tq1T = eb(nc.sbuf_tensor("tq1T", [128, NDC * DH], BF16))
            tq2T = eb(nc.sbuf_tensor("tq2T", [128, NHC * D], BF16))

            chunks = []
            for c in range(NHC):
                chunks.append(("w1", c))
            for c in range(8):
                for g2 in range(4):
                    chunks.append(("w2", (c, g2)))
            NCH = len(chunks)

            with ExitStack() as ctx3:
                e3 = ctx3.enter_context
                wst = e3(nc.sbuf_tensor("wst", [128, 2 * 1024], F32))
                wst2 = e3(nc.sbuf_tensor("wst2", [128, 2 * 1024], F32))
                wtmp = e3(nc.sbuf_tensor("wtmp", [128, 1024], F32))
                tqn = e3(nc.sbuf_tensor("tqn", [128, 2 * 1024], BF16))
                absc = e3(nc.sbuf_tensor("absc", [128, 64], F32))
                swsc = e3(nc.sbuf_tensor("swsc", [1, 8], F32))
                psw = e3(nc.psum_tensor("psw", [128, 8], F32))
                ppa = e3(nc.psum_tensor("ppa", [1, 64], F32))
                pst = [e3(nc.psum_tensor(f"pst{i}", [128, 1024], F32))
                       for i in (0, 1)]

                def chunk_src(ch):
                    kind, meta = chunks[ch]
                    if kind == "w1":
                        return w1_h[meta * 128:(meta + 1) * 128, :]
                    c, g2 = meta
                    return w2_h[c * 128:(c + 1) * 128, g2 * 1024:(g2 + 1) * 1024]

                CLIP = 1.4999998807907104
                MAGIC = 12582912.0

                with nc.Block("b3") as b3:
                    @b3.sync
                    def _(sync):
                        for ch in range(NCH):
                            if ch >= 2:
                                sync.wait_ge(S["v_p1"], ch - 1)
                            sync.dma_start(
                                out=wst[:, (ch % 2) * 1024:][:, 0:1024],
                                in_=chunk_src(ch),
                            ).then_inc(S["d_w1"], 16)
                        for ch in range(NCH):
                            if ch >= 2:
                                sync.wait_ge(S["v_q2"], ch - 1)
                            sync.dma_start(
                                out=wst2[:, (ch % 2) * 1024:][:, 0:1024],
                                in_=chunk_src(ch),
                            ).then_inc(S["d_w2"], 16)

                    @b3.tensor
                    def _(t):
                        t.wait_ge(S["v_p1"], NHC)
                        t.matmul(ppa[0:1, 0:NHC], onescol[:], absc[:, 0:NHC],
                                 start=True, stop=True)
                        t.wait_ge(S["v_p1"], NCH)
                        t.matmul(ppa[0:1, NHC:NCH], onescol[:],
                                 absc[:, NHC:NCH],
                                 start=True, stop=True).then_inc(S["t_p1"], 1)
                        t.wait_ge(S["v_sw"], 1)
                        t.matmul(psw[:, 4:8], onesrow[:], swsc[0:1, 0:4],
                                 start=True, stop=True).then_inc(S["t_sw"], 1)
                        for ch in range(NCH):
                            t.wait_ge(S["v_q2"], ch + 1)
                            if ch >= 2:
                                t.wait_ge(S["v_tqe"], ch - 1)
                            ins = None
                            for jj in range(8):
                                ins = t.matmul(
                                    pst[ch % 2][:, jj * 128:][:, 0:128],
                                    tqn[:, (ch % 2) * 1024 + jj * 128:][:, 0:128],
                                    ident[:], start=True, stop=True)
                            ins.then_inc(S["t_tq"], 1)

                    @b3.vector
                    def _(v):
                        for ch in range(NCH):
                            v.wait_ge(S["d_w1"], 16 * (ch + 1))
                            v.reduce_sum(absc[:, ch:ch + 1],
                                         wst[:, (ch % 2) * 1024:][:, 0:1024],
                                         axis=AX.X,
                                         apply_absolute_value=True,
                                         ).then_inc(S["v_p1"], 1)
                        v.wait_ge(S["t_p1"], 1)
                        v.reduce_sum(swsc[0:1, 4:5], ppa[0:1, 0:NHC], axis=AX.X)
                        v.reduce_sum(swsc[0:1, 5:6], ppa[0:1, NHC:NCH], axis=AX.X)
                        vpad(v)
                        vpad(v)
                        v.tensor_scalar(swsc[0:1, 4:6], swsc[0:1, 4:6],
                                        1.0 / (DH * D), 1e-5, ALU.mult, ALU.max)
                        vpad(v)
                        v.reciprocal(swsc[0:1, 6:8], swsc[0:1, 4:6])
                        vpad(v)
                        vpad(v)
                        v.tensor_copy(swsc[0:1, 0:1], swsc[0:1, 6:7])
                        v.tensor_copy(swsc[0:1, 1:2], swsc[0:1, 4:5])
                        v.tensor_copy(swsc[0:1, 2:3], swsc[0:1, 7:8])
                        v.tensor_copy(swsc[0:1, 3:4], swsc[0:1, 5:6])
                        vpad(v)
                        v.engine_nop().then_inc(S["v_sw"], 1)
                        v.wait_ge(S["t_sw"], 1)
                        v.tensor_copy(swv[:], psw[:, 4:8])
                        vpad(v)
                        v.tensor_scalar(cwsc[:], cw[:], swv[:, 1:2], None,
                                        ALU.mult).then_inc(S["v_sw"], 1)
                        for ch in range(NCH + 1):
                            if ch < NCH:
                                v.wait_ge(S["d_w2"], 16 * (ch + 1))
                                wsrc = wst2[:, (ch % 2) * 1024:][:, 0:1024]
                                scol = 0 if chunks[ch][0] == "w1" else 2
                                v.tensor_scalar(wtmp[:], wsrc,
                                                swv[:, scol:scol + 1],
                                                CLIP, ALU.mult, ALU.min)
                                v.tensor_scalar(wtmp[:], wtmp[:], -CLIP, MAGIC,
                                                ALU.max, ALU.add)
                                v.tensor_scalar(tqn[:, (ch % 2) * 1024:][:, 0:1024],
                                                wtmp[:], MAGIC, None,
                                                ALU.subtract).then_inc(S["v_q2"], 1)
                            if ch >= 1:
                                cp = ch - 1
                                v.wait_ge(S["t_tq"], cp + 1)
                                kind, meta = chunks[cp]
                                if kind == "w1":
                                    cc = meta
                                    dst = tq1T[:].rearrange(
                                        "p (d t) -> p d t", d=NDC
                                    )[:, :, cc * 128:(cc + 1) * 128]
                                else:
                                    cc, g2 = meta
                                    dst = tq2T[:, g2 * 8 * D:][:, 0:8 * D].rearrange(
                                        "p (h t) -> p h t", h=8
                                    )[:, :, cc * 128:(cc + 1) * 128]
                                v.tensor_copy(
                                    dst,
                                    pst[cp % 2][:].rearrange("p (j t) -> p j t", j=8),
                                ).then_inc(S["v_tqe"], 1)

            if maxphase < 4:
                if debug:
                    with nc.Block("dbgB3") as dbgb3:
                        @dbgb3.sync
                        def _(sync):
                            sync.dma_start(out=dt1_h, in_=tq1T[:]).then_inc(S["d_dbg"], 16)
                            sync.dma_start(out=dt2_h, in_=tq2T[:]).then_inc(S["d_dbg"], 16)
                            sync.dma_start(out=dsw_h, in_=swv[:]).then_inc(S["d_dbg"], 16)
                            sync.wait_ge(S["d_dbg"], 48 + 80)
                return nc
            hx = eb(nc.sbuf_tensor("hx", [128, NHC * HEXW], BF16))
            cbuf = eb(nc.sbuf_tensor("cbuf", [128, 2 * 516], BF16))
            ftmp = eb(nc.sbuf_tensor("ftmp", [128, 2 * 516], F32))
            accb = eb(nc.sbuf_tensor("accb", [128, 513], BF16))
            accm = eb(nc.sbuf_tensor("accm", [128, 513], BF16))
            s2b = eb(nc.sbuf_tensor("s2b", [128, 513], F32))
            mxb = eb(nc.sbuf_tensor("mxb", [128, 32], F32))
            rowsb = eb(nc.sbuf_tensor("rowsb", [1, 640], F32))
            xqt = eb(nc.sbuf_tensor("xqt", [128, 2 * NDC * 512], BF16))
            x2i = eb(nc.sbuf_tensor("x2i", [128, 2 * D], F32))
            outy = eb(nc.sbuf_tensor("outy", [128, D], F32))
            outst = eb(nc.sbuf_tensor("outst", [128, 2 * D], F32))
            hcar = eb(nc.sbuf_tensor("hcar", [128, 2 * NHC], BF16))
            ps1 = [eb(nc.psum_tensor(f"ps1{i}", [128, 512], F32)) for i in range(3)]
            ps2 = eb(nc.psum_tensor("ps2", [128, 1024], F32))
            ps3 = eb(nc.psum_tensor("ps3", [128, 1024], F32))

            tstart = [0]
            for td in TILES:
                tstart.append(tstart[-1] + len(td["slices"]))

            with nc.Block("b4") as b4:
                @b4.sync
                def _(sync):
                    def xqt_load(ti):
                        if ti >= 2:
                            sync.wait_ge(S["t_f1"], NHC * (ti - 1))
                        for d in range(NDC):
                            sync.dma_start(
                                out=xqt[:, (ti % 2) * NDC * 512 + d * 512:][:, 0:512],
                                in_=xqh_h[d, :, ti * FT:(ti + 1) * FT],
                            ).then_inc(S["d_xt"], 16)
                    xqt_load(0)
                    xqt_load(1)
                    for gi, (ti, si, c0, wd) in enumerate(GSL):
                        if si == 0 and ti >= 2:
                            xqt_load(ti)
                        r0 = TILES[ti]["tok0"] + c0
                        if gi >= 2:
                            sync.wait_ge(S["v_l4"], gi - 1)
                        sync.dma_start(
                            out=x2i[:, (gi % 2) * D:][:, 0:D][0:wd, :],
                            in_=x2_h[r0:r0 + wd, :],
                        ).then_inc(S["d_x2i"], 16)
                        if gi >= 2:
                            pi, psi, pc0, pwd = GSL[gi - 2]
                            pr0 = TILES[pi]["tok0"] + pc0
                            sync.wait_ge(S["v_l4"], gi - 1)
                            sync.dma_start(
                                out=out_h[pr0:pr0 + pwd, :],
                                in_=outst[:, ((gi - 2) % 2) * D:][:, 0:D][0:pwd, :],
                            ).then_inc(S["d_out"], 16)
                    for gi in (NSL - 2, NSL - 1):
                        ti, si, c0, wd = GSL[gi]
                        r0 = TILES[ti]["tok0"] + c0
                        sync.wait_ge(S["v_l4"], gi + 1)
                        sync.dma_start(
                            out=out_h[r0:r0 + wd, :],
                            in_=outst[:, (gi % 2) * D:][:, 0:D][0:wd, :],
                        ).then_inc(S["d_out"], 16)
                    sync.wait_ge(S["d_out"], 16 * NSL)

                @b4.gpsimd
                def _(g):
                    g.memset(hcar[:], 0.0)
                    g.memset(hx[:].rearrange("p (c w) -> p c w", c=NHC)[:, :, 514:515],
                             0.0)
                    g.memset(mxb[:], 1.0)
                    g.memset(accb[:], 0.0)
                    g.engine_nop().then_inc(S["g_i4"], 1)

                @b4.tensor
                def _(t):
                    for ti, td in enumerate(TILES):
                        nsg = len(td["slices"])
                        for c in range(NHC):
                            k = ti * NHC + c
                            if c == 0:
                                t.wait_ge(S["d_xt"], 16 * NDC * (ti + 1))
                                if ti == 0:
                                    t.wait_ge(S["g_i4"], 1)
                            if k >= 3:
                                t.wait_ge(S["v_ev"], k - 2)
                            ins = None
                            for d in range(NDC):
                                ins = t.matmul(
                                    ps1[c % 3][:],
                                    tq1T[:, d * DH + c * 128:][:, 0:128],
                                    xqt[:, (ti % 2) * NDC * 512 + d * 512:][:, 0:512],
                                    start=(d == 0), stop=(d == NDC - 1))
                            ins.then_inc(S["t_f1"], 1)
                        t.wait_ge(S["g_am"], NHC * (ti + 1))
                        ins = None
                        for gl, (c0, wd) in enumerate(td["slices"]):
                            ins = t.matmul(ps3[0:wd, 128 * gl:][:, 0:128],
                                           accb[:, c0:c0 + wd], ident[:],
                                           start=True, stop=True)
                        ins.then_inc(S["t_s2"], 1)
                        t.wait_ge(S["s_s2b"], ti + 1)
                        ins = None
                        for gl, (c0, wd) in enumerate(td["slices"]):
                            ins = t.matmul(ps3[0:1, 128 * gl:][:, 0:128],
                                           mxb[0:wd, 24 + gl:25 + gl],
                                           identf[0:wd, :],
                                           start=True, stop=True)
                        ins.then_inc(S["t_s2"], 1)
                        t.wait_ge(S["v_s2"], 3 * ti + 2)
                        ins = None
                        for gl, (c0, wd) in enumerate(td["slices"]):
                            ins = t.matmul(ps3[:, 128 * gl:][:, 0:wd],
                                           onesrow[:],
                                           rowsb[0:1, 128 * gl:][:, 0:wd],
                                           start=True, stop=True)
                        ins.then_inc(S["t_s2"], 1)
                        t.wait_ge(S["s_rd"], NHC * (ti + 1))
                        for si, (c0, wd) in enumerate(td["slices"]):
                            gi = tstart[ti] + si
                            if gi >= 1:
                                t.wait_ge(S["v_f2"], gi)
                            ins = None
                            for c in range(NHC):
                                for half in range(2):
                                    ins = t.matmul(
                                        ps2[0:wd, half * 512:][:, 0:512],
                                        hx[:, c * HEXW + 1 + c0:][:, 0:wd],
                                        tq2T[:, c * D + half * 512:][:, 0:512],
                                        start=(c == 0), stop=(c == NHC - 1))
                            ins.then_inc(S["t_f2"], 1)

                @b4.vector
                def _(v):
                    for ti, td in enumerate(TILES):
                        ccmin, ccmax = td["ccmin"], td["ccmax"]
                        ccw = ccmax - ccmin
                        nsg = len(td["slices"])
                        for c in range(NHC):
                            k = ti * NHC + c
                            v.wait_ge(S["t_f1"], k + 1)
                            if k >= 2:
                                v.wait_ge(S["s_si"], k - 1)
                            if c == 0 and ti >= 1:
                                v.wait_ge(S["t_f2"], tstart[ti])
                                v.wait_ge(S["s_rd"], NHC * ti)
                            v.tensor_copy(hx[:, c * HEXW:][:, 0:2],
                                          hcar[:, 2 * c:][:, 0:2])
                            v.tensor_copy(hcar[:, 2 * c:][:, 0:2],
                                          ps1[c % 3][:, 510:512])
                            v.tensor_copy(hx[:, c * HEXW + 2:][:, 0:512],
                                          ps1[c % 3][:]).then_inc(S["v_ev"], 1)
                            cbs = cbuf[:, (k % 2) * 516:][:, 0:ccw]
                            v.tensor_scalar(cbs,
                                            hx[:, c * HEXW + ccmin + 1:][:, 0:ccw],
                                            cwsc[:, 3 * c + 1:3 * c + 2],
                                            cbv_[:, c:c + 1], ALU.mult, ALU.add)
                            v.scalar_tensor_tensor(cbs,
                                                   hx[:, c * HEXW + ccmin:][:, 0:ccw],
                                                   cwsc[:, 3 * c:3 * c + 1], cbs,
                                                   ALU.mult, ALU.add)
                            v.scalar_tensor_tensor(cbs,
                                                   hx[:, c * HEXW + ccmin + 2:][:, 0:ccw],
                                                   cwsc[:, 3 * c + 2:3 * c + 3], cbs,
                                                   ALU.mult, ALU.add).then_inc(S["v_cv"], 1)
                            if c >= 1:
                                cm = c - 1
                                v.wait_ge(S["s_si"], ti * NHC + cm + 1)
                                hsm = hx[:, cm * HEXW + ccmin + 1:][:, 0:ccw]
                                acs = accb[:, ccmin:][:, 0:ccw]
                                acm = accm[:, ccmin:][:, 0:ccw]
                                if cm == 0:
                                    if ti >= 1:
                                        v.wait_ge(S["t_s2"], 3 * ti - 2)
                                    v.tensor_copy(acs, hsm)
                                    v.tensor_copy(acm, hsm).then_inc(S["g_am"], 1)
                                else:
                                    v.tensor_tensor(acs, acs, hsm, ALU.max)
                                    v.tensor_tensor(acm, acm, hsm,
                                                    ALU.min).then_inc(S["g_am"], 1)
                        cm = NHC - 1
                        v.wait_ge(S["s_si"], ti * NHC + cm + 1)
                        hsm = hx[:, cm * HEXW + ccmin + 1:][:, 0:ccw]
                        acs = accb[:, ccmin:][:, 0:ccw]
                        acm = accm[:, ccmin:][:, 0:ccw]
                        v.tensor_tensor(acs, acs, hsm, ALU.max)
                        v.tensor_tensor(acm, acm, hsm, ALU.min)
                        v.tensor_scalar(acm, acm, -1.0, None, ALU.mult)
                        v.tensor_tensor(acs, acs, acm,
                                        ALU.max).then_inc(S["g_am"], 1)
                        v.wait_ge(S["t_s2"], 3 * ti + 1)
                        for gl, (c0, wd) in enumerate(td["slices"]):
                            v.reduce_max(mxb[0:wd, gl:gl + 1],
                                         ps3[0:wd, 128 * gl:][:, 0:128], axis=AX.X)
                        v.engine_nop().then_inc(S["v_s2"], 1)
                        v.wait_ge(S["s_s2a"], ti + 1)
                        v.reciprocal(st1[:, 32:32 + nsg],
                                     mxb[:, 16:16 + nsg]).then_inc(S["v_s2b"], 1)
                        v.wait_ge(S["t_s2"], 3 * ti + 2)
                        v.tensor_copy(rowsb[0:1, 0:128 * nsg],
                                      ps3[0:1, 0:128 * nsg]).then_inc(S["v_s2"], 1)
                        v.wait_ge(S["t_s2"], 3 * ti + 3)
                        ins = None
                        for gl, (c0, wd) in enumerate(td["slices"]):
                            ins = v.tensor_copy(s2b[:, c0:c0 + wd],
                                                ps3[:, 128 * gl:][:, 0:wd])
                        ins.then_inc(S["v_s2"], 1)
                        for c in range(NHC):
                            k = ti * NHC + c
                            if k >= 2:
                                v.wait_ge(S["s_rd"], k - 1)
                            v.tensor_tensor(
                                ftmp[:, (c % 2) * 516:][:, 0:ccw],
                                hx[:, c * HEXW + ccmin + 1:][:, 0:ccw],
                                s2b[:, ccmin:][:, 0:ccw],
                                ALU.mult).then_inc(S["v_rm"], 1)
                        for si, (c0, wd) in enumerate(td["slices"]):
                            gi = tstart[ti] + si
                            b = 28 + 4 * (gi % 2)
                            v.wait_ge(S["t_f2"], gi + 1)
                            v.wait_ge(S["d_x2i"], 16 * (gi + 1))
                            if gi >= 1:
                                v.wait_ge(S["s_l4"], gi)
                            for half in range(2):
                                v.tensor_scalar(
                                    outy[0:wd, half * 512:][:, 0:512],
                                    ps2[0:wd, half * 512:][:, 0:512],
                                    mxb[0:wd, 8 + si:9 + si], None, ALU.mult)
                            oy = outy[0:wd, :]
                            ost = outst[0:wd, (gi % 2) * D:][:, 0:D]
                            ln_stats_v(v, b, oy, sub=slice(0, wd)).then_inc(S["v_f2"], 1)
                            v.wait_ge(S["s_l4"], gi + 1)
                            v.tensor_tensor(ost, ost,
                                            x2i[0:wd, (gi % 2) * D:][:, 0:D],
                                            ALU.add).then_inc(S["v_l4"], 1)

                @b4.scalar
                def _(s):
                    for ti, td in enumerate(TILES):
                        ccmin, ccmax = td["ccmin"], td["ccmax"]
                        ccw = ccmax - ccmin
                        nsg = len(td["slices"])
                        for c in range(NHC):
                            k = ti * NHC + c
                            s.wait_ge(S["v_cv"], k + 1)
                            s.activation(hx[:, c * HEXW + ccmin + 1:][:, 0:ccw],
                                         cbuf[:, (k % 2) * 516:][:, 0:ccw],
                                         AF.Silu).then_inc(S["s_si"], 1)
                        s.wait_ge(S["v_s2"], 3 * ti + 1)
                        if ti >= 1:
                            s.wait_ge(S["v_f2"], tstart[ti])
                        s.activation(mxb[:, 16:16 + nsg], mxb[:, 0:nsg],
                                     AF.Relu, bias=c_negeps[:])
                        s.activation(mxb[:, 16:16 + nsg], mxb[:, 16:16 + nsg],
                                     AF.Identity, bias=c_eps[:])
                        s.activation(mxb[:, 8:8 + nsg], mxb[:, 16:16 + nsg],
                                     AF.Identity, scale=swv[:, 3:4])
                        s.activation(mxb[:, 8:8 + nsg], mxb[:, 8:8 + nsg],
                                     AF.Identity,
                                     scale=1.0 / 127.0).then_inc(S["s_s2a"], 1)
                        s.wait_ge(S["v_s2b"], ti + 1)
                        s.activation(mxb[:, 24:24 + nsg], st1[:, 32:32 + nsg],
                                     AF.Identity, scale=127.0).then_inc(S["s_s2b"], 1)
                        for c in range(NHC):
                            k = ti * NHC + c
                            s.wait_ge(S["v_rm"], k + 1)
                            fts = ftmp[:, (c % 2) * 516:][:, 0:ccw]
                            s.activation(fts, fts, AF.Identity, bias=c_mg[:])
                            s.activation(hx[:, c * HEXW + ccmin + 1:][:, 0:ccw],
                                         fts, AF.Identity,
                                         bias=c_nmg[:]).then_inc(S["s_rd"], 1)
                        for si, (c0, wd) in enumerate(td["slices"]):
                            gi = tstart[ti] + si
                            b = 28 + 4 * (gi % 2)
                            s.wait_ge(S["v_f2"], gi + 1)
                            if gi >= 2:
                                s.wait_ge(S["d_out"], 16 * (gi - 1))
                            ln_rstd_s(s, b, sub=slice(0, wd))
                            s.activation(jkA[:, 0:256], jkB[:, 0:256], AF.Identity)
                            s.activation(jkB[:, 0:256], jkA[:, 0:256], AF.Identity)
                            s.activation(outst[0:wd, (gi % 2) * D:][:, 0:D],
                                         outy[0:wd, :], AF.Identity,
                                         scale=stS[0:wd, b + 2:b + 3],
                                         bias=stS[0:wd, b + 3:b + 4]).then_inc(S["s_l4"], 1)

            if debug:
                with nc.Block("dbgB") as dbgb:
                    @dbgb.sync
                    def _(sync):
                        sync.dma_start(out=dt1_h, in_=tq1T[:]).then_inc(S["d_dbg"], 16)
                        sync.dma_start(out=dt2_h, in_=tq2T[:]).then_inc(S["d_dbg"], 16)
                        sync.dma_start(out=dsw_h, in_=swv[:]).then_inc(S["d_dbg"], 16)
                        sync.dma_start(out=dhx_h, in_=hx[:]).then_inc(S["d_dbg"], 16)
                        sync.wait_ge(S["d_dbg"], 48 + 96)

    return nc


def kernel(L0_attn_x, L0_attn_w_qkv, L0_ffn_w1, L0_ffn_conv_w, L0_ffn_conv_b,
           L0_ffn_w2, _debug=False, _maxphase=4):
    x = np.ascontiguousarray(np.asarray(L0_attn_x, dtype=np.float32))
    wqkv = np.ascontiguousarray(np.asarray(L0_attn_w_qkv, dtype=np.float32))
    w1 = np.ascontiguousarray(np.asarray(L0_ffn_w1, dtype=np.float32))
    convw = np.ascontiguousarray(
        np.asarray(L0_ffn_conv_w, dtype=np.float32).reshape(DH, 3))
    convb = np.ascontiguousarray(
        np.asarray(L0_ffn_conv_b, dtype=np.float32).reshape(DH, 1))
    w2 = np.ascontiguousarray(np.asarray(L0_ffn_w2, dtype=np.float32))

    nc = build_nc(debug=_debug, maxphase=_maxphase)
    in_maps = [{"x": x[i], "wqkv": wqkv, "w1": w1, "convw": convw,
                "convb": convb, "w2": w2} for i in range(8)]
    res = run_bass_kernel_spmd(nc, in_maps, core_ids=list(range(8)))
    out = np.stack([np.asarray(r["out"]) for r in res.results], axis=0)
    if _debug:
        return out, res
    return out


# version 03
# speedup vs baseline: 2.1351x; 1.1219x over previous
import numpy as np
import concourse.bass as bass
import concourse.mybir as mybir
from concourse.bass_utils import run_bass_kernel_spmd
from contextlib import ExitStack

F32 = mybir.dt.float32
BF16 = mybir.dt.bfloat16
I32 = mybir.dt.int32
AF = mybir.ActivationFunctionType
ALU = mybir.AluOpType
AX = mybir.AxisListType

T = 2048
D = 1024
H = 16
E = 64
W = 128
NW = T // W
DH = 4096
NDC = D // 128
NHC = DH // 128
TGS = 512
NTG = T // TGS
FT = 512
NFT = T // FT
QKVN = 3 * D
EXP_SHIFT = -20.0
HEXW = 515


def _ffn_tiles():
    tiles = []
    for i in range(NFT):
        tok0 = FT * i - 1
        if i == 0:
            ccmin, ccmax = 1, 512
            slices = [(1, 128), (129, 128), (257, 128), (385, 127)]
        elif i < NFT - 1:
            ccmin, ccmax = 0, 512
            slices = [(0, 128), (128, 128), (256, 128), (384, 128)]
        else:
            ccmin, ccmax = 0, 513
            slices = [(0, 128), (128, 128), (256, 128), (384, 128), (512, 1)]
        tiles.append(dict(i=i, tok0=tok0, ccmin=ccmin, ccmax=ccmax,
                          slices=slices))
    return tiles


TILES = _ffn_tiles()
GSL = []
for _td in TILES:
    for _si, (_c0, _wd) in enumerate(_td["slices"]):
        GSL.append((_td["i"], _si, _c0, _wd))
NSL = len(GSL)


def build_nc(debug=False, maxphase=4):
    nc = bass.Bass("TRN2", target_bir_lowering=False,
                   dynamic_dma_scratch_size=4096,
                   detect_race_conditions=False)

    x_h = nc.dram_tensor("x", [T, D], F32, kind="ExternalInput").ap()
    wqkv_h = nc.dram_tensor("wqkv", [D, QKVN], F32, kind="ExternalInput").ap()
    w1_h = nc.dram_tensor("w1", [DH, D], F32, kind="ExternalInput").ap()
    convw_h = nc.dram_tensor("convw", [DH, 3], F32, kind="ExternalInput").ap()
    convb_h = nc.dram_tensor("convb", [DH, 1], F32, kind="ExternalInput").ap()
    w2_h = nc.dram_tensor("w2", [D, DH], F32, kind="ExternalInput").ap()
    out_h = nc.dram_tensor("out", [T, D], F32, kind="ExternalOutput").ap()

    dbgk = "ExternalOutput" if debug else "Internal"
    qh_h = nc.dram_tensor("qh", [NDC, 128, T], BF16, kind=dbgk).ap()
    x2_h = nc.dram_tensor("x2s", [T, D], F32, kind=dbgk).ap()
    xqh_h = nc.dram_tensor("xqs", [NDC, 128, T], BF16, kind=dbgk).ap()
    if debug:
        dk_h = nc.dram_tensor("dk", [128, NDC * T], BF16, kind="ExternalOutput").ap()
        dxnT_h = nc.dram_tensor("dxnT", [128, NDC * TGS], BF16, kind="ExternalOutput").ap()
        dst1_h = nc.dram_tensor("dst1", [128, 48], F32, kind="ExternalOutput").ap()
        dxnbf_h = nc.dram_tensor("dxnbf", [128, 2 * D], BF16, kind="ExternalOutput").ap()
        dv_h = nc.dram_tensor("dv", [128, NW * D], BF16, kind="ExternalOutput").ap()
        dt1_h = nc.dram_tensor("dt1", [128, NDC * DH], BF16, kind="ExternalOutput").ap()
        dt2_h = nc.dram_tensor("dt2", [128, NHC * D], BF16, kind="ExternalOutput").ap()
        dsw_h = nc.dram_tensor("dsw", [128, 4], F32, kind="ExternalOutput").ap()
        dhx_h = nc.dram_tensor("dhx", [128, NHC * HEXW], BF16, kind="ExternalOutput").ap()

    ctx = ExitStack()
    with ctx:
        e = ctx.enter_context

        ident = e(nc.sbuf_tensor("ident", [128, 128], BF16))
        identf = e(nc.sbuf_tensor("identf", [128, 128], F32))
        onesrow = e(nc.sbuf_tensor("onesrow", [1, 128], F32))
        onescol = e(nc.sbuf_tensor("onescol", [128, 1], F32))
        c_m20 = e(nc.sbuf_tensor("c_m20", [128, 1], F32))
        c_eps = e(nc.sbuf_tensor("c_eps", [128, 1], F32))
        c_negeps = e(nc.sbuf_tensor("c_negeps", [128, 1], F32))
        c_mg = e(nc.sbuf_tensor("c_mg", [128, 1], F32))
        c_nmg = e(nc.sbuf_tensor("c_nmg", [128, 1], F32))
        cw = e(nc.sbuf_tensor("cw", [128, 3 * NHC], F32))
        cbv_ = e(nc.sbuf_tensor("cbv", [128, NHC], F32))
        cwsc = e(nc.sbuf_tensor("cwsc", [128, 3 * NHC], F32))
        swv = e(nc.sbuf_tensor("swv", [128, 4], F32))
        sums = e(nc.sbuf_tensor("sums", [128, 32], F32))
        sinv = e(nc.sbuf_tensor("sinv", [128, 16], F32))
        st1 = e(nc.sbuf_tensor("st1", [128, 48], F32))
        wacc = e(nc.sbuf_tensor("wacc", [128, 2], F32))
        sqb = e(nc.sbuf_tensor("sqb", [128, D], F32))
        jkA = e(nc.sbuf_tensor("jkA", [128, 256], F32))
        jkB = e(nc.sbuf_tensor("jkB", [128, 256], F32))
        stS = e(nc.sbuf_tensor("stS", [128, 48], F32))

        S = {}
        for name in ["d_wq", "d_cw", "d_dbg",
                     "d_x", "v_st", "s_xn", "t_tr", "v_xnT",
                     "t_qk", "v_qk", "t_v", "v_v", "d_qh",
                     "d_qw", "d_xr", "d_x2", "d_xq", "d_xqt",
                     "t_sc", "s_exp", "t_tr2", "v_tr", "t_av",
                     "v_ep", "s_e1", "v_e2", "s_e2", "v_e3", "s_r1",
                     "v_e3a", "s_e3a", "v_e3b", "s_e3b",
                     "v_s2b", "s_s2a", "s_s2b", "g_e3",
                     "d_w1", "g_p1", "t_p1", "t_sw", "v_sw", "d_w2", "v_q2", "d_tq",
                     "d_xt", "d_x2i", "t_f1", "v_ev", "v_cv", "s_si", "g_am",
                     "t_s2", "v_s2", "g_rd", "t_f2", "v_f2", "s_l4", "v_l4",
                     "d_out", "g_i4", "v_p1", "t_tq", "v_tqe", "v_rm", "s_rd"]:
            S[name] = e(nc.semaphore(name))

        def ln_stats_v(v, b, src, sub=None):
            p = sub if sub is not None else slice(0, 128)
            v.scalar_tensor_tensor(sqb[p, :], src, 1.0, src, ALU.bypass, ALU.mult)
            v.reduce_sum(st1[p, b:b + 1], src, axis=AX.X)
            return v.reduce_sum(st1[p, b + 1:b + 2], sqb[p, :], axis=AX.X)

        def ln_rstd_s(s, b, sub=None):
            p = sub if sub is not None else slice(0, 128)
            s.activation(stS[p, b:b + 1], st1[p, b:b + 1], AF.Identity,
                         scale=1.0 / D)
            s.activation(stS[p, b + 1:b + 2], stS[p, b:b + 1], AF.Identity,
                         scale=stS[p, b:b + 1])
            s.activation(stS[p, b + 1:b + 2], stS[p, b + 1:b + 2], AF.Identity,
                         scale=-1.0)
            s.activation(stS[p, b + 1:b + 2], st1[p, b + 1:b + 2], AF.Identity,
                         scale=1.0 / D, bias=stS[p, b + 1:b + 2])
            s.activation(stS[p, b + 2:b + 3], stS[p, b + 1:b + 2], AF.Ln,
                         bias=c_eps[p, :])
            s.activation(stS[p, b + 2:b + 3], stS[p, b + 2:b + 3], AF.Exp,
                         scale=-0.5)
            s.activation(stS[p, b + 3:b + 4], stS[p, b:b + 1], AF.Identity,
                         scale=stS[p, b + 2:b + 3])
            return s.activation(stS[p, b + 3:b + 4], stS[p, b + 3:b + 4],
                                AF.Identity, scale=-1.0)

        def vpad(v):
            v.tensor_copy(jkA[:], jkB[:])
            return v.tensor_copy(jkB[:], jkA[:])

        with nc.Block("b0") as b0:
            @b0.sync
            def _(sync):
                for c in range(NHC):
                    sync.dma_start(
                        out=cw[:, 3 * c:3 * c + 3],
                        in_=convw_h[c * 128:(c + 1) * 128, :],
                    ).then_inc(S["d_cw"], 16)
                    sync.dma_start(
                        out=cbv_[:, c:c + 1],
                        in_=convb_h[c * 128:(c + 1) * 128, :],
                    ).then_inc(S["d_cw"], 16)
                sync.wait_ge(S["d_cw"], 32 * NHC)

            @b0.gpsimd
            def _(g):
                for idt in (ident, identf):
                    g.memset(idt[:], 0.0)
                    g.affine_select(out=idt[:], in_=idt[:],
                                    compare_op=ALU.not_equal, fill=1.0, base=0,
                                    pattern=[[-1, 128]], channel_multiplier=1)
                g.memset(onesrow[:], 1.0)
                g.memset(onescol[:], 1.0)
                g.memset(c_m20[:], EXP_SHIFT)
                g.memset(c_eps[:], 1e-5)
                g.memset(c_negeps[:], -1e-5)
                g.memset(c_mg[:], 12582912.0)
                g.memset(c_nmg[:], -12582912.0)
                g.memset(st1[:], 0.0)
                g.memset(stS[:], 0.0)
                g.memset(sqb[:], 0.0)
                g.memset(jkA[:], 0.0)
                g.memset(jkB[:], 0.0)

        with ExitStack() as ctxA:
            ea = ctxA.enter_context
            wqkv = ea(nc.sbuf_tensor("wqkv_sb", [128, NDC * QKVN], BF16))
            xnT = ea(nc.sbuf_tensor("xnT", [128, NDC * TGS], BF16))
            xnbf = ea(nc.sbuf_tensor("xnbf", [128, 2 * D], BF16))
            kf = ea(nc.sbuf_tensor("kf", [128, NDC * T], BF16))
            vt = ea(nc.sbuf_tensor("vt", [128, NW * D], BF16))
            qsb = ea(nc.sbuf_tensor("qsb", [128, 8 * TGS], BF16))
            qwin = ea(nc.sbuf_tensor("qwin", [128, 2 * NDC * 128], BF16))
            xln = ea(nc.sbuf_tensor("xln", [128, 3 * D], F32))
            xres = ea(nc.sbuf_tensor("xres", [128, 2 * D], F32))
            attno = ea(nc.sbuf_tensor("attno", [128, 2 * D], F32))
            wscr = ea(nc.sbuf_tensor("wscr", [128, 2 * D], F32))
            wscrB = ea(nc.sbuf_tensor("wscrB", [128, 2 * D], F32))
            x2b = ea(nc.sbuf_tensor("x2b", [128, 3 * D], F32))
            xqf = ea(nc.sbuf_tensor("xqf", [128, 1024], F32))
            xqbf = ea(nc.sbuf_tensor("xqbf", [128, 2 * D], BF16))
            xqtw = ea(nc.sbuf_tensor("xqtw", [128, 2 * NDC * 128], BF16,
                                     is_transpose_dest=True))
            probs = ea(nc.sbuf_tensor("probs", [128, 2 * 768], BF16))
            probsT = ea(nc.sbuf_tensor("probsT", [128, 2 * 768], BF16))

            with nc.Block("b0b") as b0b:
                @b0b.gpsimd
                def _(g):
                    for d in range(NDC):
                        g.dma_start(
                            out=wqkv[:, d * QKVN:(d + 1) * QKVN],
                            in_=wqkv_h[d * 128:(d + 1) * 128, :],
                        ).then_inc(S["d_wq"], 16)

            with ExitStack() as ctx1:
                e1 = ctx1.enter_context
                ptr = e1(nc.psum_tensor("ptr", [128, 1024], F32))
                psA = [e1(nc.psum_tensor(f"psA{i}", [128, 512], F32))
                       for i in (0, 1)]
                pv = [e1(nc.psum_tensor(f"pv{i}", [128, 1024], F32))
                      for i in (0, 1)]

                with nc.Block("b1") as b1:
                    @b1.sync
                    def _(sync):
                        for c in range(NW + 1):
                            if c < NW:
                                if c >= 3:
                                    sync.wait_ge(S["s_xn"], c - 2)
                                sync.dma_start(
                                    out=xln[:, (c % 3) * D:][:, 0:D],
                                    in_=x_h[c * 128:(c + 1) * 128, :],
                                ).then_inc(S["d_x"], 16)
                            if c % 4 == 0 and c >= 4:
                                tg = c // 4
                                for F in range(8):
                                    sync.wait_ge(S["v_qk"], (tg - 1) * 16 + F + 1)
                                    sync.dma_start(
                                        out=qh_h[F, :, (tg - 1) * TGS:tg * TGS],
                                        in_=qsb[:, F * TGS:][:, 0:TGS],
                                    ).then_inc(S["d_qh"], 16)
                        sync.wait_ge(S["d_qh"], 16 * NTG * 8)

                    @b1.vector
                    def _(v):
                        for c in range(NW + 1):
                            if c < NW:
                                b = 4 * (c % 2)
                                v.wait_ge(S["d_x"], 16 * (c + 1))
                                xc = xln[:, (c % 3) * D:][:, 0:D]
                                ln_stats_v(v, b, xc).then_inc(S["v_st"], 1)
                            if c < 1:
                                continue
                            cp = c - 1
                            tg, l = cp // 4, cp % 4
                            v.wait_ge(S["t_tr"], cp + 1)
                            if l == 0 and tg > 0:
                                v.wait_ge(S["t_qk"], 16 * tg)
                                v.wait_ge(S["t_v"], 4 * tg)
                            v.tensor_copy(
                                xnT[:].rearrange("p (d t) -> p d t", d=NDC)[:, :, l * 128:(l + 1) * 128],
                                ptr[:].rearrange("p (d t) -> p d t", d=NDC),
                            ).then_inc(S["v_xnT"], 1)
                            if l == 3:
                                for F in range(16):
                                    gidx = tg * 16 + F
                                    v.wait_ge(S["t_qk"], gidx + 1)
                                    if F < 8:
                                        if tg >= 1:
                                            v.wait_ge(S["d_qh"],
                                                      16 * (8 * (tg - 1) + F + 1))
                                        v.tensor_scalar(
                                            qsb[:, F * TGS:][:, 0:TGS],
                                            psA[F % 2][:], 0.125, None, ALU.mult,
                                        ).then_inc(S["v_qk"], 1)
                                    else:
                                        v.tensor_copy(
                                            kf[:, (F - 8) * T + tg * TGS:][:, 0:TGS],
                                            psA[F % 2][:],
                                        ).then_inc(S["v_qk"], 1)
                                for sl in range(4):
                                    gidx = tg * 4 + sl
                                    v.wait_ge(S["t_v"], gidx + 1)
                                    v.tensor_copy(
                                        vt[:, gidx * D:(gidx + 1) * D],
                                        pv[sl % 2][:],
                                    ).then_inc(S["v_v"], 1)

                    @b1.scalar
                    def _(s):
                        for c in range(NW + 1):
                            if c < NW:
                                b = 4 * (c % 2)
                                s.wait_ge(S["v_st"], c + 1)
                                ln_rstd_s(s, b)
                            if c >= 1:
                                cp = c - 1
                                b = 4 * (cp % 2)
                                if cp >= 2:
                                    s.wait_ge(S["t_tr"], cp - 1)
                                s.activation(xnbf[:, (cp % 2) * D:][:, 0:D],
                                             xln[:, (cp % 3) * D:][:, 0:D],
                                             AF.Identity, scale=stS[:, b + 2:b + 3],
                                             bias=stS[:, b + 3:b + 4]).then_inc(S["s_xn"], 1)

                    @b1.tensor
                    def _(t):
                        for tg in range(NTG):
                            for l in range(4):
                                c = tg * 4 + l
                                t.wait_ge(S["s_xn"], c + 1)
                                if c > 0:
                                    t.wait_ge(S["v_xnT"], c)
                                ins = None
                                for d in range(NDC):
                                    ins = t.matmul(
                                        ptr[:, d * 128:(d + 1) * 128],
                                        xnbf[:, (c % 2) * D + d * 128:][:, 0:128],
                                        ident[:], start=True, stop=True)
                                ins.then_inc(S["t_tr"], 1)
                            t.wait_ge(S["v_xnT"], 4 * (tg + 1))
                            if tg == 0:
                                t.wait_ge(S["d_wq"], 16 * NDC)
                            for F in range(16):
                                gidx = tg * 16 + F
                                if gidx >= 2:
                                    t.wait_ge(S["v_qk"], gidx - 1)
                                ins = None
                                for d in range(NDC):
                                    ins = t.matmul(
                                        psA[F % 2][:],
                                        wqkv[:, d * QKVN + F * 128:][:, 0:128],
                                        xnT[:, d * TGS:(d + 1) * TGS],
                                        start=(d == 0), stop=(d == NDC - 1))
                                ins.then_inc(S["t_qk"], 1)
                            for sl in range(4):
                                gidx = tg * 4 + sl
                                if gidx >= 2:
                                    t.wait_ge(S["v_v"], gidx - 1)
                                ins = None
                                for d in range(NDC):
                                    for hh in range(2):
                                        ins = t.matmul(
                                            pv[sl % 2][:, hh * 512:][:, 0:512],
                                            xnT[:, d * TGS + sl * 128:][:, 0:128],
                                            wqkv[:, d * QKVN + 2 * D + hh * 512:][:, 0:512],
                                            start=(d == 0), stop=(d == NDC - 1))
                                ins.then_inc(S["t_v"], 1)

            if debug:
                with nc.Block("dbgB1") as dbgb1:
                    @dbgb1.sync
                    def _(sync):
                        sync.dma_start(out=dxnT_h, in_=xnT[:]).then_inc(S["d_dbg"], 16)
                        sync.dma_start(out=dst1_h, in_=st1[:]).then_inc(S["d_dbg"], 16)
                        sync.dma_start(out=dxnbf_h, in_=xnbf[:]).then_inc(S["d_dbg"], 16)
                        sync.wait_ge(S["d_dbg"], 48)

            if maxphase < 2:
                return nc
            wins = []
            for n in range(NW):
                cb0 = max(n - 1, 0)
                cb1 = min(n + 1, NW - 1)
                wins.append((cb0, cb1 - cb0 + 1))

            with ExitStack() as ctx2:
                e2 = ctx2.enter_context
                scp = [e2(nc.psum_tensor(f"scp{i}", [128, 1024], F32))
                       for i in (0, 1)]
                ptp = e2(nc.psum_tensor("ptp", [128, 1024], F32))
                avp = e2(nc.psum_tensor("avp", [128, 1024], F32))

                with nc.Block("b2") as b2:
                    @b2.sync
                    def _(sync):
                        for w in range(NW + 2):
                            if w < NW:
                                n = w
                                if n >= 2:
                                    sync.wait_ge(S["t_sc"], 8 * (n - 1))
                                for d in range(NDC):
                                    sync.dma_start(
                                        out=qwin[:, (n % 2) * NDC * 128 + d * 128:][:, 0:128],
                                        in_=qh_h[d, :, n * 128:(n + 1) * 128],
                                    ).then_inc(S["d_qw"], 16)
                                if n >= 2:
                                    sync.wait_ge(S["v_e2"], n - 1)
                                sync.dma_start(
                                    out=xres[:, (n % 2) * D:][:, 0:D],
                                    in_=x_h[n * 128:(n + 1) * 128, :],
                                ).then_inc(S["d_xr"], 16)
                            if 1 <= w <= NW:
                                n = w - 1
                                sync.wait_ge(S["v_e2"], w)
                                sync.dma_start(
                                    out=x2_h[n * 128:(n + 1) * 128, :],
                                    in_=x2b[:, (n % 3) * D:][:, 0:D],
                                ).then_inc(S["d_x2"], 16)
                            if 2 <= w <= NW + 1:
                                n = w - 2
                                sync.wait_ge(S["g_e3"], n + 1)
                                if n >= 2:
                                    sync.wait_ge(S["d_xq"], 16 * NDC * (n - 1))
                                for d in range(NDC):
                                    sync.dma_start_transpose(
                                        xqtw[:, (n % 2) * NDC * 128 + d * 128:][:, 0:128],
                                        xqbf[:, (n % 2) * D + d * 128:][:, 0:128],
                                    ).then_inc(S["d_xqt"], 16)
                                sync.wait_ge(S["d_xqt"], 16 * NDC * (n + 1))
                                for d in range(NDC):
                                    sync.dma_start(
                                        out=xqh_h[d, :, n * 128:(n + 1) * 128],
                                        in_=xqtw[:, (n % 2) * NDC * 128 + d * 128:][:, 0:128],
                                    ).then_inc(S["d_xq"], 16)
                        sync.wait_ge(S["d_x2"], 16 * NW)
                        sync.wait_ge(S["d_xq"], 16 * NDC * NW)

                    @b2.tensor
                    def _(t):
                        NP = 8 * NW
                        for p in range(NP + 1):
                            if p >= 1:
                                q = p - 1
                                n, j = divmod(q, 8)
                                cb0, nch = wins[n]
                                t.wait_ge(S["s_exp"], q + 1)
                                if q >= 1:
                                    t.wait_ge(S["v_tr"], q)
                                ins = None
                                for hh in range(2):
                                    for c in range(nch):
                                        ins = t.matmul(
                                            ptp[:, hh * 512 + c * 128:][:, 0:128],
                                            probs[:, (q % 2) * 768 + hh * 384 + c * 128:][:, 0:128],
                                            ident[:], start=True, stop=True)
                                ins.then_inc(S["t_tr2"], 1)
                                t.wait_ge(S["v_tr"], q + 1)
                                if j == 0:
                                    t.wait_ge(S["v_ep"], n)
                                ins = None
                                for hh in range(2):
                                    hgl = 2 * j + hh
                                    for c in range(nch):
                                        ins = t.matmul(
                                            avp[:, hgl * 64:][:, 0:64],
                                            probsT[:, (q % 2) * 768 + hh * 384 + c * 128:][:, 0:128],
                                            vt[:, (cb0 + c) * D + hgl * 64:][:, 0:64],
                                            start=(c == 0), stop=(c == nch - 1))
                                ins.then_inc(S["t_av"], 1)
                            if p < NP:
                                n, j = divmod(p, 8)
                                cb0, nch = wins[n]
                                ctxw = nch * 128
                                if j == 0:
                                    t.wait_ge(S["d_qw"], 16 * NDC * (n + 1))
                                if p >= 2:
                                    t.wait_ge(S["s_exp"], p - 1)
                                ins = None
                                for hh in range(2):
                                    h0 = 64 * hh
                                    ins = t.matmul(
                                        scp[p % 2][:, hh * 512:][:, 0:ctxw],
                                        qwin[h0:h0 + 64,
                                             (n % 2) * NDC * 128 + j * 128:][:, 0:128],
                                        kf[h0:h0 + 64, j * T + cb0 * 128:][:, 0:ctxw],
                                        start=True, stop=True,
                                        tile_position=(h0, 0))
                                ins.then_inc(S["t_sc"], 1)

                    @b2.scalar
                    def _(s):
                        for w in range(NW + 3):
                            if 3 <= w <= NW + 2:
                                n3 = w - 3
                                qb3 = 24 + 4 * (n3 % 2)
                                s.wait_ge(S["v_e3"], n3 + 1)
                                s.activation(xqbf[:, (n3 % 2) * D:][:, 0:D],
                                             xqf[:], AF.Identity,
                                             bias=c_nmg[:])
                                s.activation(xqbf[:, (n3 % 2) * D:][:, 0:D],
                                             xqbf[:, (n3 % 2) * D:][:, 0:D],
                                             AF.Identity,
                                             scale=stS[:, qb3 + 1:qb3 + 2],
                                             ).then_inc(S["g_e3"], 1)
                            if w < NW:
                                n = w
                                cb0, nch = wins[n]
                                ctxw = nch * 128
                                for j in range(8):
                                    p = 8 * n + j
                                    s.wait_ge(S["t_sc"], p + 1)
                                    if p >= 2:
                                        s.wait_ge(S["t_tr2"], p - 1)
                                    if j == 0 and n >= 2:
                                        s.wait_ge(S["v_ep"], n - 1)
                                    ins = None
                                    for hh in range(2):
                                        ins = s.activation(
                                            probs[:, (p % 2) * 768 + hh * 384:][:, 0:ctxw],
                                            scp[p % 2][:, hh * 512:][:, 0:ctxw],
                                            AF.Exp, bias=c_m20[:],
                                            accum_out=sums[:, 16 * (n % 2) + 2 * j + hh:][:, 0:1])
                                    ins.then_inc(S["s_exp"], 1)
                            if 1 <= w <= NW:
                                n1 = w - 1
                                b1c = 8 + 4 * (n1 % 2)
                                s.wait_ge(S["v_ep"], w)
                                ln_rstd_s(s, b1c)
                            if 2 <= w <= NW + 1:
                                n2 = w - 2
                                b2c = 16 + 4 * (n2 % 2)
                                s.wait_ge(S["v_e2"], n2 + 1)
                                if n2 >= 2:
                                    s.wait_ge(S["v_e3"], n2 - 1)
                                ln_rstd_s(s, b2c)
                            if 1 <= w <= NW:
                                n1 = w - 1
                                b1c = 8 + 4 * (n1 % 2)
                                s.activation(wscr[:, (n1 % 2) * D:][:, 0:D],
                                             attno[:, (n1 % 2) * D:][:, 0:D],
                                             AF.Identity, scale=stS[:, b1c + 2:b1c + 3],
                                             bias=stS[:, b1c + 3:b1c + 4]).then_inc(S["s_e1"], 1)
                            if 2 <= w <= NW + 1:
                                n2 = w - 2
                                b2c = 16 + 4 * (n2 % 2)
                                s.activation(wscrB[:, (n2 % 2) * D:][:, 0:D],
                                             x2b[:, (n2 % 3) * D:][:, 0:D],
                                             AF.Identity, scale=stS[:, b2c + 2:b2c + 3],
                                             bias=stS[:, b2c + 3:b2c + 4]).then_inc(S["s_e2"], 1)
                                qb = 24 + 4 * (n2 % 2)
                                qa = 24 + 2 * (n2 % 2)
                                s.wait_ge(S["v_e3a"], n2 + 1)
                                s.activation(stS[:, qb:qb + 1], st1[:, qa:qa + 1],
                                             AF.Relu, bias=c_negeps[:])
                                s.activation(stS[:, qb:qb + 1], stS[:, qb:qb + 1],
                                             AF.Identity, bias=c_eps[:])
                                s.activation(stS[:, qb + 1:qb + 2], stS[:, qb:qb + 1],
                                             AF.Identity,
                                             scale=1.0 / 127.0).then_inc(S["s_e3a"], 1)
                            if 2 <= w <= NW + 1:
                                n2r = w - 2
                                qbr = 24 + 4 * (n2r % 2)
                                qar = 24 + 2 * (n2r % 2)
                                s.wait_ge(S["v_e3b"], n2r + 1)
                                s.activation(stS[:, qbr + 2:qbr + 3],
                                             st1[:, qar + 1:qar + 2],
                                             AF.Identity,
                                             scale=127.0).then_inc(S["s_e3b"], 1)

                    @b2.vector
                    def _(v):
                        for w in range(NW + 2):
                            if w < NW:
                                n = w
                                cb0, nch = wins[n]
                                ctxw = nch * 128
                                for j in range(8):
                                    p = 8 * n + j
                                    v.wait_ge(S["t_tr2"], p + 1)
                                    if p >= 2:
                                        v.wait_ge(S["t_av"], p - 1)
                                    v.tensor_copy(
                                        probsT[:, (p % 2) * 768:][:, 0:768]
                                            .rearrange("p (a b) -> p a b", a=2)[:, :, 0:ctxw],
                                        ptp[:].rearrange("p (a b) -> p a b", a=2)[:, :, 0:ctxw],
                                    ).then_inc(S["v_tr"], 1)
                                b = 8 + 4 * (n % 2)
                                v.wait_ge(S["t_av"], 8 * (n + 1))
                                v.wait_ge(S["s_exp"], 8 * (n + 1))
                                if n >= 2:
                                    v.wait_ge(S["s_e1"], n - 1)
                                v.reciprocal(sinv[:], sums[:, 16 * (n % 2):][:, 0:16])
                                vpad(v)
                                ao = attno[:, (n % 2) * D:][:, 0:D]
                                for hgl in range(H):
                                    v.tensor_scalar(ao[:, hgl * 64:][:, 0:64],
                                                    avp[:, hgl * 64:][:, 0:64],
                                                    sinv[:, hgl:hgl + 1], None,
                                                    ALU.mult)
                                ln_stats_v(v, b, ao).then_inc(S["v_ep"], 1)
                            if 1 <= w <= NW:
                                n = w - 1
                                b = 16 + 4 * (n % 2)
                                v.wait_ge(S["s_e1"], n + 1)
                                v.wait_ge(S["d_xr"], 16 * (n + 1))
                                if n >= 2:
                                    v.wait_ge(S["d_x2"], 16 * (n - 1))
                                xb2 = x2b[:, (n % 3) * D:][:, 0:D]
                                v.tensor_tensor(xb2, wscr[:, (n % 2) * D:][:, 0:D],
                                                xres[:, (n % 2) * D:][:, 0:D],
                                                ALU.add)
                                ln_stats_v(v, b, xb2).then_inc(S["v_e2"], 1)
                            if 2 <= w <= NW + 1:
                                n = w - 2
                                v.wait_ge(S["s_e2"], n + 1)
                                qb = 24 + 4 * (n % 2)
                                qa = 24 + 2 * (n % 2)
                                xn3 = wscrB[:, (n % 2) * D:][:, 0:D]
                                v.reduce_max(st1[:, qa:qa + 1], xn3, axis=AX.X,
                                             apply_absolute_value=True).then_inc(S["v_e3a"], 1)
                                v.wait_ge(S["s_e3a"], n + 1)
                                v.reciprocal(st1[:, qa + 1:qa + 2],
                                             stS[:, qb:qb + 1]).then_inc(S["v_e3b"], 1)
                                v.wait_ge(S["s_e3b"], n + 1)
                                if n >= 2:
                                    v.wait_ge(S["d_xqt"], 16 * NDC * (n - 1))
                                MAGIC = 12582912.0
                                ins = None
                                for hf in range(2):
                                    ins = v.tensor_scalar(
                                        xqf[:, hf * 512:][:, 0:512],
                                        xn3[:, hf * 512:][:, 0:512],
                                        stS[:, qb + 2:qb + 3], MAGIC, ALU.mult, ALU.add)
                                ins.then_inc(S["v_e3"], 1)

            if debug:
                with nc.Block("dbgA") as dbga:
                    @dbga.sync
                    def _(sync):
                        sync.dma_start(out=dk_h, in_=kf[:]).then_inc(S["d_dbg"], 16)
                        sync.dma_start(out=dv_h, in_=vt[:]).then_inc(S["d_dbg"], 16)
                        sync.wait_ge(S["d_dbg"], 48 + 32)

        if maxphase < 3:
            return nc
        with ExitStack() as ctxB:
            eb = ctxB.enter_context
            tq1T = eb(nc.sbuf_tensor("tq1T", [128, NDC * DH], BF16))
            tq2T = eb(nc.sbuf_tensor("tq2T", [128, NHC * D], BF16))

            chunks = []
            for c in range(NHC):
                chunks.append(("w1", c))
            for c in range(8):
                for g2 in range(4):
                    chunks.append(("w2", (c, g2)))
            NCH = len(chunks)

            with ExitStack() as ctx3:
                e3 = ctx3.enter_context
                wst = e3(nc.sbuf_tensor("wst", [128, 2 * 1024], F32))
                wst2 = e3(nc.sbuf_tensor("wst2", [128, 2 * 1024], F32))
                wtmp = e3(nc.sbuf_tensor("wtmp", [128, 1024], F32))
                tqn = e3(nc.sbuf_tensor("tqn", [128, 2 * 1024], BF16))
                absc = e3(nc.sbuf_tensor("absc", [128, 64], F32))
                swsc = e3(nc.sbuf_tensor("swsc", [1, 8], F32))
                psw = e3(nc.psum_tensor("psw", [128, 8], F32))
                ppa = e3(nc.psum_tensor("ppa", [1, 64], F32))
                pst = [e3(nc.psum_tensor(f"pst{i}", [128, 1024], F32))
                       for i in (0, 1)]

                def chunk_src(ch):
                    kind, meta = chunks[ch]
                    if kind == "w1":
                        return w1_h[meta * 128:(meta + 1) * 128, :]
                    c, g2 = meta
                    return w2_h[c * 128:(c + 1) * 128, g2 * 1024:(g2 + 1) * 1024]

                CLIP = 1.4999998807907104
                MAGIC = 12582912.0

                with nc.Block("b3") as b3:
                    @b3.sync
                    def _(sync):
                        for ch in range(NCH):
                            if ch >= 2:
                                sync.wait_ge(S["v_p1"], ch - 1)
                            sync.dma_start(
                                out=wst[:, (ch % 2) * 1024:][:, 0:1024],
                                in_=chunk_src(ch),
                            ).then_inc(S["d_w1"], 16)
                        for ch in range(NCH):
                            if ch >= 2:
                                sync.wait_ge(S["v_q2"], ch - 1)
                            sync.dma_start(
                                out=wst2[:, (ch % 2) * 1024:][:, 0:1024],
                                in_=chunk_src(ch),
                            ).then_inc(S["d_w2"], 16)

                    @b3.tensor
                    def _(t):
                        t.wait_ge(S["v_p1"], NHC)
                        t.matmul(ppa[0:1, 0:NHC], onescol[:], absc[:, 0:NHC],
                                 start=True, stop=True)
                        t.wait_ge(S["v_p1"], NCH)
                        t.matmul(ppa[0:1, NHC:NCH], onescol[:],
                                 absc[:, NHC:NCH],
                                 start=True, stop=True).then_inc(S["t_p1"], 1)
                        t.wait_ge(S["v_sw"], 1)
                        t.matmul(psw[:, 4:8], onesrow[:], swsc[0:1, 0:4],
                                 start=True, stop=True).then_inc(S["t_sw"], 1)
                        for ch in range(NCH):
                            t.wait_ge(S["v_q2"], ch + 1)
                            if ch >= 2:
                                t.wait_ge(S["v_tqe"], ch - 1)
                            ins = None
                            for jj in range(8):
                                ins = t.matmul(
                                    pst[ch % 2][:, jj * 128:][:, 0:128],
                                    tqn[:, (ch % 2) * 1024 + jj * 128:][:, 0:128],
                                    ident[:], start=True, stop=True)
                            ins.then_inc(S["t_tq"], 1)

                    @b3.vector
                    def _(v):
                        for ch in range(NCH):
                            v.wait_ge(S["d_w1"], 16 * (ch + 1))
                            v.reduce_sum(absc[:, ch:ch + 1],
                                         wst[:, (ch % 2) * 1024:][:, 0:1024],
                                         axis=AX.X,
                                         apply_absolute_value=True,
                                         ).then_inc(S["v_p1"], 1)
                        v.wait_ge(S["t_p1"], 1)
                        v.reduce_sum(swsc[0:1, 4:5], ppa[0:1, 0:NHC], axis=AX.X)
                        v.reduce_sum(swsc[0:1, 5:6], ppa[0:1, NHC:NCH], axis=AX.X)
                        vpad(v)
                        vpad(v)
                        v.tensor_scalar(swsc[0:1, 4:6], swsc[0:1, 4:6],
                                        1.0 / (DH * D), 1e-5, ALU.mult, ALU.max)
                        vpad(v)
                        v.reciprocal(swsc[0:1, 6:8], swsc[0:1, 4:6])
                        vpad(v)
                        vpad(v)
                        v.tensor_copy(swsc[0:1, 0:1], swsc[0:1, 6:7])
                        v.tensor_copy(swsc[0:1, 1:2], swsc[0:1, 4:5])
                        v.tensor_copy(swsc[0:1, 2:3], swsc[0:1, 7:8])
                        v.tensor_copy(swsc[0:1, 3:4], swsc[0:1, 5:6])
                        vpad(v)
                        v.engine_nop().then_inc(S["v_sw"], 1)
                        v.wait_ge(S["t_sw"], 1)
                        v.tensor_copy(swv[:], psw[:, 4:8])
                        vpad(v)
                        v.tensor_scalar(cwsc[:], cw[:], swv[:, 1:2], None,
                                        ALU.mult).then_inc(S["v_sw"], 1)
                        for ch in range(NCH + 1):
                            if ch < NCH:
                                v.wait_ge(S["d_w2"], 16 * (ch + 1))
                                wsrc = wst2[:, (ch % 2) * 1024:][:, 0:1024]
                                scol = 0 if chunks[ch][0] == "w1" else 2
                                v.tensor_scalar(wtmp[:], wsrc,
                                                swv[:, scol:scol + 1],
                                                CLIP, ALU.mult, ALU.min)
                                v.tensor_scalar(wtmp[:], wtmp[:], -CLIP, MAGIC,
                                                ALU.max, ALU.add)
                                v.tensor_scalar(tqn[:, (ch % 2) * 1024:][:, 0:1024],
                                                wtmp[:], MAGIC, None,
                                                ALU.subtract).then_inc(S["v_q2"], 1)
                            if ch >= 1:
                                cp = ch - 1
                                v.wait_ge(S["t_tq"], cp + 1)
                                kind, meta = chunks[cp]
                                if kind == "w1":
                                    cc = meta
                                    dst = tq1T[:].rearrange(
                                        "p (d t) -> p d t", d=NDC
                                    )[:, :, cc * 128:(cc + 1) * 128]
                                else:
                                    cc, g2 = meta
                                    dst = tq2T[:, g2 * 8 * D:][:, 0:8 * D].rearrange(
                                        "p (h t) -> p h t", h=8
                                    )[:, :, cc * 128:(cc + 1) * 128]
                                v.tensor_copy(
                                    dst,
                                    pst[cp % 2][:].rearrange("p (j t) -> p j t", j=8),
                                ).then_inc(S["v_tqe"], 1)

            if maxphase < 4:
                if debug:
                    with nc.Block("dbgB3") as dbgb3:
                        @dbgb3.sync
                        def _(sync):
                            sync.dma_start(out=dt1_h, in_=tq1T[:]).then_inc(S["d_dbg"], 16)
                            sync.dma_start(out=dt2_h, in_=tq2T[:]).then_inc(S["d_dbg"], 16)
                            sync.dma_start(out=dsw_h, in_=swv[:]).then_inc(S["d_dbg"], 16)
                            sync.wait_ge(S["d_dbg"], 48 + 80)
                return nc
            hx = eb(nc.sbuf_tensor("hx", [128, NHC * HEXW], BF16))
            cbuf = eb(nc.sbuf_tensor("cbuf", [128, 2 * 516], BF16))
            ftmp = eb(nc.sbuf_tensor("ftmp", [128, 2 * 516], F32))
            accb = eb(nc.sbuf_tensor("accb", [128, 513], BF16))
            accm = eb(nc.sbuf_tensor("accm", [128, 513], BF16))
            s2b = eb(nc.sbuf_tensor("s2b", [128, 513], F32))
            mxb = eb(nc.sbuf_tensor("mxb", [128, 32], F32))
            rowsb = eb(nc.sbuf_tensor("rowsb", [1, 640], F32))
            xqt = eb(nc.sbuf_tensor("xqt", [128, 2 * NDC * 512], BF16))
            x2i = eb(nc.sbuf_tensor("x2i", [128, 2 * D], F32))
            outy = eb(nc.sbuf_tensor("outy", [128, D], F32))
            outst = eb(nc.sbuf_tensor("outst", [128, 2 * D], F32))
            hcar = eb(nc.sbuf_tensor("hcar", [128, 2 * NHC], BF16))
            ps1 = [eb(nc.psum_tensor(f"ps1{i}", [128, 512], F32)) for i in range(3)]
            ps2 = eb(nc.psum_tensor("ps2", [128, 1024], F32))
            ps3 = eb(nc.psum_tensor("ps3", [128, 1024], F32))

            tstart = [0]
            for td in TILES:
                tstart.append(tstart[-1] + len(td["slices"]))

            with nc.Block("b4") as b4:
                @b4.sync
                def _(sync):
                    def xqt_load(ti):
                        if ti >= 2:
                            sync.wait_ge(S["t_f1"], NHC * (ti - 1))
                        for d in range(NDC):
                            sync.dma_start(
                                out=xqt[:, (ti % 2) * NDC * 512 + d * 512:][:, 0:512],
                                in_=xqh_h[d, :, ti * FT:(ti + 1) * FT],
                            ).then_inc(S["d_xt"], 16)
                    xqt_load(0)
                    xqt_load(1)
                    for gi, (ti, si, c0, wd) in enumerate(GSL):
                        if si == 0 and ti >= 2:
                            xqt_load(ti)
                        r0 = TILES[ti]["tok0"] + c0
                        if gi >= 2:
                            sync.wait_ge(S["v_l4"], gi - 1)
                        sync.dma_start(
                            out=x2i[:, (gi % 2) * D:][:, 0:D][0:wd, :],
                            in_=x2_h[r0:r0 + wd, :],
                        ).then_inc(S["d_x2i"], 16)
                        if gi >= 2:
                            pi, psi, pc0, pwd = GSL[gi - 2]
                            pr0 = TILES[pi]["tok0"] + pc0
                            sync.wait_ge(S["v_l4"], gi - 1)
                            sync.dma_start(
                                out=out_h[pr0:pr0 + pwd, :],
                                in_=outst[:, ((gi - 2) % 2) * D:][:, 0:D][0:pwd, :],
                            ).then_inc(S["d_out"], 16)
                    for gi in (NSL - 2, NSL - 1):
                        ti, si, c0, wd = GSL[gi]
                        r0 = TILES[ti]["tok0"] + c0
                        sync.wait_ge(S["v_l4"], gi + 1)
                        sync.dma_start(
                            out=out_h[r0:r0 + wd, :],
                            in_=outst[:, (gi % 2) * D:][:, 0:D][0:wd, :],
                        ).then_inc(S["d_out"], 16)
                    sync.wait_ge(S["d_out"], 16 * NSL)

                @b4.gpsimd
                def _(g):
                    g.memset(hcar[:], 0.0)
                    g.memset(hx[:].rearrange("p (c w) -> p c w", c=NHC)[:, :, 514:515],
                             0.0)
                    g.memset(mxb[:], 1.0)
                    g.memset(accb[:], 0.0)
                    g.engine_nop().then_inc(S["g_i4"], 1)

                @b4.tensor
                def _(t):
                    for ti, td in enumerate(TILES):
                        nsg = len(td["slices"])
                        for c in range(NHC):
                            k = ti * NHC + c
                            if c == 0:
                                t.wait_ge(S["d_xt"], 16 * NDC * (ti + 1))
                                if ti == 0:
                                    t.wait_ge(S["g_i4"], 1)
                            if k >= 3:
                                t.wait_ge(S["v_ev"], k - 2)
                            ins = None
                            for d in range(NDC):
                                ins = t.matmul(
                                    ps1[c % 3][:],
                                    tq1T[:, d * DH + c * 128:][:, 0:128],
                                    xqt[:, (ti % 2) * NDC * 512 + d * 512:][:, 0:512],
                                    start=(d == 0), stop=(d == NDC - 1))
                            ins.then_inc(S["t_f1"], 1)
                        t.wait_ge(S["g_am"], NHC * (ti + 1))
                        ins = None
                        for gl, (c0, wd) in enumerate(td["slices"]):
                            ins = t.matmul(ps3[0:wd, 128 * gl:][:, 0:128],
                                           accb[:, c0:c0 + wd], ident[:],
                                           start=True, stop=True)
                        ins.then_inc(S["t_s2"], 1)
                        t.wait_ge(S["s_s2b"], ti + 1)
                        ins = None
                        for gl, (c0, wd) in enumerate(td["slices"]):
                            ins = t.matmul(ps3[0:1, 128 * gl:][:, 0:128],
                                           mxb[0:wd, 24 + gl:25 + gl],
                                           identf[0:wd, :],
                                           start=True, stop=True)
                        ins.then_inc(S["t_s2"], 1)
                        t.wait_ge(S["v_s2"], 3 * ti + 2)
                        ins = None
                        for gl, (c0, wd) in enumerate(td["slices"]):
                            ins = t.matmul(ps3[:, 128 * gl:][:, 0:wd],
                                           onesrow[:],
                                           rowsb[0:1, 128 * gl:][:, 0:wd],
                                           start=True, stop=True)
                        ins.then_inc(S["t_s2"], 1)
                        t.wait_ge(S["s_rd"], NHC * (ti + 1))
                        for si, (c0, wd) in enumerate(td["slices"]):
                            gi = tstart[ti] + si
                            if gi >= 1:
                                t.wait_ge(S["v_f2"], gi)
                            ins = None
                            for c in range(NHC):
                                for half in range(2):
                                    ins = t.matmul(
                                        ps2[0:wd, half * 512:][:, 0:512],
                                        hx[:, c * HEXW + 1 + c0:][:, 0:wd],
                                        tq2T[:, c * D + half * 512:][:, 0:512],
                                        start=(c == 0), stop=(c == NHC - 1))
                            ins.then_inc(S["t_f2"], 1)

                @b4.vector
                def _(v):
                    for ti, td in enumerate(TILES):
                        ccmin, ccmax = td["ccmin"], td["ccmax"]
                        ccw = ccmax - ccmin
                        nsg = len(td["slices"])
                        for c in range(NHC):
                            k = ti * NHC + c
                            v.wait_ge(S["t_f1"], k + 1)
                            if k >= 2:
                                v.wait_ge(S["s_si"], k - 1)
                            if c == 0 and ti >= 1:
                                v.wait_ge(S["t_f2"], tstart[ti])
                                v.wait_ge(S["s_rd"], NHC * ti)
                            v.tensor_copy(hx[:, c * HEXW:][:, 0:2],
                                          hcar[:, 2 * c:][:, 0:2])
                            v.tensor_copy(hcar[:, 2 * c:][:, 0:2],
                                          ps1[c % 3][:, 510:512])
                            v.tensor_copy(hx[:, c * HEXW + 2:][:, 0:512],
                                          ps1[c % 3][:]).then_inc(S["v_ev"], 1)
                            cbs = cbuf[:, (k % 2) * 516:][:, 0:ccw]
                            v.tensor_scalar(cbs,
                                            hx[:, c * HEXW + ccmin + 1:][:, 0:ccw],
                                            cwsc[:, 3 * c + 1:3 * c + 2],
                                            cbv_[:, c:c + 1], ALU.mult, ALU.add)
                            v.scalar_tensor_tensor(cbs,
                                                   hx[:, c * HEXW + ccmin:][:, 0:ccw],
                                                   cwsc[:, 3 * c:3 * c + 1], cbs,
                                                   ALU.mult, ALU.add)
                            v.scalar_tensor_tensor(cbs,
                                                   hx[:, c * HEXW + ccmin + 2:][:, 0:ccw],
                                                   cwsc[:, 3 * c + 2:3 * c + 3], cbs,
                                                   ALU.mult, ALU.add).then_inc(S["v_cv"], 1)
                            if c >= 1:
                                cm = c - 1
                                v.wait_ge(S["s_si"], ti * NHC + cm + 1)
                                hsm = hx[:, cm * HEXW + ccmin + 1:][:, 0:ccw]
                                acs = accb[:, ccmin:][:, 0:ccw]
                                acm = accm[:, ccmin:][:, 0:ccw]
                                if cm == 0:
                                    if ti >= 1:
                                        v.wait_ge(S["t_s2"], 3 * ti - 2)
                                    v.tensor_copy(acs, hsm)
                                    v.tensor_copy(acm, hsm).then_inc(S["g_am"], 1)
                                else:
                                    v.tensor_tensor(acs, acs, hsm, ALU.max)
                                    v.tensor_tensor(acm, acm, hsm,
                                                    ALU.min).then_inc(S["g_am"], 1)
                        cm = NHC - 1
                        v.wait_ge(S["s_si"], ti * NHC + cm + 1)
                        hsm = hx[:, cm * HEXW + ccmin + 1:][:, 0:ccw]
                        acs = accb[:, ccmin:][:, 0:ccw]
                        acm = accm[:, ccmin:][:, 0:ccw]
                        v.tensor_tensor(acs, acs, hsm, ALU.max)
                        v.tensor_tensor(acm, acm, hsm, ALU.min)
                        v.tensor_scalar(acm, acm, -1.0, None, ALU.mult)
                        v.tensor_tensor(acs, acs, acm,
                                        ALU.max).then_inc(S["g_am"], 1)
                        v.wait_ge(S["t_s2"], 3 * ti + 1)
                        for gl, (c0, wd) in enumerate(td["slices"]):
                            v.reduce_max(mxb[0:wd, gl:gl + 1],
                                         ps3[0:wd, 128 * gl:][:, 0:128], axis=AX.X)
                        v.engine_nop().then_inc(S["v_s2"], 1)
                        v.wait_ge(S["s_s2a"], ti + 1)
                        v.reciprocal(st1[:, 32:32 + nsg],
                                     mxb[:, 16:16 + nsg]).then_inc(S["v_s2b"], 1)
                        v.wait_ge(S["t_s2"], 3 * ti + 2)
                        v.tensor_copy(rowsb[0:1, 0:128 * nsg],
                                      ps3[0:1, 0:128 * nsg]).then_inc(S["v_s2"], 1)
                        v.wait_ge(S["t_s2"], 3 * ti + 3)
                        ins = None
                        for gl, (c0, wd) in enumerate(td["slices"]):
                            ins = v.tensor_copy(s2b[:, c0:c0 + wd],
                                                ps3[:, 128 * gl:][:, 0:wd])
                        ins.then_inc(S["v_s2"], 1)
                        for c in range(NHC):
                            k = ti * NHC + c
                            if k >= 2:
                                v.wait_ge(S["s_rd"], k - 1)
                            v.tensor_tensor(
                                ftmp[:, (c % 2) * 516:][:, 0:ccw],
                                hx[:, c * HEXW + ccmin + 1:][:, 0:ccw],
                                s2b[:, ccmin:][:, 0:ccw],
                                ALU.mult).then_inc(S["v_rm"], 1)
                        for si, (c0, wd) in enumerate(td["slices"]):
                            gi = tstart[ti] + si
                            b = 28 + 4 * (gi % 2)
                            v.wait_ge(S["t_f2"], gi + 1)
                            v.wait_ge(S["d_x2i"], 16 * (gi + 1))
                            if gi >= 1:
                                v.wait_ge(S["s_l4"], gi)
                            for half in range(2):
                                v.tensor_scalar(
                                    outy[0:wd, half * 512:][:, 0:512],
                                    ps2[0:wd, half * 512:][:, 0:512],
                                    mxb[0:wd, 8 + si:9 + si], None, ALU.mult)
                            oy = outy[0:wd, :]
                            ost = outst[0:wd, (gi % 2) * D:][:, 0:D]
                            ln_stats_v(v, b, oy, sub=slice(0, wd)).then_inc(S["v_f2"], 1)
                            v.wait_ge(S["s_l4"], gi + 1)
                            v.tensor_tensor(ost, ost,
                                            x2i[0:wd, (gi % 2) * D:][:, 0:D],
                                            ALU.add).then_inc(S["v_l4"], 1)

                @b4.scalar
                def _(s):
                    for ti, td in enumerate(TILES):
                        ccmin, ccmax = td["ccmin"], td["ccmax"]
                        ccw = ccmax - ccmin
                        nsg = len(td["slices"])
                        for c in range(NHC):
                            k = ti * NHC + c
                            s.wait_ge(S["v_cv"], k + 1)
                            s.activation(hx[:, c * HEXW + ccmin + 1:][:, 0:ccw],
                                         cbuf[:, (k % 2) * 516:][:, 0:ccw],
                                         AF.Silu).then_inc(S["s_si"], 1)
                        s.wait_ge(S["v_s2"], 3 * ti + 1)
                        if ti >= 1:
                            s.wait_ge(S["v_f2"], tstart[ti])
                        s.activation(mxb[:, 16:16 + nsg], mxb[:, 0:nsg],
                                     AF.Relu, bias=c_negeps[:])
                        s.activation(mxb[:, 16:16 + nsg], mxb[:, 16:16 + nsg],
                                     AF.Identity, bias=c_eps[:])
                        s.activation(mxb[:, 8:8 + nsg], mxb[:, 16:16 + nsg],
                                     AF.Identity, scale=swv[:, 3:4])
                        s.activation(mxb[:, 8:8 + nsg], mxb[:, 8:8 + nsg],
                                     AF.Identity,
                                     scale=1.0 / 127.0).then_inc(S["s_s2a"], 1)
                        s.wait_ge(S["v_s2b"], ti + 1)
                        s.activation(mxb[:, 24:24 + nsg], st1[:, 32:32 + nsg],
                                     AF.Identity, scale=127.0).then_inc(S["s_s2b"], 1)
                        for c in range(NHC):
                            k = ti * NHC + c
                            s.wait_ge(S["v_rm"], k + 1)
                            fts = ftmp[:, (c % 2) * 516:][:, 0:ccw]
                            s.activation(fts, fts, AF.Identity, bias=c_mg[:])
                            s.activation(hx[:, c * HEXW + ccmin + 1:][:, 0:ccw],
                                         fts, AF.Identity,
                                         bias=c_nmg[:]).then_inc(S["s_rd"], 1)
                        for si, (c0, wd) in enumerate(td["slices"]):
                            gi = tstart[ti] + si
                            b = 28 + 4 * (gi % 2)
                            s.wait_ge(S["v_f2"], gi + 1)
                            if gi >= 2:
                                s.wait_ge(S["d_out"], 16 * (gi - 1))
                            ln_rstd_s(s, b, sub=slice(0, wd))
                            s.activation(jkA[:, 0:256], jkB[:, 0:256], AF.Identity)
                            s.activation(jkB[:, 0:256], jkA[:, 0:256], AF.Identity)
                            s.activation(outst[0:wd, (gi % 2) * D:][:, 0:D],
                                         outy[0:wd, :], AF.Identity,
                                         scale=stS[0:wd, b + 2:b + 3],
                                         bias=stS[0:wd, b + 3:b + 4]).then_inc(S["s_l4"], 1)

            if debug:
                with nc.Block("dbgB") as dbgb:
                    @dbgb.sync
                    def _(sync):
                        sync.dma_start(out=dt1_h, in_=tq1T[:]).then_inc(S["d_dbg"], 16)
                        sync.dma_start(out=dt2_h, in_=tq2T[:]).then_inc(S["d_dbg"], 16)
                        sync.dma_start(out=dsw_h, in_=swv[:]).then_inc(S["d_dbg"], 16)
                        sync.dma_start(out=dhx_h, in_=hx[:]).then_inc(S["d_dbg"], 16)
                        sync.wait_ge(S["d_dbg"], 48 + 96)

    return nc


def kernel(L0_attn_x, L0_attn_w_qkv, L0_ffn_w1, L0_ffn_conv_w, L0_ffn_conv_b,
           L0_ffn_w2, _debug=False, _maxphase=4):
    x = np.ascontiguousarray(np.asarray(L0_attn_x, dtype=np.float32))
    wqkv = np.ascontiguousarray(np.asarray(L0_attn_w_qkv, dtype=np.float32))
    w1 = np.ascontiguousarray(np.asarray(L0_ffn_w1, dtype=np.float32))
    convw = np.ascontiguousarray(
        np.asarray(L0_ffn_conv_w, dtype=np.float32).reshape(DH, 3))
    convb = np.ascontiguousarray(
        np.asarray(L0_ffn_conv_b, dtype=np.float32).reshape(DH, 1))
    w2 = np.ascontiguousarray(np.asarray(L0_ffn_w2, dtype=np.float32))

    nc = build_nc(debug=_debug, maxphase=_maxphase)
    in_maps = [{"x": x[i], "wqkv": wqkv, "w1": w1, "convw": convw,
                "convb": convb, "w2": w2} for i in range(8)]
    res = run_bass_kernel_spmd(nc, in_maps, core_ids=list(range(8)))
    out = np.stack([np.asarray(r["out"]) for r in res.results], axis=0)
    if _debug:
        return out, res
    return out


# version 05
# speedup vs baseline: 2.2534x; 1.0554x over previous
import numpy as np
import concourse.bass as bass
import concourse.mybir as mybir
from concourse.bass_utils import run_bass_kernel_spmd
from contextlib import ExitStack

F32 = mybir.dt.float32
BF16 = mybir.dt.bfloat16
I32 = mybir.dt.int32
AF = mybir.ActivationFunctionType
ALU = mybir.AluOpType
AX = mybir.AxisListType

T = 2048
D = 1024
H = 16
E = 64
W = 128
NW = T // W
DH = 4096
NDC = D // 128
NHC = DH // 128
TGS = 512
NTG = T // TGS
FT = 512
NFT = T // FT
QKVN = 3 * D
EXP_SHIFT = -20.0
HEXW = 515


def _ffn_tiles():
    tiles = []
    for i in range(NFT):
        tok0 = FT * i - 1
        if i == 0:
            ccmin, ccmax = 1, 512
            slices = [(1, 128), (129, 128), (257, 128), (385, 127)]
        elif i < NFT - 1:
            ccmin, ccmax = 0, 512
            slices = [(0, 128), (128, 128), (256, 128), (384, 128)]
        else:
            ccmin, ccmax = 0, 513
            slices = [(0, 128), (128, 128), (256, 128), (384, 128), (512, 1)]
        tiles.append(dict(i=i, tok0=tok0, ccmin=ccmin, ccmax=ccmax,
                          slices=slices))
    return tiles


TILES = _ffn_tiles()
GSL = []
for _td in TILES:
    for _si, (_c0, _wd) in enumerate(_td["slices"]):
        GSL.append((_td["i"], _si, _c0, _wd))
NSL = len(GSL)


def build_nc(debug=False, maxphase=4):
    nc = bass.Bass("TRN2", target_bir_lowering=False,
                   dynamic_dma_scratch_size=4096,
                   detect_race_conditions=False)

    x_h = nc.dram_tensor("x", [T, D], F32, kind="ExternalInput").ap()
    wqkv_h = nc.dram_tensor("wqkv", [D, QKVN], F32, kind="ExternalInput").ap()
    w1_h = nc.dram_tensor("w1", [DH, D], F32, kind="ExternalInput").ap()
    convw_h = nc.dram_tensor("convw", [DH, 3], F32, kind="ExternalInput").ap()
    convb_h = nc.dram_tensor("convb", [DH, 1], F32, kind="ExternalInput").ap()
    w2_h = nc.dram_tensor("w2", [D, DH], F32, kind="ExternalInput").ap()
    out_h = nc.dram_tensor("out", [T, D], F32, kind="ExternalOutput").ap()

    dbgk = "ExternalOutput" if debug else "Internal"
    qh_h = nc.dram_tensor("qh", [NDC, 128, T], BF16, kind=dbgk).ap()
    x2_h = nc.dram_tensor("x2s", [T, D], F32, kind=dbgk).ap()
    xqh_h = nc.dram_tensor("xqs", [NDC, 128, T], BF16, kind=dbgk).ap()
    if debug:
        dk_h = nc.dram_tensor("dk", [128, NDC * T], BF16, kind="ExternalOutput").ap()
        dxnT_h = nc.dram_tensor("dxnT", [128, NDC * TGS], BF16, kind="ExternalOutput").ap()
        dst1_h = nc.dram_tensor("dst1", [128, 48], F32, kind="ExternalOutput").ap()
        dxnbf_h = nc.dram_tensor("dxnbf", [128, 2 * D], BF16, kind="ExternalOutput").ap()
        dv_h = nc.dram_tensor("dv", [128, NW * D], BF16, kind="ExternalOutput").ap()
        dt1_h = nc.dram_tensor("dt1", [128, NDC * DH], BF16, kind="ExternalOutput").ap()
        dt2_h = nc.dram_tensor("dt2", [128, NHC * D], BF16, kind="ExternalOutput").ap()
        dsw_h = nc.dram_tensor("dsw", [128, 4], F32, kind="ExternalOutput").ap()
        dhx_h = nc.dram_tensor("dhx", [128, NHC * HEXW], BF16, kind="ExternalOutput").ap()

    ctx = ExitStack()
    with ctx:
        e = ctx.enter_context

        ident = e(nc.sbuf_tensor("ident", [128, 128], BF16))
        identf = e(nc.sbuf_tensor("identf", [128, 128], F32))
        onesrow = e(nc.sbuf_tensor("onesrow", [1, 128], F32))
        onescol = e(nc.sbuf_tensor("onescol", [128, 1], F32))
        c_m20 = e(nc.sbuf_tensor("c_m20", [128, 1], F32))
        c_eps = e(nc.sbuf_tensor("c_eps", [128, 1], F32))
        c_negeps = e(nc.sbuf_tensor("c_negeps", [128, 1], F32))
        c_mg = e(nc.sbuf_tensor("c_mg", [128, 1], F32))
        c_nmg = e(nc.sbuf_tensor("c_nmg", [128, 1], F32))
        cw = e(nc.sbuf_tensor("cw", [128, 3 * NHC], F32))
        cbv_ = e(nc.sbuf_tensor("cbv", [128, NHC], F32))
        cwsc = e(nc.sbuf_tensor("cwsc", [128, 3 * NHC], F32))
        swv = e(nc.sbuf_tensor("swv", [128, 4], F32))
        sums = e(nc.sbuf_tensor("sums", [128, 32], F32))
        sinv = e(nc.sbuf_tensor("sinv", [128, 16], F32))
        st1 = e(nc.sbuf_tensor("st1", [128, 48], F32))
        wacc = e(nc.sbuf_tensor("wacc", [128, 2], F32))
        sqb = e(nc.sbuf_tensor("sqb", [128, D], F32))
        jkA = e(nc.sbuf_tensor("jkA", [128, 256], F32))
        jkB = e(nc.sbuf_tensor("jkB", [128, 256], F32))
        stS = e(nc.sbuf_tensor("stS", [128, 48], F32))

        S = {}
        for name in ["d_wq", "d_cw", "d_dbg",
                     "d_x", "v_st", "s_xn", "t_tr", "v_xnT",
                     "t_qk", "v_qk", "t_v", "v_v", "d_qh",
                     "d_qw", "d_xr", "d_x2", "d_xq", "d_xqt",
                     "t_sc", "s_exp", "t_tr2", "v_tr", "t_av",
                     "v_ep", "s_e1", "v_e2", "s_e2", "v_e3", "s_r1",
                     "v_e3a", "s_e3a", "v_e3b", "s_e3b",
                     "v_s2b", "s_s2a", "s_s2b", "g_e3",
                     "d_w1", "g_p1", "t_p1", "t_sw", "v_sw", "d_w2", "v_q2", "d_tq",
                     "d_xt", "d_x2i", "t_f1", "v_ev", "v_cv", "s_si", "g_am",
                     "t_s2", "v_s2", "g_rd", "t_f2", "v_f2", "s_l4", "v_l4",
                     "d_out", "g_i4", "v_p1", "t_tq", "v_tqe", "v_rm", "s_rd",
                     "d_w1g", "d_w2g"]:
            S[name] = e(nc.semaphore(name))

        def ln_stats_v(v, b, src, sub=None):
            p = sub if sub is not None else slice(0, 128)
            v.scalar_tensor_tensor(sqb[p, :], src, 1.0, src, ALU.bypass, ALU.mult)
            v.reduce_sum(st1[p, b:b + 1], src, axis=AX.X)
            return v.reduce_sum(st1[p, b + 1:b + 2], sqb[p, :], axis=AX.X)

        def ln_rstd_s(s, b, sub=None):
            p = sub if sub is not None else slice(0, 128)
            s.activation(stS[p, b:b + 1], st1[p, b:b + 1], AF.Identity,
                         scale=1.0 / D)
            s.activation(stS[p, b + 1:b + 2], stS[p, b:b + 1], AF.Identity,
                         scale=stS[p, b:b + 1])
            s.activation(stS[p, b + 1:b + 2], stS[p, b + 1:b + 2], AF.Identity,
                         scale=-1.0)
            s.activation(stS[p, b + 1:b + 2], st1[p, b + 1:b + 2], AF.Identity,
                         scale=1.0 / D, bias=stS[p, b + 1:b + 2])
            s.activation(stS[p, b + 2:b + 3], stS[p, b + 1:b + 2], AF.Ln,
                         bias=c_eps[p, :])
            s.activation(stS[p, b + 2:b + 3], stS[p, b + 2:b + 3], AF.Exp,
                         scale=-0.5)
            s.activation(stS[p, b + 3:b + 4], stS[p, b:b + 1], AF.Identity,
                         scale=stS[p, b + 2:b + 3])
            return s.activation(stS[p, b + 3:b + 4], stS[p, b + 3:b + 4],
                                AF.Identity, scale=-1.0)

        def vpad(v):
            v.tensor_copy(jkA[:], jkB[:])
            return v.tensor_copy(jkB[:], jkA[:])

        with nc.Block("b0") as b0:
            @b0.sync
            def _(sync):
                for c in range(NHC):
                    sync.dma_start(
                        out=cw[:, 3 * c:3 * c + 3],
                        in_=convw_h[c * 128:(c + 1) * 128, :],
                    ).then_inc(S["d_cw"], 16)
                    sync.dma_start(
                        out=cbv_[:, c:c + 1],
                        in_=convb_h[c * 128:(c + 1) * 128, :],
                    ).then_inc(S["d_cw"], 16)
                sync.wait_ge(S["d_cw"], 32 * NHC)

            @b0.gpsimd
            def _(g):
                for idt in (ident, identf):
                    g.memset(idt[:], 0.0)
                    g.affine_select(out=idt[:], in_=idt[:],
                                    compare_op=ALU.not_equal, fill=1.0, base=0,
                                    pattern=[[-1, 128]], channel_multiplier=1)
                g.memset(onesrow[:], 1.0)
                g.memset(onescol[:], 1.0)
                g.memset(c_m20[:], EXP_SHIFT)
                g.memset(c_eps[:], 1e-5)
                g.memset(c_negeps[:], -1e-5)
                g.memset(c_mg[:], 12582912.0)
                g.memset(c_nmg[:], -12582912.0)
                g.memset(st1[:], 0.0)
                g.memset(stS[:], 0.0)
                g.memset(sqb[:], 0.0)
                g.memset(jkA[:], 0.0)
                g.memset(jkB[:], 0.0)

        with ExitStack() as ctxA:
            ea = ctxA.enter_context
            wqkv = ea(nc.sbuf_tensor("wqkv_sb", [128, NDC * QKVN], BF16))
            xnT = ea(nc.sbuf_tensor("xnT", [128, NDC * TGS], BF16))
            xnbf = ea(nc.sbuf_tensor("xnbf", [128, 2 * D], BF16))
            kf = ea(nc.sbuf_tensor("kf", [128, NDC * T], BF16))
            vt = ea(nc.sbuf_tensor("vt", [128, NW * D], BF16))
            qsb = ea(nc.sbuf_tensor("qsb", [128, 8 * TGS], BF16))
            qwin = ea(nc.sbuf_tensor("qwin", [128, 2 * NDC * 128], BF16))
            xln = ea(nc.sbuf_tensor("xln", [128, 3 * D], F32))
            xres = ea(nc.sbuf_tensor("xres", [128, 2 * D], F32))
            attno = ea(nc.sbuf_tensor("attno", [128, 2 * D], F32))
            wscr = ea(nc.sbuf_tensor("wscr", [128, 2 * D], F32))
            wscrB = ea(nc.sbuf_tensor("wscrB", [128, 2 * D], F32))
            x2b = ea(nc.sbuf_tensor("x2b", [128, 3 * D], F32))
            xqf = ea(nc.sbuf_tensor("xqf", [128, 1024], F32))
            xqbf = ea(nc.sbuf_tensor("xqbf", [128, 2 * D], BF16))
            xqtw = ea(nc.sbuf_tensor("xqtw", [128, 2 * NDC * 128], BF16,
                                     is_transpose_dest=True))
            probs = ea(nc.sbuf_tensor("probs", [128, 2 * 768], BF16))
            probsT = ea(nc.sbuf_tensor("probsT", [128, 2 * 768], BF16))

            with nc.Block("b0b") as b0b:
                @b0b.gpsimd
                def _(g):
                    for d in range(NDC):
                        g.dma_start(
                            out=wqkv[:, d * QKVN:(d + 1) * QKVN],
                            in_=wqkv_h[d * 128:(d + 1) * 128, :],
                        ).then_inc(S["d_wq"], 16)

            with ExitStack() as ctx1:
                e1 = ctx1.enter_context
                ptr = e1(nc.psum_tensor("ptr", [128, 1024], F32))
                psA = [e1(nc.psum_tensor(f"psA{i}", [128, 512], F32))
                       for i in (0, 1)]
                pv = [e1(nc.psum_tensor(f"pv{i}", [128, 1024], F32))
                      for i in (0, 1)]

                with nc.Block("b1") as b1:
                    @b1.sync
                    def _(sync):
                        for c in range(NW + 1):
                            if c < NW:
                                if c >= 3:
                                    sync.wait_ge(S["s_xn"], c - 2)
                                sync.dma_start(
                                    out=xln[:, (c % 3) * D:][:, 0:D],
                                    in_=x_h[c * 128:(c + 1) * 128, :],
                                ).then_inc(S["d_x"], 16)
                            if c % 4 == 0 and c >= 4:
                                tg = c // 4
                                for F in range(8):
                                    sync.wait_ge(S["v_qk"], (tg - 1) * 16 + F + 1)
                                    sync.dma_start(
                                        out=qh_h[F, :, (tg - 1) * TGS:tg * TGS],
                                        in_=qsb[:, F * TGS:][:, 0:TGS],
                                    ).then_inc(S["d_qh"], 16)
                        sync.wait_ge(S["d_qh"], 16 * NTG * 8)

                    @b1.vector
                    def _(v):
                        for c in range(NW + 1):
                            if c < NW:
                                b = 4 * (c % 2)
                                v.wait_ge(S["d_x"], 16 * (c + 1))
                                xc = xln[:, (c % 3) * D:][:, 0:D]
                                ln_stats_v(v, b, xc).then_inc(S["v_st"], 1)
                            if c < 1:
                                continue
                            cp = c - 1
                            tg, l = cp // 4, cp % 4
                            v.wait_ge(S["t_tr"], cp + 1)
                            if l == 0 and tg > 0:
                                v.wait_ge(S["t_qk"], 16 * tg)
                                v.wait_ge(S["t_v"], 4 * tg)
                            v.tensor_copy(
                                xnT[:].rearrange("p (d t) -> p d t", d=NDC)[:, :, l * 128:(l + 1) * 128],
                                ptr[:].rearrange("p (d t) -> p d t", d=NDC),
                            ).then_inc(S["v_xnT"], 1)
                            if l == 3:
                                for F in range(16):
                                    gidx = tg * 16 + F
                                    v.wait_ge(S["t_qk"], gidx + 1)
                                    if F < 8:
                                        if tg >= 1:
                                            v.wait_ge(S["d_qh"],
                                                      16 * (8 * (tg - 1) + F + 1))
                                        v.tensor_scalar(
                                            qsb[:, F * TGS:][:, 0:TGS],
                                            psA[F % 2][:], 0.125, None, ALU.mult,
                                        ).then_inc(S["v_qk"], 1)
                                    else:
                                        v.tensor_copy(
                                            kf[:, (F - 8) * T + tg * TGS:][:, 0:TGS],
                                            psA[F % 2][:],
                                        ).then_inc(S["v_qk"], 1)
                                for sl in range(4):
                                    gidx = tg * 4 + sl
                                    v.wait_ge(S["t_v"], gidx + 1)
                                    v.tensor_copy(
                                        vt[:, gidx * D:(gidx + 1) * D],
                                        pv[sl % 2][:],
                                    ).then_inc(S["v_v"], 1)

                    @b1.scalar
                    def _(s):
                        for c in range(NW + 1):
                            if c < NW:
                                b = 4 * (c % 2)
                                s.wait_ge(S["v_st"], c + 1)
                                ln_rstd_s(s, b)
                            if c >= 1:
                                cp = c - 1
                                b = 4 * (cp % 2)
                                if cp >= 2:
                                    s.wait_ge(S["t_tr"], cp - 1)
                                s.activation(xnbf[:, (cp % 2) * D:][:, 0:D],
                                             xln[:, (cp % 3) * D:][:, 0:D],
                                             AF.Identity, scale=stS[:, b + 2:b + 3],
                                             bias=stS[:, b + 3:b + 4]).then_inc(S["s_xn"], 1)

                    @b1.tensor
                    def _(t):
                        for tg in range(NTG):
                            for l in range(4):
                                c = tg * 4 + l
                                t.wait_ge(S["s_xn"], c + 1)
                                if c > 0:
                                    t.wait_ge(S["v_xnT"], c)
                                ins = None
                                for d in range(NDC):
                                    ins = t.matmul(
                                        ptr[:, d * 128:(d + 1) * 128],
                                        xnbf[:, (c % 2) * D + d * 128:][:, 0:128],
                                        ident[:], start=True, stop=True)
                                ins.then_inc(S["t_tr"], 1)
                            t.wait_ge(S["v_xnT"], 4 * (tg + 1))
                            if tg == 0:
                                t.wait_ge(S["d_wq"], 16 * NDC)
                            for F in range(16):
                                gidx = tg * 16 + F
                                if gidx >= 2:
                                    t.wait_ge(S["v_qk"], gidx - 1)
                                ins = None
                                for d in range(NDC):
                                    ins = t.matmul(
                                        psA[F % 2][:],
                                        wqkv[:, d * QKVN + F * 128:][:, 0:128],
                                        xnT[:, d * TGS:(d + 1) * TGS],
                                        start=(d == 0), stop=(d == NDC - 1))
                                ins.then_inc(S["t_qk"], 1)
                            for sl in range(4):
                                gidx = tg * 4 + sl
                                if gidx >= 2:
                                    t.wait_ge(S["v_v"], gidx - 1)
                                ins = None
                                for d in range(NDC):
                                    for hh in range(2):
                                        ins = t.matmul(
                                            pv[sl % 2][:, hh * 512:][:, 0:512],
                                            xnT[:, d * TGS + sl * 128:][:, 0:128],
                                            wqkv[:, d * QKVN + 2 * D + hh * 512:][:, 0:512],
                                            start=(d == 0), stop=(d == NDC - 1))
                                ins.then_inc(S["t_v"], 1)

            if debug:
                with nc.Block("dbgB1") as dbgb1:
                    @dbgb1.sync
                    def _(sync):
                        sync.dma_start(out=dxnT_h, in_=xnT[:]).then_inc(S["d_dbg"], 16)
                        sync.dma_start(out=dst1_h, in_=st1[:]).then_inc(S["d_dbg"], 16)
                        sync.dma_start(out=dxnbf_h, in_=xnbf[:]).then_inc(S["d_dbg"], 16)
                        sync.wait_ge(S["d_dbg"], 48)

            if maxphase < 2:
                return nc
            wins = []
            for n in range(NW):
                cb0 = max(n - 1, 0)
                cb1 = min(n + 1, NW - 1)
                wins.append((cb0, cb1 - cb0 + 1))

            with ExitStack() as ctx2:
                e2 = ctx2.enter_context
                scp = [e2(nc.psum_tensor(f"scp{i}", [128, 1024], F32))
                       for i in (0, 1)]
                ptp = e2(nc.psum_tensor("ptp", [128, 1024], F32))
                avp = e2(nc.psum_tensor("avp", [128, 1024], F32))

                with nc.Block("b2") as b2:
                    @b2.sync
                    def _(sync):
                        for w in range(NW + 2):
                            if w < NW:
                                n = w
                                if n >= 2:
                                    sync.wait_ge(S["t_sc"], 8 * (n - 1))
                                for d in range(NDC):
                                    sync.dma_start(
                                        out=qwin[:, (n % 2) * NDC * 128 + d * 128:][:, 0:128],
                                        in_=qh_h[d, :, n * 128:(n + 1) * 128],
                                    ).then_inc(S["d_qw"], 16)
                                if n >= 2:
                                    sync.wait_ge(S["v_e2"], n - 1)
                                sync.dma_start(
                                    out=xres[:, (n % 2) * D:][:, 0:D],
                                    in_=x_h[n * 128:(n + 1) * 128, :],
                                ).then_inc(S["d_xr"], 16)
                            if 1 <= w <= NW:
                                n = w - 1
                                sync.wait_ge(S["v_e2"], w)
                                sync.dma_start(
                                    out=x2_h[n * 128:(n + 1) * 128, :],
                                    in_=x2b[:, (n % 3) * D:][:, 0:D],
                                ).then_inc(S["d_x2"], 16)
                            if 2 <= w <= NW + 1:
                                n = w - 2
                                sync.wait_ge(S["g_e3"], n + 1)
                                if n >= 2:
                                    sync.wait_ge(S["d_xq"], 16 * NDC * (n - 1))
                                for d in range(NDC):
                                    sync.dma_start_transpose(
                                        xqtw[:, (n % 2) * NDC * 128 + d * 128:][:, 0:128],
                                        xqbf[:, (n % 2) * D + d * 128:][:, 0:128],
                                    ).then_inc(S["d_xqt"], 16)
                                sync.wait_ge(S["d_xqt"], 16 * NDC * (n + 1))
                                for d in range(NDC):
                                    sync.dma_start(
                                        out=xqh_h[d, :, n * 128:(n + 1) * 128],
                                        in_=xqtw[:, (n % 2) * NDC * 128 + d * 128:][:, 0:128],
                                    ).then_inc(S["d_xq"], 16)
                        sync.wait_ge(S["d_x2"], 16 * NW)
                        sync.wait_ge(S["d_xq"], 16 * NDC * NW)

                    @b2.tensor
                    def _(t):
                        NP = 8 * NW
                        for p in range(NP + 1):
                            if p >= 1:
                                q = p - 1
                                n, j = divmod(q, 8)
                                cb0, nch = wins[n]
                                t.wait_ge(S["s_exp"], q + 1)
                                if q >= 1:
                                    t.wait_ge(S["v_tr"], q)
                                ins = None
                                for hh in range(2):
                                    for c in range(nch):
                                        ins = t.matmul(
                                            ptp[:, hh * 512 + c * 128:][:, 0:128],
                                            probs[:, (q % 2) * 768 + hh * 384 + c * 128:][:, 0:128],
                                            ident[:], start=True, stop=True)
                                ins.then_inc(S["t_tr2"], 1)
                                t.wait_ge(S["v_tr"], q + 1)
                                if j == 0:
                                    t.wait_ge(S["v_ep"], n)
                                ins = None
                                for hh in range(2):
                                    hgl = 2 * j + hh
                                    for c in range(nch):
                                        ins = t.matmul(
                                            avp[:, hgl * 64:][:, 0:64],
                                            probsT[:, (q % 2) * 768 + hh * 384 + c * 128:][:, 0:128],
                                            vt[:, (cb0 + c) * D + hgl * 64:][:, 0:64],
                                            start=(c == 0), stop=(c == nch - 1))
                                ins.then_inc(S["t_av"], 1)
                            if p < NP:
                                n, j = divmod(p, 8)
                                cb0, nch = wins[n]
                                ctxw = nch * 128
                                if j == 0:
                                    t.wait_ge(S["d_qw"], 16 * NDC * (n + 1))
                                if p >= 2:
                                    t.wait_ge(S["s_exp"], p - 1)
                                ins = None
                                for hh in range(2):
                                    h0 = 64 * hh
                                    ins = t.matmul(
                                        scp[p % 2][:, hh * 512:][:, 0:ctxw],
                                        qwin[h0:h0 + 64,
                                             (n % 2) * NDC * 128 + j * 128:][:, 0:128],
                                        kf[h0:h0 + 64, j * T + cb0 * 128:][:, 0:ctxw],
                                        start=True, stop=True,
                                        tile_position=(h0, 0))
                                ins.then_inc(S["t_sc"], 1)

                    @b2.scalar
                    def _(s):
                        for w in range(NW + 3):
                            if 3 <= w <= NW + 2:
                                n3 = w - 3
                                qb3 = 24 + 4 * (n3 % 2)
                                s.wait_ge(S["v_e3"], n3 + 1)
                                s.activation(xqbf[:, (n3 % 2) * D:][:, 0:D],
                                             xqf[:], AF.Identity,
                                             bias=c_nmg[:])
                                s.activation(xqbf[:, (n3 % 2) * D:][:, 0:D],
                                             xqbf[:, (n3 % 2) * D:][:, 0:D],
                                             AF.Identity,
                                             scale=stS[:, qb3 + 1:qb3 + 2],
                                             ).then_inc(S["g_e3"], 1)
                            if w < NW:
                                n = w
                                cb0, nch = wins[n]
                                ctxw = nch * 128
                                for j in range(8):
                                    p = 8 * n + j
                                    s.wait_ge(S["t_sc"], p + 1)
                                    if p >= 2:
                                        s.wait_ge(S["t_tr2"], p - 1)
                                    if j == 0 and n >= 2:
                                        s.wait_ge(S["v_ep"], n - 1)
                                    ins = None
                                    for hh in range(2):
                                        ins = s.activation(
                                            probs[:, (p % 2) * 768 + hh * 384:][:, 0:ctxw],
                                            scp[p % 2][:, hh * 512:][:, 0:ctxw],
                                            AF.Exp, bias=c_m20[:],
                                            accum_out=sums[:, 16 * (n % 2) + 2 * j + hh:][:, 0:1])
                                    ins.then_inc(S["s_exp"], 1)
                            if 1 <= w <= NW:
                                n1 = w - 1
                                b1c = 8 + 4 * (n1 % 2)
                                s.wait_ge(S["v_ep"], w)
                                ln_rstd_s(s, b1c)
                            if 2 <= w <= NW + 1:
                                n2 = w - 2
                                b2c = 16 + 4 * (n2 % 2)
                                s.wait_ge(S["v_e2"], n2 + 1)
                                if n2 >= 2:
                                    s.wait_ge(S["v_e3"], n2 - 1)
                                ln_rstd_s(s, b2c)
                            if 1 <= w <= NW:
                                n1 = w - 1
                                b1c = 8 + 4 * (n1 % 2)
                                s.activation(wscr[:, (n1 % 2) * D:][:, 0:D],
                                             attno[:, (n1 % 2) * D:][:, 0:D],
                                             AF.Identity, scale=stS[:, b1c + 2:b1c + 3],
                                             bias=stS[:, b1c + 3:b1c + 4]).then_inc(S["s_e1"], 1)
                            if 2 <= w <= NW + 1:
                                n2 = w - 2
                                b2c = 16 + 4 * (n2 % 2)
                                s.activation(wscrB[:, (n2 % 2) * D:][:, 0:D],
                                             x2b[:, (n2 % 3) * D:][:, 0:D],
                                             AF.Identity, scale=stS[:, b2c + 2:b2c + 3],
                                             bias=stS[:, b2c + 3:b2c + 4]).then_inc(S["s_e2"], 1)
                                qb = 24 + 4 * (n2 % 2)
                                qa = 24 + 2 * (n2 % 2)
                                s.wait_ge(S["v_e3a"], n2 + 1)
                                s.activation(stS[:, qb:qb + 1], st1[:, qa:qa + 1],
                                             AF.Relu, bias=c_negeps[:])
                                s.activation(stS[:, qb:qb + 1], stS[:, qb:qb + 1],
                                             AF.Identity, bias=c_eps[:])
                                s.activation(stS[:, qb + 1:qb + 2], stS[:, qb:qb + 1],
                                             AF.Identity,
                                             scale=1.0 / 127.0).then_inc(S["s_e3a"], 1)
                            if 2 <= w <= NW + 1:
                                n2r = w - 2
                                qbr = 24 + 4 * (n2r % 2)
                                qar = 24 + 2 * (n2r % 2)
                                s.wait_ge(S["v_e3b"], n2r + 1)
                                s.activation(stS[:, qbr + 2:qbr + 3],
                                             st1[:, qar + 1:qar + 2],
                                             AF.Identity,
                                             scale=127.0).then_inc(S["s_e3b"], 1)

                    @b2.vector
                    def _(v):
                        for w in range(NW + 2):
                            if w < NW:
                                n = w
                                cb0, nch = wins[n]
                                ctxw = nch * 128
                                for j in range(8):
                                    p = 8 * n + j
                                    v.wait_ge(S["t_tr2"], p + 1)
                                    if p >= 2:
                                        v.wait_ge(S["t_av"], p - 1)
                                    v.tensor_copy(
                                        probsT[:, (p % 2) * 768:][:, 0:768]
                                            .rearrange("p (a b) -> p a b", a=2)[:, :, 0:ctxw],
                                        ptp[:].rearrange("p (a b) -> p a b", a=2)[:, :, 0:ctxw],
                                    ).then_inc(S["v_tr"], 1)
                                b = 8 + 4 * (n % 2)
                                v.wait_ge(S["t_av"], 8 * (n + 1))
                                v.wait_ge(S["s_exp"], 8 * (n + 1))
                                if n >= 2:
                                    v.wait_ge(S["s_e1"], n - 1)
                                v.reciprocal(sinv[:], sums[:, 16 * (n % 2):][:, 0:16])
                                vpad(v)
                                ao = attno[:, (n % 2) * D:][:, 0:D]
                                for hgl in range(H):
                                    v.tensor_scalar(ao[:, hgl * 64:][:, 0:64],
                                                    avp[:, hgl * 64:][:, 0:64],
                                                    sinv[:, hgl:hgl + 1], None,
                                                    ALU.mult)
                                ln_stats_v(v, b, ao).then_inc(S["v_ep"], 1)
                            if 1 <= w <= NW:
                                n = w - 1
                                b = 16 + 4 * (n % 2)
                                v.wait_ge(S["s_e1"], n + 1)
                                v.wait_ge(S["d_xr"], 16 * (n + 1))
                                if n >= 2:
                                    v.wait_ge(S["d_x2"], 16 * (n - 1))
                                xb2 = x2b[:, (n % 3) * D:][:, 0:D]
                                v.tensor_tensor(xb2, wscr[:, (n % 2) * D:][:, 0:D],
                                                xres[:, (n % 2) * D:][:, 0:D],
                                                ALU.add)
                                ln_stats_v(v, b, xb2).then_inc(S["v_e2"], 1)
                            if 2 <= w <= NW + 1:
                                n = w - 2
                                v.wait_ge(S["s_e2"], n + 1)
                                qb = 24 + 4 * (n % 2)
                                qa = 24 + 2 * (n % 2)
                                xn3 = wscrB[:, (n % 2) * D:][:, 0:D]
                                v.reduce_max(st1[:, qa:qa + 1], xn3, axis=AX.X,
                                             apply_absolute_value=True).then_inc(S["v_e3a"], 1)
                                v.wait_ge(S["s_e3a"], n + 1)
                                v.reciprocal(st1[:, qa + 1:qa + 2],
                                             stS[:, qb:qb + 1]).then_inc(S["v_e3b"], 1)
                                v.wait_ge(S["s_e3b"], n + 1)
                                if n >= 2:
                                    v.wait_ge(S["d_xqt"], 16 * NDC * (n - 1))
                                MAGIC = 12582912.0
                                ins = None
                                for hf in range(2):
                                    ins = v.tensor_scalar(
                                        xqf[:, hf * 512:][:, 0:512],
                                        xn3[:, hf * 512:][:, 0:512],
                                        stS[:, qb + 2:qb + 3], MAGIC, ALU.mult, ALU.add)
                                ins.then_inc(S["v_e3"], 1)

            if debug:
                with nc.Block("dbgA") as dbga:
                    @dbga.sync
                    def _(sync):
                        sync.dma_start(out=dk_h, in_=kf[:]).then_inc(S["d_dbg"], 16)
                        sync.dma_start(out=dv_h, in_=vt[:]).then_inc(S["d_dbg"], 16)
                        sync.wait_ge(S["d_dbg"], 48 + 32)

        if maxphase < 3:
            return nc
        with ExitStack() as ctxB:
            eb = ctxB.enter_context
            tq1T = eb(nc.sbuf_tensor("tq1T", [128, NDC * DH], BF16))
            tq2T = eb(nc.sbuf_tensor("tq2T", [128, NHC * D], BF16))

            chunks = []
            for c in range(NHC):
                chunks.append(("w1", c))
            for c in range(8):
                for g2 in range(4):
                    chunks.append(("w2", (c, g2)))
            NCH = len(chunks)

            with ExitStack() as ctx3:
                e3 = ctx3.enter_context
                wst = e3(nc.sbuf_tensor("wst", [128, 4 * 1024], F32))
                wst2 = e3(nc.sbuf_tensor("wst2", [128, 4 * 1024], F32))
                wtmp = e3(nc.sbuf_tensor("wtmp", [128, 1024], F32))
                tqn = e3(nc.sbuf_tensor("tqn", [128, 2 * 1024], BF16))
                absc = e3(nc.sbuf_tensor("absc", [128, 64], F32))
                swsc = e3(nc.sbuf_tensor("swsc", [1, 8], F32))
                psw = e3(nc.psum_tensor("psw", [128, 8], F32))
                ppa = e3(nc.psum_tensor("ppa", [1, 64], F32))
                pst = [e3(nc.psum_tensor(f"pst{i}", [128, 1024], F32))
                       for i in (0, 1)]

                def chunk_src(ch):
                    kind, meta = chunks[ch]
                    if kind == "w1":
                        return w1_h[meta * 128:(meta + 1) * 128, :]
                    c, g2 = meta
                    return w2_h[c * 128:(c + 1) * 128, g2 * 1024:(g2 + 1) * 1024]

                CLIP = 1.4999998807907104
                MAGIC = 12582912.0

                with nc.Block("b3") as b3:
                    @b3.sync
                    def _(sync):
                        for ch in range(NCH):
                            if ch >= 4:
                                sync.wait_ge(S["v_p1"], ch - 3)
                            sync.dma_start(
                                out=wst[:, (ch % 4) * 1024:][:, 0:1024],
                                in_=chunk_src(ch),
                            ).then_inc(S["d_w1"], 16)

                    @b3.scalar
                    def _(sc):
                        for ch in range(NCH):
                            if ch >= 4:
                                sc.wait_ge(S["v_q2"], ch - 3)
                            sc.dma_start(
                                out=wst2[:, (ch % 4) * 1024:][:, 0:1024],
                                in_=chunk_src(ch),
                            ).then_inc(S["d_w2"], 16)

                    @b3.tensor
                    def _(t):
                        t.wait_ge(S["v_p1"], NHC)
                        t.matmul(ppa[0:1, 0:NHC], onescol[:], absc[:, 0:NHC],
                                 start=True, stop=True)
                        t.wait_ge(S["v_p1"], NCH)
                        t.matmul(ppa[0:1, NHC:NCH], onescol[:],
                                 absc[:, NHC:NCH],
                                 start=True, stop=True).then_inc(S["t_p1"], 1)
                        t.wait_ge(S["v_sw"], 1)
                        t.matmul(psw[:, 4:8], onesrow[:], swsc[0:1, 0:4],
                                 start=True, stop=True).then_inc(S["t_sw"], 1)
                        for ch in range(NCH):
                            t.wait_ge(S["v_q2"], ch + 1)
                            if ch >= 2:
                                t.wait_ge(S["v_tqe"], ch - 1)
                            ins = None
                            for jj in range(8):
                                ins = t.matmul(
                                    pst[ch % 2][:, jj * 128:][:, 0:128],
                                    tqn[:, (ch % 2) * 1024 + jj * 128:][:, 0:128],
                                    ident[:], start=True, stop=True)
                            ins.then_inc(S["t_tq"], 1)

                    @b3.vector
                    def _(v):
                        for ch in range(NCH):
                            v.wait_ge(S["d_w1"], 16 * (ch + 1))
                            v.reduce_sum(absc[:, ch:ch + 1],
                                         wst[:, (ch % 4) * 1024:][:, 0:1024],
                                         axis=AX.X,
                                         apply_absolute_value=True,
                                         ).then_inc(S["v_p1"], 1)
                        v.wait_ge(S["t_p1"], 1)
                        v.reduce_sum(swsc[0:1, 4:5], ppa[0:1, 0:NHC], axis=AX.X)
                        v.reduce_sum(swsc[0:1, 5:6], ppa[0:1, NHC:NCH], axis=AX.X)
                        vpad(v)
                        vpad(v)
                        v.tensor_scalar(swsc[0:1, 4:6], swsc[0:1, 4:6],
                                        1.0 / (DH * D), 1e-5, ALU.mult, ALU.max)
                        vpad(v)
                        v.reciprocal(swsc[0:1, 6:8], swsc[0:1, 4:6])
                        vpad(v)
                        vpad(v)
                        v.tensor_copy(swsc[0:1, 0:1], swsc[0:1, 6:7])
                        v.tensor_copy(swsc[0:1, 1:2], swsc[0:1, 4:5])
                        v.tensor_copy(swsc[0:1, 2:3], swsc[0:1, 7:8])
                        v.tensor_copy(swsc[0:1, 3:4], swsc[0:1, 5:6])
                        vpad(v)
                        v.engine_nop().then_inc(S["v_sw"], 1)
                        v.wait_ge(S["t_sw"], 1)
                        v.tensor_copy(swv[:], psw[:, 4:8])
                        vpad(v)
                        v.tensor_scalar(cwsc[:], cw[:], swv[:, 1:2], None,
                                        ALU.mult).then_inc(S["v_sw"], 1)
                        for ch in range(NCH + 1):
                            if ch < NCH:
                                v.wait_ge(S["d_w2"], 16 * (ch + 1))
                                wsrc = wst2[:, (ch % 4) * 1024:][:, 0:1024]
                                scol = 0 if chunks[ch][0] == "w1" else 2
                                v.tensor_scalar(wtmp[:], wsrc,
                                                swv[:, scol:scol + 1],
                                                CLIP, ALU.mult, ALU.min)
                                v.tensor_scalar(wtmp[:], wtmp[:], -CLIP, MAGIC,
                                                ALU.max, ALU.add)
                                v.tensor_scalar(tqn[:, (ch % 2) * 1024:][:, 0:1024],
                                                wtmp[:], MAGIC, None,
                                                ALU.subtract).then_inc(S["v_q2"], 1)
                            if ch >= 1:
                                cp = ch - 1
                                v.wait_ge(S["t_tq"], cp + 1)
                                kind, meta = chunks[cp]
                                if kind == "w1":
                                    cc = meta
                                    dst = tq1T[:].rearrange(
                                        "p (d t) -> p d t", d=NDC
                                    )[:, :, cc * 128:(cc + 1) * 128]
                                else:
                                    cc, g2 = meta
                                    dst = tq2T[:, g2 * 8 * D:][:, 0:8 * D].rearrange(
                                        "p (h t) -> p h t", h=8
                                    )[:, :, cc * 128:(cc + 1) * 128]
                                v.tensor_copy(
                                    dst,
                                    pst[cp % 2][:].rearrange("p (j t) -> p j t", j=8),
                                ).then_inc(S["v_tqe"], 1)

            if maxphase < 4:
                if debug:
                    with nc.Block("dbgB3") as dbgb3:
                        @dbgb3.sync
                        def _(sync):
                            sync.dma_start(out=dt1_h, in_=tq1T[:]).then_inc(S["d_dbg"], 16)
                            sync.dma_start(out=dt2_h, in_=tq2T[:]).then_inc(S["d_dbg"], 16)
                            sync.dma_start(out=dsw_h, in_=swv[:]).then_inc(S["d_dbg"], 16)
                            sync.wait_ge(S["d_dbg"], 48 + 80)
                return nc
            hx = eb(nc.sbuf_tensor("hx", [128, NHC * HEXW], BF16))
            cbuf = eb(nc.sbuf_tensor("cbuf", [128, 2 * 516], BF16))
            ftmp = eb(nc.sbuf_tensor("ftmp", [128, 2 * 516], F32))
            accb = eb(nc.sbuf_tensor("accb", [128, 513], BF16))
            accm = eb(nc.sbuf_tensor("accm", [128, 513], BF16))
            s2b = eb(nc.sbuf_tensor("s2b", [128, 513], F32))
            mxb = eb(nc.sbuf_tensor("mxb", [128, 32], F32))
            rowsb = eb(nc.sbuf_tensor("rowsb", [1, 640], F32))
            xqt = eb(nc.sbuf_tensor("xqt", [128, 2 * NDC * 512], BF16))
            x2i = eb(nc.sbuf_tensor("x2i", [128, 2 * D], F32))
            outy = eb(nc.sbuf_tensor("outy", [128, D], F32))
            outst = eb(nc.sbuf_tensor("outst", [128, 2 * D], F32))
            hcar = eb(nc.sbuf_tensor("hcar", [128, 2 * NHC], BF16))
            ps1 = [eb(nc.psum_tensor(f"ps1{i}", [128, 512], F32)) for i in range(3)]
            ps2 = eb(nc.psum_tensor("ps2", [128, 1024], F32))
            ps3 = eb(nc.psum_tensor("ps3", [128, 1024], F32))

            tstart = [0]
            for td in TILES:
                tstart.append(tstart[-1] + len(td["slices"]))

            with nc.Block("b4") as b4:
                @b4.sync
                def _(sync):
                    def xqt_load(ti):
                        if ti >= 2:
                            sync.wait_ge(S["t_f1"], NHC * (ti - 1))
                        for d in range(NDC):
                            sync.dma_start(
                                out=xqt[:, (ti % 2) * NDC * 512 + d * 512:][:, 0:512],
                                in_=xqh_h[d, :, ti * FT:(ti + 1) * FT],
                            ).then_inc(S["d_xt"], 16)
                    xqt_load(0)
                    xqt_load(1)
                    for gi, (ti, si, c0, wd) in enumerate(GSL):
                        if si == 0 and ti >= 2:
                            xqt_load(ti)
                        r0 = TILES[ti]["tok0"] + c0
                        if gi >= 2:
                            sync.wait_ge(S["v_l4"], gi - 1)
                        sync.dma_start(
                            out=x2i[:, (gi % 2) * D:][:, 0:D][0:wd, :],
                            in_=x2_h[r0:r0 + wd, :],
                        ).then_inc(S["d_x2i"], 16)
                        if gi >= 2:
                            pi, psi, pc0, pwd = GSL[gi - 2]
                            pr0 = TILES[pi]["tok0"] + pc0
                            sync.wait_ge(S["v_l4"], gi - 1)
                            sync.dma_start(
                                out=out_h[pr0:pr0 + pwd, :],
                                in_=outst[:, ((gi - 2) % 2) * D:][:, 0:D][0:pwd, :],
                            ).then_inc(S["d_out"], 16)
                    for gi in (NSL - 2, NSL - 1):
                        ti, si, c0, wd = GSL[gi]
                        r0 = TILES[ti]["tok0"] + c0
                        sync.wait_ge(S["v_l4"], gi + 1)
                        sync.dma_start(
                            out=out_h[r0:r0 + wd, :],
                            in_=outst[:, (gi % 2) * D:][:, 0:D][0:wd, :],
                        ).then_inc(S["d_out"], 16)
                    sync.wait_ge(S["d_out"], 16 * NSL)

                @b4.gpsimd
                def _(g):
                    g.memset(hcar[:], 0.0)
                    g.memset(hx[:].rearrange("p (c w) -> p c w", c=NHC)[:, :, 514:515],
                             0.0)
                    g.memset(mxb[:], 1.0)
                    g.memset(accb[:], 0.0)
                    g.engine_nop().then_inc(S["g_i4"], 1)

                @b4.tensor
                def _(t):
                    for ti, td in enumerate(TILES):
                        nsg = len(td["slices"])
                        for c in range(NHC):
                            k = ti * NHC + c
                            if c == 0:
                                t.wait_ge(S["d_xt"], 16 * NDC * (ti + 1))
                                if ti == 0:
                                    t.wait_ge(S["g_i4"], 1)
                            if k >= 3:
                                t.wait_ge(S["v_ev"], k - 2)
                            ins = None
                            for d in range(NDC):
                                ins = t.matmul(
                                    ps1[c % 3][:],
                                    tq1T[:, d * DH + c * 128:][:, 0:128],
                                    xqt[:, (ti % 2) * NDC * 512 + d * 512:][:, 0:512],
                                    start=(d == 0), stop=(d == NDC - 1))
                            ins.then_inc(S["t_f1"], 1)
                        t.wait_ge(S["g_am"], NHC * (ti + 1))
                        ins = None
                        for gl, (c0, wd) in enumerate(td["slices"]):
                            ins = t.matmul(ps3[0:wd, 128 * gl:][:, 0:128],
                                           accb[:, c0:c0 + wd], ident[:],
                                           start=True, stop=True)
                        ins.then_inc(S["t_s2"], 1)
                        t.wait_ge(S["s_s2b"], ti + 1)
                        ins = None
                        for gl, (c0, wd) in enumerate(td["slices"]):
                            ins = t.matmul(ps3[0:1, 128 * gl:][:, 0:128],
                                           mxb[0:wd, 24 + gl:25 + gl],
                                           identf[0:wd, :],
                                           start=True, stop=True)
                        ins.then_inc(S["t_s2"], 1)
                        t.wait_ge(S["v_s2"], 3 * ti + 2)
                        ins = None
                        for gl, (c0, wd) in enumerate(td["slices"]):
                            ins = t.matmul(ps3[:, 128 * gl:][:, 0:wd],
                                           onesrow[:],
                                           rowsb[0:1, 128 * gl:][:, 0:wd],
                                           start=True, stop=True)
                        ins.then_inc(S["t_s2"], 1)
                        for si, (c0, wd) in enumerate(td["slices"]):
                            gi = tstart[ti] + si
                            if gi >= 1:
                                t.wait_ge(S["v_f2"], gi)
                            ins = None
                            for c in range(NHC):
                                if si == 0:
                                    t.wait_ge(S["s_rd"], ti * NHC + c + 1)
                                for half in range(2):
                                    ins = t.matmul(
                                        ps2[0:wd, half * 512:][:, 0:512],
                                        hx[:, c * HEXW + 1 + c0:][:, 0:wd],
                                        tq2T[:, c * D + half * 512:][:, 0:512],
                                        start=(c == 0), stop=(c == NHC - 1))
                            ins.then_inc(S["t_f2"], 1)

                @b4.vector
                def _(v):
                    for ti, td in enumerate(TILES):
                        ccmin, ccmax = td["ccmin"], td["ccmax"]
                        ccw = ccmax - ccmin
                        nsg = len(td["slices"])
                        for c in range(NHC):
                            k = ti * NHC + c
                            v.wait_ge(S["t_f1"], k + 1)
                            if k >= 2:
                                v.wait_ge(S["s_si"], k - 1)
                            if c == 0 and ti >= 1:
                                v.wait_ge(S["t_f2"], tstart[ti])
                                v.wait_ge(S["s_rd"], NHC * ti)
                            v.tensor_copy(hx[:, c * HEXW:][:, 0:2],
                                          hcar[:, 2 * c:][:, 0:2])
                            v.tensor_copy(hcar[:, 2 * c:][:, 0:2],
                                          ps1[c % 3][:, 510:512])
                            v.tensor_copy(hx[:, c * HEXW + 2:][:, 0:512],
                                          ps1[c % 3][:]).then_inc(S["v_ev"], 1)
                            cbs = cbuf[:, (k % 2) * 516:][:, 0:ccw]
                            v.tensor_scalar(cbs,
                                            hx[:, c * HEXW + ccmin + 1:][:, 0:ccw],
                                            cwsc[:, 3 * c + 1:3 * c + 2],
                                            cbv_[:, c:c + 1], ALU.mult, ALU.add)
                            v.scalar_tensor_tensor(cbs,
                                                   hx[:, c * HEXW + ccmin:][:, 0:ccw],
                                                   cwsc[:, 3 * c:3 * c + 1], cbs,
                                                   ALU.mult, ALU.add)
                            v.scalar_tensor_tensor(cbs,
                                                   hx[:, c * HEXW + ccmin + 2:][:, 0:ccw],
                                                   cwsc[:, 3 * c + 2:3 * c + 3], cbs,
                                                   ALU.mult, ALU.add).then_inc(S["v_cv"], 1)
                            if c >= 1:
                                cm = c - 1
                                v.wait_ge(S["s_si"], ti * NHC + cm + 1)
                                hsm = hx[:, cm * HEXW + ccmin + 1:][:, 0:ccw]
                                acs = accb[:, ccmin:][:, 0:ccw]
                                acm = accm[:, ccmin:][:, 0:ccw]
                                if cm == 0:
                                    if ti >= 1:
                                        v.wait_ge(S["t_s2"], 3 * ti - 2)
                                    v.tensor_copy(acs, hsm)
                                    v.tensor_copy(acm, hsm).then_inc(S["g_am"], 1)
                                else:
                                    v.tensor_tensor(acs, acs, hsm, ALU.max)
                                    v.tensor_tensor(acm, acm, hsm,
                                                    ALU.min).then_inc(S["g_am"], 1)
                        cm = NHC - 1
                        v.wait_ge(S["s_si"], ti * NHC + cm + 1)
                        hsm = hx[:, cm * HEXW + ccmin + 1:][:, 0:ccw]
                        acs = accb[:, ccmin:][:, 0:ccw]
                        acm = accm[:, ccmin:][:, 0:ccw]
                        v.tensor_tensor(acs, acs, hsm, ALU.max)
                        v.tensor_tensor(acm, acm, hsm, ALU.min)
                        v.tensor_scalar(acm, acm, -1.0, None, ALU.mult)
                        v.tensor_tensor(acs, acs, acm,
                                        ALU.max).then_inc(S["g_am"], 1)
                        v.wait_ge(S["t_s2"], 3 * ti + 1)
                        for gl, (c0, wd) in enumerate(td["slices"]):
                            v.reduce_max(mxb[0:wd, gl:gl + 1],
                                         ps3[0:wd, 128 * gl:][:, 0:128], axis=AX.X)
                        v.engine_nop().then_inc(S["v_s2"], 1)
                        v.wait_ge(S["s_s2a"], ti + 1)
                        v.reciprocal(st1[:, 32:32 + nsg],
                                     mxb[:, 16:16 + nsg]).then_inc(S["v_s2b"], 1)
                        v.wait_ge(S["t_s2"], 3 * ti + 2)
                        v.tensor_copy(rowsb[0:1, 0:128 * nsg],
                                      ps3[0:1, 0:128 * nsg]).then_inc(S["v_s2"], 1)
                        v.wait_ge(S["t_s2"], 3 * ti + 3)
                        ins = None
                        for gl, (c0, wd) in enumerate(td["slices"]):
                            ins = v.tensor_copy(s2b[:, c0:c0 + wd],
                                                ps3[:, 128 * gl:][:, 0:wd])
                        ins.then_inc(S["v_s2"], 1)
                        for c in range(NHC):
                            k = ti * NHC + c
                            if k >= 2:
                                v.wait_ge(S["s_rd"], k - 1)
                            v.tensor_tensor(
                                ftmp[:, (c % 2) * 516:][:, 0:ccw],
                                hx[:, c * HEXW + ccmin + 1:][:, 0:ccw],
                                s2b[:, ccmin:][:, 0:ccw],
                                ALU.mult).then_inc(S["v_rm"], 1)
                        for si, (c0, wd) in enumerate(td["slices"]):
                            gi = tstart[ti] + si
                            b = 28 + 4 * (gi % 2)
                            v.wait_ge(S["t_f2"], gi + 1)
                            v.wait_ge(S["d_x2i"], 16 * (gi + 1))
                            if gi >= 1:
                                v.wait_ge(S["s_l4"], gi)
                            for half in range(2):
                                v.tensor_scalar(
                                    outy[0:wd, half * 512:][:, 0:512],
                                    ps2[0:wd, half * 512:][:, 0:512],
                                    mxb[0:wd, 8 + si:9 + si], None, ALU.mult)
                            oy = outy[0:wd, :]
                            ost = outst[0:wd, (gi % 2) * D:][:, 0:D]
                            ln_stats_v(v, b, oy, sub=slice(0, wd)).then_inc(S["v_f2"], 1)
                            v.wait_ge(S["s_l4"], gi + 1)
                            v.tensor_tensor(ost, ost,
                                            x2i[0:wd, (gi % 2) * D:][:, 0:D],
                                            ALU.add).then_inc(S["v_l4"], 1)

                @b4.scalar
                def _(s):
                    for ti, td in enumerate(TILES):
                        ccmin, ccmax = td["ccmin"], td["ccmax"]
                        ccw = ccmax - ccmin
                        nsg = len(td["slices"])
                        for c in range(NHC):
                            k = ti * NHC + c
                            s.wait_ge(S["v_cv"], k + 1)
                            s.activation(hx[:, c * HEXW + ccmin + 1:][:, 0:ccw],
                                         cbuf[:, (k % 2) * 516:][:, 0:ccw],
                                         AF.Silu).then_inc(S["s_si"], 1)
                        s.wait_ge(S["v_s2"], 3 * ti + 1)
                        if ti >= 1:
                            s.wait_ge(S["v_f2"], tstart[ti])
                        s.activation(mxb[:, 16:16 + nsg], mxb[:, 0:nsg],
                                     AF.Relu, bias=c_negeps[:])
                        s.activation(mxb[:, 16:16 + nsg], mxb[:, 16:16 + nsg],
                                     AF.Identity, bias=c_eps[:])
                        s.activation(mxb[:, 8:8 + nsg], mxb[:, 16:16 + nsg],
                                     AF.Identity, scale=swv[:, 3:4])
                        s.activation(mxb[:, 8:8 + nsg], mxb[:, 8:8 + nsg],
                                     AF.Identity,
                                     scale=1.0 / 127.0).then_inc(S["s_s2a"], 1)
                        s.wait_ge(S["v_s2b"], ti + 1)
                        s.activation(mxb[:, 24:24 + nsg], st1[:, 32:32 + nsg],
                                     AF.Identity, scale=127.0).then_inc(S["s_s2b"], 1)
                        for c in range(NHC):
                            k = ti * NHC + c
                            s.wait_ge(S["v_rm"], k + 1)
                            fts = ftmp[:, (c % 2) * 516:][:, 0:ccw]
                            s.activation(fts, fts, AF.Identity, bias=c_mg[:])
                            s.activation(hx[:, c * HEXW + ccmin + 1:][:, 0:ccw],
                                         fts, AF.Identity,
                                         bias=c_nmg[:]).then_inc(S["s_rd"], 1)
                        for si, (c0, wd) in enumerate(td["slices"]):
                            gi = tstart[ti] + si
                            b = 28 + 4 * (gi % 2)
                            s.wait_ge(S["v_f2"], gi + 1)
                            if gi >= 2:
                                s.wait_ge(S["d_out"], 16 * (gi - 1))
                            ln_rstd_s(s, b, sub=slice(0, wd))
                            s.activation(jkA[:, 0:256], jkB[:, 0:256], AF.Identity)
                            s.activation(jkB[:, 0:256], jkA[:, 0:256], AF.Identity)
                            s.activation(outst[0:wd, (gi % 2) * D:][:, 0:D],
                                         outy[0:wd, :], AF.Identity,
                                         scale=stS[0:wd, b + 2:b + 3],
                                         bias=stS[0:wd, b + 3:b + 4]).then_inc(S["s_l4"], 1)

            if debug:
                with nc.Block("dbgB") as dbgb:
                    @dbgb.sync
                    def _(sync):
                        sync.dma_start(out=dt1_h, in_=tq1T[:]).then_inc(S["d_dbg"], 16)
                        sync.dma_start(out=dt2_h, in_=tq2T[:]).then_inc(S["d_dbg"], 16)
                        sync.dma_start(out=dsw_h, in_=swv[:]).then_inc(S["d_dbg"], 16)
                        sync.dma_start(out=dhx_h, in_=hx[:]).then_inc(S["d_dbg"], 16)
                        sync.wait_ge(S["d_dbg"], 48 + 96)

    return nc


def kernel(L0_attn_x, L0_attn_w_qkv, L0_ffn_w1, L0_ffn_conv_w, L0_ffn_conv_b,
           L0_ffn_w2, _debug=False, _maxphase=4):
    x = np.ascontiguousarray(np.asarray(L0_attn_x, dtype=np.float32))
    wqkv = np.ascontiguousarray(np.asarray(L0_attn_w_qkv, dtype=np.float32))
    w1 = np.ascontiguousarray(np.asarray(L0_ffn_w1, dtype=np.float32))
    convw = np.ascontiguousarray(
        np.asarray(L0_ffn_conv_w, dtype=np.float32).reshape(DH, 3))
    convb = np.ascontiguousarray(
        np.asarray(L0_ffn_conv_b, dtype=np.float32).reshape(DH, 1))
    w2 = np.ascontiguousarray(np.asarray(L0_ffn_w2, dtype=np.float32))

    nc = build_nc(debug=_debug, maxphase=_maxphase)
    in_maps = [{"x": x[i], "wqkv": wqkv, "w1": w1, "convw": convw,
                "convb": convb, "w2": w2} for i in range(8)]
    res = run_bass_kernel_spmd(nc, in_maps, core_ids=list(range(8)))
    out = np.stack([np.asarray(r["out"]) for r in res.results], axis=0)
    if _debug:
        return out, res
    return out
